# version 24
# baseline (speedup 1.0000x reference)
import contextlib
import math

import numpy as np
import ml_dtypes

import concourse.bass as bass
import concourse.mybir as mybir
from concourse.bass_utils import run_bass_kernel_spmd

F32 = mybir.dt.float32
BF16 = mybir.dt.bfloat16
I32 = mybir.dt.int32
AF = mybir.ActivationFunctionType
ALU = mybir.AluOpType
AX = mybir.AxisListType

D = 1024
T = 4096
NKT = 8
NTT = 32
NCH = 8

TREE_DEPTH = 7
TREE_PAR = 8
N_NODES = 255
TREE_TILE_BASE = [0, 1, 2, 3, 4, 5, 7, 11]
NJT = 19
JW = NJT * 128

TWO_PI = 6.283185307179586
C1 = float(np.float32(6.28125))
C2 = float(np.float32(TWO_PI - 6.28125))
C3 = float(np.float32(TWO_PI - 6.28125 - float(np.float32(TWO_PI - 6.28125))))
INV2PI = float(np.float32(1.0 / TWO_PI))
HALF_PI = float(np.float32(math.pi / 2))
PI_F = float(np.float32(math.pi))
TWOPI_F = float(np.float32(TWO_PI))
MAGIC = 12582912.0
LN_EPS = 1e-5

BF = ml_dtypes.bfloat16



def _tree_perm():
    slot_src = np.full(JW, -1, dtype=np.int64)
    for p in range(TREE_PAR):
        for d in range(TREE_DEPTH + 1):
            base = 2 ** d - 1
            for k in range(2 ** d):
                j = p * N_NODES + base + k
                idx = p * (2 ** d) + k
                slot_src[128 * TREE_TILE_BASE[d] + idx] = j
    return slot_src


def _tree_scatter():
    smeta = []
    mats0, mats1 = [], []
    for d in range(TREE_DEPTH):
        n_par = TREE_PAR * 2 ** d
        n_child = 2 * n_par
        pt0, ct0 = TREE_TILE_BASE[d], TREE_TILE_BASE[d + 1]
        n_ct = (n_child + 127) // 128
        for ci in range(n_ct):
            S0 = np.zeros((128, 128), dtype=np.float32)
            S1 = np.zeros((128, 128), dtype=np.float32)
            for cl in range(min(128, n_child - ci * 128)):
                cg = ci * 128 + cl
                q = cg // 2
                b = cg % 2
                (S1 if b else S0)[q % 128, cl] = 1.0
            smeta.append((ct0 + ci, pt0 + (ci * 64) // 128))
            mats0.append(S0)
            mats1.append(S1)
    return smeta, (np.concatenate(mats0, 1).astype(BF),
                   np.concatenate(mats1, 1).astype(BF))


TREE_SMETA, (_S0ALL, _S1ALL) = _tree_scatter()
N_SC = len(TREE_SMETA)


def _band_patch():
    Cm = np.zeros((128, 146), dtype=np.float32)
    for i in range(128):
        for w in range(146):
            delta = i - (w - 9)
            Cm[i, w] = -0.01 * (10.0 - min(abs(delta), 10.0))
    return Cm


def host_prep(inp):
    f32 = np.float32
    cont = np.ascontiguousarray
    c = {}
    _ = inp["L1_attn_Wk"]
    c["Wn"] = cont(inp["L0_attn_W"].astype(f32))
    c["b_col"] = cont(inp["L0_attn_b"].astype(f32).reshape(8, 128).T)
    c["w1T"] = cont(inp["L0_ffn_w1"].astype(f32).T)
    c["b1_col"] = cont(inp["L0_ffn_b1"].astype(f32).reshape(32, 128).T)
    c["w2T"] = cont(inp["L0_ffn_w2"].astype(f32).T)
    c["b2_row"] = cont(inp["L0_ffn_b2"].astype(f32).reshape(1, 1024))
    c["WvT"] = cont(inp["L1_attn_Wv"].astype(f32).T)
    c["WoT"] = cont(inp["L1_attn_Wo"].astype(f32).T)
    c["fr_col"] = cont(inp["L1_attn_freqs"].astype(f32).reshape(8, 128).T)
    slot = _tree_perm()
    used = slot >= 0
    WinP = np.zeros((JW, D), dtype=f32)
    binP = np.zeros(JW, dtype=f32)
    WoutP = np.zeros((D, JW), dtype=f32)
    WinP[used] = inp["L1_ffn_W_in"].astype(f32)[slot[used]]
    binP[used] = inp["L1_ffn_b_in"].astype(f32)[slot[used]]
    WoutP[:, used] = inp["L1_ffn_W_out"].astype(f32)[:, slot[used]]
    c["WinT"] = cont(WinP.T)
    c["bin_col"] = cont(binP.reshape(NJT, 128).T)
    c["WoutT"] = cont(WoutP.T)
    c["ident"] = np.eye(128, dtype=f32).astype(BF)
    c["eps_col"] = np.full((128, 1), 1e-5, dtype=f32)
    c["band"] = _band_patch()
    c["iota128_row"] = np.arange(128, dtype=f32).reshape(1, 128)
    c["iota32_row"] = np.arange(32, dtype=f32).reshape(1, 32)
    t_idx = np.arange(T)
    c["brow"] = (t_idx // 128).astype(BF).reshape(1, T)
    c["urow"] = (t_idx % 128).astype(BF).reshape(1, T)
    c["S0all"] = _S0ALL
    c["S1all"] = _S1ALL
    per_core = [{"x": cont(inp["L0_attn_x"].astype(f32)[i])} for i in range(8)]
    return c, per_core


CONST_SPECS = {
    "Wn": ([1024, 1024], F32), "b_col": ([128, 8], F32),
    "w1T": ([1024, 4096], F32), "b1_col": ([128, 32], F32),
    "w2T": ([4096, 1024], F32), "b2_row": ([1, 1024], F32),
    "WvT": ([1024, 1024], F32), "WoT": ([1024, 1024], F32),
    "fr_col": ([128, 8], F32), "WinT": ([1024, JW], F32),
    "bin_col": ([128, NJT], F32), "WoutT": ([JW, 1024], F32),
    "ident": ([128, 128], BF16), "band": ([128, 146], F32),
    "eps_col": ([128, 1], F32),
    "iota128_row": ([1, 128], F32), "iota32_row": ([1, 32], F32),
    "brow": ([1, T], BF16), "urow": ([1, T], BF16),
    "S0all": ([128, N_SC * 128], BF16), "S1all": ([128, N_SC * 128], BF16),
}



class Sem:
    __slots__ = ("h", "n")

    def __init__(self, nc, es, name):
        self.h = es.enter_context(nc.semaphore(name))
        self.n = 0

    def inc(self, inst, k=1):
        inst.then_inc(self.h, k)
        self.n += k
        return self.n

    def dinc(self, inst):
        return self.inc(inst, 16)


class DStream:

    def __init__(self, ctx, name, nbuf=2):
        self.sems = [ctx.S(name + str(j)) for j in range(nbuf)]
        self.nbuf = nbuf
        self.count = 0

    def issue(self, inst):
        i = self.count
        self.sems[i % self.nbuf].inc(inst, 16)
        self.count += 1
        return i

    def wait_done(self, eng, i):
        if i < 0:
            return
        eng.wait_ge(self.sems[i % self.nbuf].h, 16 * (i // self.nbuf + 1))

    def wait_all(self, eng):
        for s in self.sems:
            if s.n:
                eng.wait_ge(s.h, s.n)


_SS_WRAP = {
    "tensor_copy", "tensor_tensor", "tensor_add", "tensor_sub", "tensor_mul",
    "tensor_max", "tensor_scalar", "tensor_single_scalar", "tensor_scalar_add",
    "tensor_scalar_sub", "tensor_scalar_mul", "tensor_scalar_max",
    "tensor_scalar_min", "tensor_relu", "tensor_reduce", "reduce_max",
    "reduce_sum", "scalar_tensor_tensor", "tensor_tensor_scan", "memset",
    "memzero", "reciprocal", "select", "transpose", "activation", "copy",
    "sign", "square", "sqrt", "mul", "add", "engine_nop", "bn_stats",
    "bn_aggr", "stream_shuffle", "cody_waite_cascade", "add_range_wrap",
    "affine_then_add", "affine_mul_reduce", "tensor_tensor_reduce",
}


class SS:

    def __init__(self, raw, sem):
        self._r = raw
        self._s = sem

    def __getattr__(self, name):
        f = getattr(self._r, name)
        if name in _SS_WRAP:
            def w(*a, **k):
                if self._s.n:
                    self._r.wait_ge(self._s.h, self._s.n)
                ins = f(*a, **k)
                self._s.inc(ins)
                return ins
            return w
        return f


class Chain:

    def __init__(self, raw):
        self.raw = raw
        self.prev = None

    def wait_ge(self, h, n):
        self.raw.wait_ge(h, n)

    def do(self, sem, f):
        if self.prev is not None:
            ps, pn = self.prev
            self.raw.wait_ge(ps.h, pn)
        ins = f(self.raw)
        sem.inc(ins)
        self.prev = (sem, sem.n)
        return ins


class Ctx:
    def __init__(self, nc, es, block):
        self.nc = nc
        self.es = es
        self.block = block
        self.t = {}
        self.sem = {}
        self.store_streams = []

    def S(self, name):
        if name not in self.sem:
            self.sem[name] = Sem(self.nc, self.es, name)
        return self.sem[name]

    def sb(self, name, shape, dtype, es=None, side="left"):
        h = (es or self.es).enter_context(
            self.nc.sbuf_tensor(name, shape, dtype, side=side))
        self.t[name] = h
        return h

    def ps(self, name, shape, dtype, es=None):
        h = (es or self.es).enter_context(self.nc.psum_tensor(name, shape, dtype))
        self.t[name] = h
        return h


def barrier(ctx):
    bl = ctx.block
    sems = [ctx.S("bar_" + e) for e in ("sy", "te", "ve", "se", "ge")]
    all_waits = [(s.h, s.n) for nm, s in ctx.sem.items()
                 if s.n and not nm.startswith("bar_")]

    def mk(idx):
        def f(eng):
            for h, n in all_waits:
                eng.wait_ge(h, n)
            sems[idx].inc(eng.nop())
        return f
    bl.sync(mk(0)); bl.tensor(mk(1)); bl.vector(mk(2)); bl.scalar(mk(3)); bl.gpsimd(mk(4))
    targets = [s.n for s in sems]

    def mkw(my_idx):
        def f(eng):
            for i, s in enumerate(sems):
                if i != my_idx:
                    eng.wait_ge(s.h, targets[i])
        return f
    bl.sync(mkw(0)); bl.tensor(mkw(1)); bl.vector(mkw(2)); bl.scalar(mkw(3)); bl.gpsimd(mkw(4))


def emit_preload(ctx, names, es=None):
    bl = ctx.block
    LDC = ctx.S("ldc")
    table = {
        "ident": ("ident_sb", [128, 128], BF16, False),
        "eps_col": ("eps_col_sb", [128, 1], F32, False),
        "band": ("band_sb", [128, 146], F32, False),
        "b_col": ("b_col_sb", [128, 8], F32, False),
        "b1_col": ("b1_col_sb", [128, 32], F32, False),
        "bin_col": ("bin_col_sb", [128, NJT], F32, False),
        "fr_col": ("fr_col_sb", [128, 8], F32, False),
        "S0all": ("S0_sb", [128, N_SC * 128], BF16, False),
        "S1all": ("S1_sb", [128, N_SC * 128], BF16, False),
        "iota128_row": ("iota128_sb", [128, 128], F32, True),
        "iota32_row": ("iota32_sb", [128, 32], F32, True),
        "b2_row": ("b2bc_sb", [128, 1024], F32, True),
        "brow": ("brow_sb", [128, T], BF16, True),
        "urow": ("urow_sb", [128, T], BF16, True),
    }
    plans = []
    for nm in names:
        sbn, shape, dt, bc = table[nm]
        if sbn not in ctx.t:
            ctx.sb(sbn, shape, dt, es)
        plans.append((sbn, nm, bc))

    def sec(sync):
        for sbn, nm, bc in plans:
            src = ctx.t["d_" + nm]
            if bc:
                ins = sync.dma_start(out=ctx.t[sbn][:],
                                     in_=src[0:1, :].partition_broadcast(128))
            else:
                ins = sync.dma_start(out=ctx.t[sbn][:], in_=src)
            LDC.dinc(ins)
    bl.sync(sec)




def emit_ln(ctx, tag, src_dram, dst_name, spill_dram, es, dst_es=None,
            dst_side="left"):
    bl = ctx.block
    DEP = 4
    dst = ctx.sb(dst_name, [128, NKT * T], BF16, dst_es or es, side=dst_side)
    xb = ctx.sb(tag + "_x", [128, DEP * D], F32, es)
    xnT = ctx.sb(tag + "_xnT", [128, DEP * D], BF16, es)
    st = ctx.sb(tag + "_st", [128, 16 * DEP], F32, es)
    pT = ctx.ps(tag + "_pT", [128, DEP * 1024], BF16, es)
    ident = ctx.t["ident_sb"]
    ldc_target = ctx.S("ldc").n

    LD = [DStream(ctx, tag + "ldx"), DStream(ctx, tag + "ldy")]
    VE = ctx.S(tag + "ve"); SE = ctx.S(tag + "se"); TEs = ctx.S(tag + "te")
    SP = ctx.S("spill")
    xnc = {}; agc = {}; rsc = {}; cpc = {}; tec = {}; stc = {}

    dstv = dst[:].rearrange("p (k t) -> p k t", k=NKT)

    for i in range(NTT + 2):
        s = i % DEP

        if i < NTT:
            def sec_load(eng, i=i, s=s):
                if i >= DEP:
                    eng.wait_ge(VE.h, xnc[i - DEP])
                LD[i % 2].issue(eng.dma_start(
                    out=xb[:, s * D:(s + 1) * D],
                    in_=src_dram[i * 128:(i + 1) * 128, :]))
            (bl.sync if i % 2 == 0 else bl.gpsimd)(sec_load)

        def sec_ve(vector, i=i):
            vector = SS(vector, VE)
            if i < NTT:
                s = i % DEP
                c = st[:, s * 16:s * 16 + 16]
                LD[i % 2].wait_done(vector, i // 2)
                if i >= DEP:
                    vector.wait_ge(SE.h, rsc[i - DEP])
                vector.bn_stats(out=c[:, 0:6], in_=xb[:, s * D:s * D + 512])
                vector.bn_stats(out=c[:, 6:12], in_=xb[:, s * D + 512:s * D + 1024])
                vector.bn_aggr(out=c[:, 12:14], in_=c[:, 0:12])
                agc[i] = VE.n
            if 1 <= i <= NTT:
                j = i - 1
                sj = j % DEP
                c = st[:, sj * 16:sj * 16 + 16]
                vector.wait_ge(SE.h, rsc[j])
                vector.reciprocal(out=c[:, 15:16], in_=c[:, 14:15])
                vector.tensor_scalar(out=xnT[:, sj * D:(sj + 1) * D],
                                     in0=xb[:, sj * D:(sj + 1) * D],
                                     scalar1=c[:, 12:13], scalar2=c[:, 15:16],
                                     op0=ALU.subtract, op1=ALU.mult)
                xnc[j] = VE.n
            if i >= 2 and (i - 2) % 2 == 1:
                j = i - 2
                sj = j % DEP
                vector.wait_ge(TEs.h, tec[j])
                vector.tensor_copy(out=dstv[:, :, j * 128:(j + 1) * 128],
                                   in_=pT[:, sj * 1024:(sj + 1) * 1024].rearrange(
                                       "p (k t) -> p k t", k=NKT))
                cpc[j] = ("ve", VE.n)
        bl.vector(sec_ve)

        if i >= 1 and i - 1 < NTT:
            def sec_pe(tensor, i=i):
                j = i - 1
                sj = j % DEP
                if j == 0:
                    tensor.wait_ge(ctx.S("ldc").h, ldc_target)
                tensor.wait_ge(VE.h, xnc[j])
                if j >= DEP:
                    eng, n = cpc[j - DEP]
                    tensor.wait_ge(VE.h if eng == "ve" else SE.h, n)
                ins = None
                for k in range(NKT):
                    ins = tensor.transpose(
                        pT[:, sj * 1024 + k * 128:sj * 1024 + (k + 1) * 128],
                        xnT[:, sj * D + k * 128:sj * D + (k + 1) * 128],
                        ident[:])
                TEs.inc(ins)
                tec[j] = TEs.n
            bl.tensor(sec_pe)

        def sec_se(scalar, i=i):
            scalar = SS(scalar, SE)
            if i < NTT:
                s = i % DEP
                if i == 0:
                    scalar.wait_ge(ctx.S("ldc").h, ldc_target)
                scalar.wait_ge(VE.h, agc[i])
                scalar.activation(out=st[:, s * 16 + 14:s * 16 + 15],
                                  in_=st[:, s * 16 + 13:s * 16 + 14],
                                  func=AF.Sqrt, bias=ctx.t["eps_col_sb"][:, 0:1])
                rsc[i] = SE.n
            if i >= 2 and (i - 2) % 2 == 0:
                j = i - 2
                sj = j % DEP
                scalar.wait_ge(TEs.h, tec[j])
                scalar.copy(out=dstv[:, :, j * 128:(j + 1) * 128],
                            in_=pT[:, sj * 1024:(sj + 1) * 1024].rearrange(
                                "p (k t) -> p k t", k=NKT))
                cpc[j] = ("se", SE.n)
        bl.scalar(sec_se)

    if spill_dram is not None:
        spillv = spill_dram.rearrange("p (k t) -> p k t", k=NKT)
        chunk_counts = []

        def sec_spill(sync):
            for q in range(NCH):
                for j in (4 * q + 2, 4 * q + 3):
                    eng, n = cpc[j]
                    sync.wait_ge(VE.h if eng == "ve" else SE.h, n)
                SP.dinc(sync.dma_start(out=spillv[:, :, q * 512:(q + 1) * 512],
                                       in_=dstv[:, :, q * 512:(q + 1) * 512]))
                chunk_counts.append(SP.n)
        bl.sync(sec_spill)
        ctx.spill_chunks = chunk_counts
    return dst




def emit_fproj(ctx, xnC_dram, es, dst_es):
    bl = ctx.block
    fC = ctx.sb("fC", [128, NKT * T], BF16, dst_es)
    fT = ctx.sb("fT", [128, NTT * 1024], BF16, dst_es)
    Wb = ctx.sb("Wb", [128, NKT * 1024], BF16, es)
    xchunk = ctx.sb("b_xc", [128, 2 * NKT * 512], BF16, es)
    pf = ctx.ps("b_pf", [128, 2 * 512], F32, es)
    pT = ctx.ps("b_pT", [128, 2 * 1024], BF16, es)
    ident = ctx.t["ident_sb"]
    b_col = ctx.t["b_col_sb"]
    dWn = ctx.t["d_Wn"]
    xnCv_dram = xnC_dram.rearrange("p (k t) -> p k t", k=NKT)
    ldc_target = ctx.S("ldc").n

    LG = ctx.S("b_lg"); LXC = DStream(ctx, "b_lxc"); TE = ctx.S("b_te"); SE = ctx.S("b_se")
    T2 = ctx.S("b_t2"); V2 = ctx.S("b_v2")

    def sec_wload(gpsimd):
        for k in range(NKT):
            LG.dinc(gpsimd.dma_start(out=Wb[:, k * 1024:(k + 1) * 1024],
                                     in_=dWn[k * 128:(k + 1) * 128, :]))
    bl.gpsimd(sec_wload)

    def sec_xcload(sync):
        for q in range(NCH):
            sync.wait_ge(ctx.S("spill").h, ctx.spill_chunks[q])
            if q >= 2:
                sync.wait_ge(TE.h, (q - 1) * NKT)
            LXC.issue(sync.dma_start(
                out=xchunk[:, (q % 2) * NKT * 512:(q % 2 + 1) * NKT * 512].rearrange(
                    "p (k t) -> p k t", k=NKT),
                in_=xnCv_dram[:, :, q * 512:(q + 1) * 512]))
    bl.sync(sec_xcload)

    def sec_mm(tensor):
        tensor.wait_ge(LG.h, 16 * NKT)
        tensor.wait_ge(ctx.S("ldc").h, ldc_target)
        for q in range(NCH):
            LXC.wait_done(tensor, q)
            for j in range(NKT):
                r = q * NKT + j
                if r >= 2:
                    tensor.wait_ge(SE.h, r - 1)
                ins = None
                for k in range(NKT):
                    ins = tensor.matmul(pf[:, (r % 2) * 512:(r % 2 + 1) * 512],
                                        Wb[:, k * 1024 + j * 128:k * 1024 + (j + 1) * 128],
                                        xchunk[:, (q % 2) * NKT * 512 + k * 512:(q % 2) * NKT * 512 + (k + 1) * 512],
                                        start=(k == 0), stop=(k == NKT - 1))
                TE.inc(ins)
        tensor.wait_ge(SE.h, NKT * NCH)
        for i in range(NTT):
            if i >= 2:
                tensor.wait_ge(V2.h, i - 1)
            ins = None
            for k in range(NKT):
                ins = tensor.transpose(pT[:, (i % 2) * 1024 + k * 128:(i % 2) * 1024 + (k + 1) * 128],
                                       fC[:, k * T + i * 128:k * T + (i + 1) * 128],
                                       ident[:])
            T2.inc(ins)
    bl.tensor(sec_mm)

    def sec_evac(scalar):
        scalar = SS(scalar, SE)
        for q in range(NCH):
            for j in range(NKT):
                r = q * NKT + j
                scalar.wait_ge(TE.h, r + 1)
                scalar.activation(out=fC[:, j * T + q * 512:j * T + (q + 1) * 512],
                                  in_=pf[:, (r % 2) * 512:(r % 2 + 1) * 512],
                                  func=AF.Identity, bias=b_col[:, j:j + 1])
    bl.scalar(sec_evac)

    def sec_trcopy(vector):
        vector = SS(vector, V2)
        for i in range(NTT):
            vector.wait_ge(T2.h, i + 1)
            vector.tensor_copy(out=fT[:, i * 1024:(i + 1) * 1024],
                               in_=pT[:, (i % 2) * 1024:(i % 2 + 1) * 1024])
    bl.vector(sec_trcopy)
    return fC, fT




def emit_attn0(ctx, x_dram, x1p_dram, es):
    bl = ctx.block
    fC, fT = ctx.t["fC"], ctx.t["fT"]
    e_sb = ctx.sb("e_sb", [128, 2 * T], F32, es)
    a_sb = ctx.sb("a_sb", [128, 2 * T], BF16, es)
    aT_sb = ctx.sb("aT_sb", [128, T], BF16, es)
    xr = ctx.sb("c_xr", [128, 2 * D], F32, es)
    x1p = ctx.sb("c_x1p", [128, 2 * D], F32, es)
    cst = ctx.sb("c_st", [128, 4], F32, es)
    inv = ctx.sb("c_inv", [128, 2], F32, es)
    pe_ = ctx.ps("c_pe", [128, 2 * 512], F32, es)
    pA = ctx.ps("c_pA", [128, 2 * 1024], BF16, es)
    pO = ctx.ps("c_pO", [128, 2 * 1024], F32, es)
    band = ctx.t["band_sb"]
    b2bc = ctx.t["b2bc_sb"]
    ident = ctx.t["ident_sb"]

    TEs = ctx.S("c_te"); EVa = ctx.S("c_eva"); EVb = ctx.S("c_evb"); GE = ctx.S("c_ge")
    SEx = ctx.S("c_se"); VEi = ctx.S("c_vi"); TT_ = ctx.S("c_tt")
    TEV = ctx.S("c_tev"); TO = ctx.S("c_to"); OV = ctx.S("c_ov")
    VEm = ctx.S("c_vm")
    ldc_c = ctx.S("ldc").n
    LXR = DStream(ctx, "c_lxr"); SP1 = DStream(ctx, "c_sp1")
    ctx.store_streams.append(SP1)

    def sec_pe(tensor):
        for s in range(NTT + 1):
            if s < NTT:
                for q in range(NCH):
                    r = s * NCH + q
                    if r >= 2:
                        if r % 2 == 0:
                            tensor.wait_ge(EVa.h, r // 2)
                        else:
                            tensor.wait_ge(EVb.h, (r - 1) // 2)
                    ins = None
                    for k in range(NKT):
                        ins = tensor.matmul(pe_[:, (r % 2) * 512:(r % 2 + 1) * 512],
                                            fC[:, k * T + s * 128:k * T + (s + 1) * 128],
                                            fC[:, k * T + q * 512:k * T + (q + 1) * 512],
                                            start=(k == 0), stop=(k == NKT - 1))
                    TEs.inc(ins)
            if s >= 1:
                sp = s - 1
                tensor.wait_ge(SEx.h, sp + 1)
                for g in range(4):
                    gi = sp * 4 + g
                    if gi >= 2:
                        tensor.wait_ge(TEV.h, gi - 1)
                    ins = None
                    for u in range(8):
                        i = g * 8 + u
                        ins = tensor.transpose(
                            pA[:, (gi % 2) * 1024 + u * 128:(gi % 2) * 1024 + (u + 1) * 128],
                            a_sb[:, (sp % 2) * T + i * 128:(sp % 2) * T + (i + 1) * 128],
                            ident[:])
                    TT_.inc(ins)
                    if g >= 1:
                        tensor.wait_ge(TEV.h, sp * 4 + g)
                        if g == 1 and sp >= 2:
                            tensor.wait_ge(OV.h, 2 * (sp - 1))
                        for q2 in range(2):
                            for i in range((g - 1) * 8, g * 8):
                                tensor.matmul(
                                    pO[:, (sp % 2) * 1024 + q2 * 512:(sp % 2) * 1024 + (q2 + 1) * 512],
                                    aT_sb[:, i * 128:(i + 1) * 128],
                                    fT[:, i * 1024 + q2 * 512:i * 1024 + (q2 + 1) * 512],
                                    start=(i == 0), stop=False)
                tensor.wait_ge(TEV.h, sp * 4 + 4)
                for q2 in range(2):
                    ins = None
                    for i in range(24, NTT):
                        ins = tensor.matmul(
                            pO[:, (sp % 2) * 1024 + q2 * 512:(sp % 2) * 1024 + (q2 + 1) * 512],
                            aT_sb[:, i * 128:(i + 1) * 128],
                            fT[:, i * 1024 + q2 * 512:i * 1024 + (q2 + 1) * 512],
                            start=False, stop=(i == NTT - 1))
                    TO.inc(ins)
    bl.tensor(sec_pe)

    def sec_pool(gpsimd):
        gpsimd = SS(gpsimd, GE)
        gpsimd.wait_ge(ctx.S("ldc").h, ldc_c)
        for s in range(NTT):
            gpsimd.wait_ge(EVa.h, (s + 1) * 4)
            gpsimd.wait_ge(EVb.h, (s + 1) * 4)
            base = (s % 2) * T
            lo = s * 128 - 9
            if s == 0:
                gpsimd.tensor_add(out=e_sb[:, base:base + 137],
                                  in0=e_sb[:, base:base + 137], in1=band[:, 9:146])
            elif s == NTT - 1:
                gpsimd.tensor_add(out=e_sb[:, base + lo:base + T],
                                  in0=e_sb[:, base + lo:base + T], in1=band[:, 0:137])
            else:
                gpsimd.tensor_add(out=e_sb[:, base + lo:base + lo + 146],
                                  in0=e_sb[:, base + lo:base + lo + 146], in1=band[:])
    bl.gpsimd(sec_pool)

    def sec_act(scalar):
        ch = Chain(scalar)

        def emit_exp(s):
            ch.wait_ge(VEm.h, s + 1)
            if s >= 2:
                ch.wait_ge(TT_.h, (s - 1) * 4)
                ch.wait_ge(VEi.h, s - 1)
            ch.do(SEx, lambda sc, s=s: sc.activation(
                out=a_sb[:, (s % 2) * T:(s % 2 + 1) * T],
                in_=e_sb[:, (s % 2) * T:(s % 2 + 1) * T],
                func=AF.Exp, bias=cst[:, (s % 2):(s % 2) + 1],
                accum_out=cst[:, 2 + (s % 2):3 + (s % 2)]))

        def emit_tevc(sp):
            for g in range(4):
                gi = sp * 4 + g
                ch.wait_ge(TT_.h, gi + 1)
                if sp >= 1 and g == 0:
                    ch.wait_ge(TO.h, 2 * sp)
                ch.do(TEV, lambda sc, g=g, gi=gi: sc.copy(
                    out=aT_sb[:, g * 1024:(g + 1) * 1024],
                    in_=pA[:, (gi % 2) * 1024:(gi % 2 + 1) * 1024]))

        for s in range(NTT):
            if s >= 1:
                emit_exp(s - 1)
            for q in range(0, NCH, 2):
                r = s * NCH + q
                ch.wait_ge(TEs.h, r + 1)
                if s >= 2 and q == 0:
                    ch.wait_ge(SEx.h, s - 1)
                ch.do(EVa, lambda sc, s=s, q=q, r=r: sc.copy(
                    out=e_sb[:, (s % 2) * T + q * 512:(s % 2) * T + (q + 1) * 512],
                    in_=pe_[:, (r % 2) * 512:(r % 2 + 1) * 512]))
            if s >= 1:
                emit_tevc(s - 1)
        emit_exp(NTT - 1)
        emit_tevc(NTT - 1)
    bl.scalar(sec_act)

    def sec_dve(vector):
        ch = Chain(vector)
        ch.wait_ge(ctx.S("ldc").h, ldc_c)

        def emit_tail(sp):
            ch.wait_ge(SEx.h, sp + 1)
            ch.do(VEi, lambda v, sp=sp: v.reciprocal(
                out=inv[:, (sp % 2):(sp % 2) + 1],
                in_=cst[:, 2 + (sp % 2):3 + (sp % 2)]))
            ch.wait_ge(TO.h, 2 * sp + 2)
            LXR.wait_done(ch, sp)
            SP1.wait_done(ch, sp - 2)
            ch.do(OV, lambda v, sp=sp: v.scalar_tensor_tensor(
                out=x1p[:, (sp % 2) * D:(sp % 2 + 1) * D],
                in0=pO[:, (sp % 2) * 1024:(sp % 2 + 1) * 1024],
                scalar=inv[:, (sp % 2):(sp % 2) + 1],
                in1=xr[:, (sp % 2) * D:(sp % 2 + 1) * D],
                op0=ALU.mult, op1=ALU.add))
            ch.do(OV, lambda v, sp=sp: v.tensor_add(
                out=x1p[:, (sp % 2) * D:(sp % 2 + 1) * D],
                in0=x1p[:, (sp % 2) * D:(sp % 2 + 1) * D],
                in1=b2bc[:]))

        for s in range(NTT):
            for q in range(1, NCH, 2):
                r = s * NCH + q
                ch.wait_ge(TEs.h, r + 1)
                if s >= 2 and q == 1:
                    ch.wait_ge(SEx.h, s - 1)
                ch.do(EVb, lambda v, s=s, q=q, r=r: v.tensor_copy(
                    out=e_sb[:, (s % 2) * T + q * 512:(s % 2) * T + (q + 1) * 512],
                    in_=pe_[:, (r % 2) * 512:(r % 2 + 1) * 512]))
            ch.wait_ge(GE.h, s + 1)
            ch.do(VEm, lambda v, s=s: v.reduce_max(
                out=cst[:, (s % 2):(s % 2) + 1],
                in_=e_sb[:, (s % 2) * T:(s % 2) * T + T], axis=AX.X, negate=True))
            if s >= 1:
                emit_tail(s - 1)
        emit_tail(NTT - 1)
    bl.vector(sec_dve)

    def sec_sync(sync):
        def store(sp):
            sync.wait_ge(OV.h, 2 * (sp + 1))
            SP1.issue(sync.dma_start(out=x1p_dram[sp * 128:(sp + 1) * 128, :],
                                     in_=x1p[:, (sp % 2) * D:(sp % 2 + 1) * D]))
        for s in range(NTT):
            if s >= 2:
                sync.wait_ge(OV.h, 2 * (s - 1))
            LXR.issue(sync.dma_start(out=xr[:, (s % 2) * D:(s % 2 + 1) * D],
                                     in_=x_dram[s * 128:(s + 1) * 128, :]))
            if s >= 2:
                store(s - 2)
        store(NTT - 2)
        store(NTT - 1)
        SP1.wait_all(sync)
    bl.sync(sec_sync)




def emit_ffn0(ctx, xnC_dram, x1p_dram, x1_dram, es):
    bl = ctx.block
    sw1T = ctx.sb("sw1T", [128, NKT * T], BF16, es)
    sw2T = ctx.sb("sw2T", [128, NTT * 1024], BF16, es)
    xchunk = ctx.sb("d_xc", [128, 2 * NKT * 512], BF16, es)
    x1p = ctx.sb("d_x1p", [128, 2 * D], F32, es)
    x1o = ctx.sb("d_x1o", [128, 2 * D], F32, es)
    es_prep = contextlib.ExitStack()
    wraw = ctx.sb("d_wraw", [128, 4 * T], BF16, es_prep)
    a1bc = ctx.sb("d_a1bc", [128, T], BF16, es_prep)
    a2bc = ctx.sb("d_a2bc", [128, 1024], BF16, es_prep)
    ones128 = ctx.sb("d_ones", [128, 128], BF16, es_prep)
    pAw = ctx.ps("d_pAw", [128, NCH * 512], F32, es_prep)
    b1_col = ctx.t["b1_col_sb"]
    dw1T, dw2T = ctx.t["d_w1T"], ctx.t["d_w2T"]
    xnCv_dram = xnC_dram.rearrange("p (k t) -> p k t", k=NKT)

    LW = ctx.S("d_lw"); SE = ctx.S("d_se"); TE = ctx.S("d_te"); VE = ctx.S("d_ve")
    LXC = ctx.S("d_lxc"); LX1 = DStream(ctx, "d_lx1"); SX1 = DStream(ctx, "d_sx1")
    ctx.store_streams.append(SX1)
    TH = ctx.S("d_th"); SH = ctx.S("d_sh"); TO = ctx.S("d_to"); OV = ctx.S("d_ov")

    def sec_ones(vector):
        vector = SS(vector, VE)
        vector.memset(ones128[:], 1.0)
    bl.vector(sec_ones)

    def sec_wload(gpsimd):
        for k in range(NKT):
            if k >= 4:
                gpsimd.wait_ge(TE.h, (k - 3) * NCH)
            LW.dinc(gpsimd.dma_start(out=wraw[:, (k % 4) * T:(k % 4 + 1) * T],
                                     in_=dw1T[k * 128:(k + 1) * 128, :]))
        for jt in range(NTT):
            r = NKT + jt
            if jt < 4:
                gpsimd.wait_ge(TE.h, NKT * NCH)
            else:
                gpsimd.wait_ge(TE.h, NKT * NCH + (jt - 3) * 2)
            LW.dinc(gpsimd.dma_start(out=wraw[:, (r % 4) * T:(r % 4) * T + 1024],
                                     in_=dw2T[jt * 128:(jt + 1) * 128, :]))
    bl.gpsimd(sec_wload)

    def sec_wact(scalar):
        scalar = SS(scalar, SE)
        for k in range(NKT):
            w = wraw[:, (k % 4) * T:(k % 4 + 1) * T]
            scalar.wait_ge(LW.h, 16 * (k + 1))
            scalar.activation(out=sw1T[:, k * T:(k + 1) * T], in_=w,
                              func=AF.Sign)
            scalar.activation(out=w, in_=w, func=AF.Abs)
        for jt in range(NTT):
            r = NKT + jt
            w = wraw[:, (r % 4) * T:(r % 4) * T + 1024]
            scalar.wait_ge(LW.h, 16 * (NKT + jt + 1))
            scalar.activation(out=sw2T[:, jt * 1024:(jt + 1) * 1024],
                              in_=w, func=AF.Sign)
            scalar.activation(out=w, in_=w, func=AF.Abs)
    bl.scalar(sec_wact)

    def sec_alpha_mm(tensor):
        tensor.wait_ge(VE.h, 1)
        for k in range(NKT):
            tensor.wait_ge(SE.h, 2 * k + 2)
            for jc in range(NCH):
                TE.inc(tensor.matmul(pAw[:, jc * 512:(jc + 1) * 512], ones128[:],
                                     wraw[:, (k % 4) * T + jc * 512:(k % 4) * T + (jc + 1) * 512],
                                     start=(k == 0), stop=(k == NKT - 1)))
        for jt in range(NTT):
            r = NKT + jt
            tensor.wait_ge(SE.h, 2 * NKT + 2 * jt + 2)
            if jt == 0:
                tensor.wait_ge(VE.h, 2)
            ins = None
            for dc in range(2):
                ins = tensor.matmul(pAw[:, dc * 512:(dc + 1) * 512], ones128[:],
                                    wraw[:, (r % 4) * T + dc * 512:(r % 4) * T + (dc + 1) * 512],
                                    start=(jt == 0), stop=(jt == NTT - 1))
            TE.inc(ins, 2)
    bl.tensor(sec_alpha_mm)

    def sec_alpha_ev(vector):
        vector = SS(vector, VE)
        vector.wait_ge(TE.h, NKT * NCH)
        vector.tensor_scalar(out=a1bc[:], in0=pAw[:], scalar1=1.0 / D,
                             scalar2=None, op0=ALU.mult)
        vector.wait_ge(SE.h, 2 * NKT)
        for k in range(NKT):
            vector.tensor_mul(out=sw1T[:, k * T:(k + 1) * T],
                              in0=sw1T[:, k * T:(k + 1) * T], in1=a1bc[:])
        vector.wait_ge(TE.h, NKT * NCH + NTT * 2)
        vector.tensor_scalar(out=a2bc[:], in0=pAw[:, 0:1024], scalar1=1.0 / T,
                             scalar2=None, op0=ALU.mult)
        vector.wait_ge(SE.h, 2 * NKT + 2 * NTT)
        for jt in range(NTT):
            vector.tensor_mul(out=sw2T[:, jt * 1024:(jt + 1) * 1024],
                              in0=sw2T[:, jt * 1024:(jt + 1) * 1024], in1=a2bc[:])
    bl.vector(sec_alpha_ev)
    VE_PREP = ctx.S("d_ve").n
    es_prep.close()
    hC = ctx.sb("d_hC", [128, NTT * 512], BF16, es)

    pH = ctx.ps("d_pH", [128, 2 * 512], F32, es)
    pO = ctx.ps("d_pO", [128, 2 * 1024], F32, es)

    ldc_d = ctx.S("ldc").n

    def sec_sync(sync):
        sync.wait_ge(ctx.S("ldc").h, ldc_d)
        sync.wait_ge(ctx.S("spill").h, ctx.S("spill").n)

        def store(i):
            sync.wait_ge(OV.h, i + 1)
            s_tok = (i // 4) * 512 + (i % 4) * 128
            SX1.issue(sync.dma_start(out=x1_dram[s_tok:s_tok + 128, :],
                                     in_=x1o[:, (i % 2) * D:(i % 2 + 1) * D]))
        for q in range(NCH):
            if q >= 2:
                sync.wait_ge(TH.h, (q - 1) * NTT)
            LXC.dinc(sync.dma_start(
                out=xchunk[:, (q % 2) * NKT * 512:(q % 2 + 1) * NKT * 512].rearrange(
                    "p (k t) -> p k t", k=NKT),
                in_=xnCv_dram[:, :, q * 512:(q + 1) * 512]))
            for sub in range(4):
                i = q * 4 + sub
                if i >= 2:
                    sync.wait_ge(OV.h, i - 1)
                s_tok = q * 512 + sub * 128
                LX1.issue(sync.dma_start(out=x1p[:, (i % 2) * D:(i % 2 + 1) * D],
                                         in_=x1p_dram[s_tok:s_tok + 128, :]))
                if i >= 2:
                    store(i - 2)
        store(4 * NCH - 2)
        store(4 * NCH - 1)
        SX1.wait_all(sync)
    bl.sync(sec_sync)

    def sec_pe(tensor):
        for q in range(NCH):
            tensor.wait_ge(LXC.h, 16 * (q + 1))
            if q == 0:
                tensor.wait_ge(VE.h, VE_PREP)
            xoff = (q % 2) * NKT * 512
            for jt in range(NTT):
                r = q * NTT + jt
                if r >= 2:
                    tensor.wait_ge(SH.h, r - 1)
                ins = None
                for k in range(NKT):
                    ins = tensor.matmul(pH[:, (r % 2) * 512:(r % 2 + 1) * 512],
                                        sw1T[:, k * T + jt * 128:k * T + (jt + 1) * 128],
                                        xchunk[:, xoff + k * 512:xoff + (k + 1) * 512],
                                        start=(k == 0), stop=(k == NKT - 1))
                TH.inc(ins)
            tensor.wait_ge(SH.h, (q + 1) * NTT)
            for sub in range(4):
                i = q * 4 + sub
                if i >= 2:
                    tensor.wait_ge(OV.h, i - 1)
                for dc in range(2):
                    ins = None
                    for jt in range(NTT):
                        ins = tensor.matmul(
                            pO[:, (i % 2) * 1024 + dc * 512:(i % 2) * 1024 + (dc + 1) * 512],
                            hC[:, jt * 512 + sub * 128:jt * 512 + (sub + 1) * 128],
                            sw2T[:, jt * 1024 + dc * 512:jt * 1024 + (dc + 1) * 512],
                            start=(jt == 0), stop=(jt == NTT - 1))
                    TO.inc(ins)
    bl.tensor(sec_pe)

    def sec_hevac(scalar):
        scalar = SS(scalar, SH)
        for q in range(NCH):
            for jt in range(NTT):
                r = q * NTT + jt
                scalar.wait_ge(TH.h, r + 1)
                if q >= 1 and jt == 0:
                    scalar.wait_ge(TO.h, q * 8)
                scalar.activation(out=hC[:, jt * 512:(jt + 1) * 512],
                                  in_=pH[:, (r % 2) * 512:(r % 2 + 1) * 512],
                                  func=AF.Gelu, bias=b1_col[:, jt:jt + 1])
    bl.scalar(sec_hevac)



    def sec_oevac(vector):
        vector = SS(vector, OV)
        for q in range(NCH):
            for sub in range(4):
                i = q * 4 + sub
                vector.wait_ge(TO.h, 2 * i + 2)
                LX1.wait_done(vector, i)
                SX1.wait_done(vector, i - 2)
                vector.tensor_add(out=x1o[:, (i % 2) * D:(i % 2 + 1) * D],
                                  in0=pO[:, (i % 2) * 1024:(i % 2 + 1) * 1024],
                                  in1=x1p[:, (i % 2) * D:(i % 2 + 1) * D])
    bl.vector(sec_oevac)




def emit_rotor(ctx, x1_dram, x2p_dram, rC_dram, es):
    bl = ctx.block
    xn1C = ctx.t["xn1C"]
    es_t = contextlib.ExitStack()
    Wvo = ctx.sb("Wvo", [128, NKT * 1024], BF16, es)
    v_sb = ctx.sb("f_v", [128, 2 * T], BF16, es_t)
    sint = ctx.sb("f_sin", [128, T], BF16, es_t)
    cost = ctx.sb("f_cos", [128, T], BF16, es_t)
    tmp = ctx.sb("f_tmp", [128, T], BF16, es_t)
    Pb = ctx.sb("f_P", [128, T], BF16, es_t)
    Qb = ctx.sb("f_Q", [128, T], BF16, es_t)
    Pc = ctx.sb("f_Pc", [128, T], BF16, es_t)
    Qc = ctx.sb("f_Qc", [128, T], BF16, es_t)
    ones_r = ctx.sb("f_ones", [128, T], BF16, es_t)
    gtab = ctx.sb("f_g", [128, 32], F32, es_t)
    hpi_col = ctx.sb("f_hpi", [128, 1], F32, es_t)
    thB = ctx.sb("f_thB", [128, NKT * 128], F32, es_t)
    uuB = ctx.sb("f_uuB", [128, NKT * 128], F32, es_t)
    sinB = ctx.sb("f_sinB", [128, NKT * 128], BF16, es_t)
    cosB16 = ctx.sb("f_cosB", [128, NKT * 128], BF16, es_t)
    cosB = ctx.sb("f_cosBW", [128, NKT * 128], F32, es_t)
    thA = ctx.sb("f_thA", [128, NKT * 32], F32, es_t)
    sinA = ctx.sb("f_sinA", [128, NKT * 32], BF16, es_t)
    cosA16 = ctx.sb("f_cosA", [128, NKT * 32], BF16, es_t)
    cosA = ctx.sb("f_cosAW", [128, NKT * 32], F32, es_t)
    pV = ctx.ps("f_pV", [128, 2 * 512], F32, es_t)
    iota128 = ctx.t["iota128_sb"]
    iota32 = ctx.t["iota32_sb"]
    fr_col = ctx.t["fr_col_sb"]
    dWvT, dWoT = ctx.t["d_WvT"], ctx.t["d_WoT"]

    LWV = ctx.S("f_lwv"); LWO2 = ctx.S("f_lwo")
    TE = ctx.S("f_te"); SEs = ctx.S("f_se"); VEs = ctx.S("f_ve")
    GEs = ctx.S("f_ge"); SRC = ctx.S("f_src")
    ldc_target = ctx.S("ldc").n

    def bc3(ap2d, mid, inner):
        if ap2d.shape[1] == inner:
            return ap2d.rearrange("p (o u) -> p o u", o=1).to_broadcast(
                [128, mid, inner])
        return ap2d.rearrange("p (b o) -> p b o", o=1).to_broadcast(
            [128, mid, inner])

    def sec_wload_v(gpsimd):
        for k in range(NKT):
            LWV.dinc(gpsimd.dma_start(out=Wvo[:, k * 1024:(k + 1) * 1024],
                                      in_=dWvT[k * 128:(k + 1) * 128, :]))
    bl.gpsimd(sec_wload_v)

    def sec_g(vector):
        vector = SS(vector, VEs)
        vector.wait_ge(ctx.S("ldc").h, ldc_target)
        g, kg = gtab[:, 0:8], gtab[:, 24:32]
        vector.tensor_scalar(out=g, in0=fr_col[:], scalar1=128.0, scalar2=None,
                             op0=ALU.mult)
        vector.tensor_scalar(out=kg, in0=g, scalar1=INV2PI, scalar2=MAGIC,
                             op0=ALU.mult, op1=ALU.add)
        vector.tensor_scalar(out=kg, in0=kg, scalar1=MAGIC, scalar2=None,
                             op0=ALU.subtract)
        vector.scalar_tensor_tensor(out=g, in0=kg, scalar=-C1, in1=g,
                                    op0=ALU.mult, op1=ALU.add)
        vector.scalar_tensor_tensor(out=g, in0=kg, scalar=-C2, in1=g,
                                    op0=ALU.mult, op1=ALU.add)
        vector.memset(ones_r[:], 1.0)
        vector.memset(hpi_col[:], HALF_PI)
        thBv = thB[:].rearrange("p (k u) -> p k u", k=NKT)
        vector.tensor_mul(out=thBv, in0=bc3(fr_col[:], NKT, 128),
                          in1=bc3(iota128[:], NKT, 128))
        vector.tensor_scalar(out=uuB[:], in0=thB[:], scalar1=INV2PI,
                             scalar2=MAGIC, op0=ALU.mult, op1=ALU.add)
        vector.tensor_scalar(out=uuB[:], in0=uuB[:], scalar1=MAGIC,
                             scalar2=None, op0=ALU.subtract)
        vector.scalar_tensor_tensor(out=thB[:], in0=uuB[:], scalar=-TWOPI_F,
                                    in1=thB[:], op0=ALU.mult, op1=ALU.add)
        vector.tensor_scalar(out=cosB[:], in0=thB[:], scalar1=-PI_F,
                             scalar2=PI_F, op0=ALU.max, op1=ALU.min)
        thAv = thA[:].rearrange("p (k b) -> p k b", k=NKT)
        vector.tensor_mul(out=thAv, in0=bc3(gtab[:, 0:8], NKT, 32),
                          in1=bc3(iota32[:, 0:32], NKT, 32))
        vector.tensor_scalar(out=uuB[:, 0:NKT * 32], in0=thA[:], scalar1=INV2PI,
                             scalar2=MAGIC, op0=ALU.mult, op1=ALU.add)
        vector.tensor_scalar(out=uuB[:, 0:NKT * 32], in0=uuB[:, 0:NKT * 32],
                             scalar1=MAGIC, scalar2=None, op0=ALU.subtract)
        vector.scalar_tensor_tensor(out=thA[:], in0=uuB[:, 0:NKT * 32],
                                    scalar=-TWOPI_F,
                                    in1=thA[:], op0=ALU.mult, op1=ALU.add)
        vector.tensor_scalar(out=cosA[:], in0=thA[:], scalar1=-PI_F,
                             scalar2=PI_F, op0=ALU.max, op1=ALU.min)
    bl.vector(sec_g)
    ve_gbuild = VEs.n

    def sec_se_tab(scalar):
        scalar = SS(scalar, SEs)
        scalar.wait_ge(VEs.h, ve_gbuild)
        scalar.activation(out=sinB[:], in_=cosB[:], func=AF.Sin)
        scalar.activation(out=cosB[:], in_=cosB[:], func=AF.Abs)
        scalar.activation(out=cosB16[:], in_=cosB[:], func=AF.Sin,
                          scale=-1.0, bias=hpi_col[:, 0:1])
        scalar.activation(out=sinA[:], in_=cosA[:], func=AF.Sin)
        scalar.activation(out=cosA[:], in_=cosA[:], func=AF.Abs)
        scalar.activation(out=cosA16[:], in_=cosA[:], func=AF.Sin,
                          scale=-1.0, bias=hpi_col[:, 0:1])
    bl.scalar(sec_se_tab)
    se_base = SEs.n
    te_base = TE.n

    def sec_vmm(tensor):
        tensor.wait_ge(LWV.h, 16 * NKT)
        for k in range(NKT):
            for q in range(NCH):
                m = k * NCH + q
                if m >= 2:
                    kp2, qp2 = divmod(m - 2, NCH)
                    tensor.wait_ge(SEs.h, se_base + 8 * kp2 + qp2 + 1)
                ins = None
                for dt_ in range(NKT):
                    ins = tensor.matmul(
                        pV[:, (m % 2) * 512:(m % 2 + 1) * 512],
                        Wvo[:, dt_ * 1024 + k * 128:dt_ * 1024 + (k + 1) * 128],
                        xn1C[:, dt_ * T + q * 512:dt_ * T + (q + 1) * 512],
                        start=(dt_ == 0), stop=(dt_ == NKT - 1))
                TE.inc(ins)
    bl.tensor(sec_vmm)

    cnt = {nm: {} for nm in ("recon", "P", "Q", "S1", "S2", "m1", "m2", "r")}
    for kc in range(NKT + 1):
        k = kc
        kp = kc - 1

        if kp >= 0:
            def sec_ve_scan(vector, k=kp):
                vector = SS(vector, VEs)
                if k >= 1:
                    vector.wait_ge(SRC.h, 16 * k)
                vector.tensor_tensor_scan(out=Pc[:], data0=ones_r[:], data1=Pb[:],
                                          initial=0.0, op0=ALU.mult, op1=ALU.add)
                cnt["S1"][k] = VEs.n
                vector.tensor_tensor_scan(out=Qc[:], data0=ones_r[:], data1=Qb[:],
                                          initial=0.0, op0=ALU.mult, op1=ALU.add)
                cnt["S2"][k] = VEs.n
                vector.tensor_mul(out=Pb[:], in0=cost[:], in1=Pc[:])
                cnt["m1"][k] = VEs.n
                vector.tensor_mul(out=Qb[:], in0=sint[:], in1=Qc[:])
                cnt["m2"][k] = VEs.n
                vector.tensor_add(out=Pc[:], in0=Pb[:], in1=Qb[:])
                cnt["r"][k] = VEs.n
            bl.vector(sec_ve_scan)

            def sec_store(sync, k=kp):
                sync.wait_ge(VEs.h, cnt["r"][k])
                SRC.dinc(sync.dma_start(out=rC_dram[:, k * T:(k + 1) * T],
                                        in_=Pc[:]))
            bl.sync(sec_store)

        if k < NKT:
            def sec_ve_recon(vector, k=k):
                vector = SS(vector, VEs)
                vector.wait_ge(SEs.h, 6)
                As = bc3(sinA[:, k * 32:(k + 1) * 32], 32, 128)
                Ac = bc3(cosA16[:, k * 32:(k + 1) * 32], 32, 128)
                Bs = bc3(sinB[:, k * 128:(k + 1) * 128], 32, 128)
                Bc = bc3(cosB16[:, k * 128:(k + 1) * 128], 32, 128)
                sv = sint[:].rearrange("p (b u) -> p b u", b=32)
                cv = cost[:].rearrange("p (b u) -> p b u", b=32)
                tv = tmp[:].rearrange("p (b u) -> p b u", b=32)
                vector.tensor_mul(out=sv, in0=As, in1=Bc)
                vector.tensor_mul(out=tv, in0=Ac, in1=Bs)
                vector.tensor_add(out=sint[:], in0=sint[:], in1=tmp[:])
                vector.tensor_mul(out=cv, in0=Ac, in1=Bc)
                vector.tensor_mul(out=tv, in0=As, in1=Bs)
                vector.tensor_sub(out=cost[:], in0=cost[:], in1=tmp[:])
                cnt["recon"][k] = VEs.n
            bl.vector(sec_ve_recon)

        if k < NKT:
            def sec_se_ev(scalar, k=k):
                scalar = SS(scalar, SEs)
                if k >= 2:
                    scalar.wait_ge(VEs.h, cnt["Q"][k - 2])
                vo = (k % 2) * T
                for q in range(NCH):
                    m = k * NCH + q
                    scalar.wait_ge(TE.h, te_base + m + 1)
                    scalar.copy(out=v_sb[:, vo + q * 512:vo + (q + 1) * 512],
                                in_=pV[:, (m % 2) * 512:(m % 2 + 1) * 512])
                assert SEs.n == se_base + 8 * (k + 1), (SEs.n, se_base, k)
            bl.scalar(sec_se_ev)

            def sec_ve_p(vector, k=k):
                vector = SS(vector, VEs)
                vector.wait_ge(SEs.h, se_base + 8 * (k + 1))
                vector.tensor_mul(out=Pb[:], in0=v_sb[:, (k % 2) * T:(k % 2 + 1) * T],
                                  in1=cost[:])
                cnt["P"][k] = VEs.n
            bl.vector(sec_ve_p)

            def sec_ve_q(vector, k=k):
                vector = SS(vector, VEs)
                vector.tensor_mul(out=Qb[:], in0=v_sb[:, (k % 2) * T:(k % 2 + 1) * T],
                                  in1=sint[:])
                cnt["Q"][k] = VEs.n
            bl.vector(sec_ve_q)

    te_vmm_all = TE.n

    def sec_wload_o(gpsimd):
        gpsimd.wait_ge(TE.h, te_vmm_all)
        for k in range(NKT):
            LWO2.dinc(gpsimd.dma_start(out=Wvo[:, k * 1024:(k + 1) * 1024],
                                       in_=dWoT[k * 128:(k + 1) * 128, :]))
    bl.gpsimd(sec_wload_o)
    lw_o = ctx.S("f_lwo").n

    es_t.close()
    rbT = ctx.sb("f_rbT", [128, 2 * 1024], BF16, es)
    x1r = ctx.sb("f_x1r", [128, 2 * D], F32, es)
    x2p = ctx.sb("f_x2p", [128, 2 * D], F32, es)
    pO = ctx.ps("f_pO", [128, 2 * 1024], F32, es)
    TO = ctx.S("f_to"); OV = ctx.S("f_ov")
    LX1 = DStream(ctx, "f_lx1"); SX2 = DStream(ctx, "f_sx2")
    LRB = DStream(ctx, "f_lrb")
    ctx.store_streams.append(SX2)
    rCv_dram = rC_dram.rearrange("p (k t) -> p k t", k=NKT)

    def sec_sync2(sync):
        sync.wait_ge(SRC.h, 16 * NKT)

        def store(bp):
            sync.wait_ge(OV.h, bp + 1)
            SX2.issue(sync.dma_start(out=x2p_dram[bp * 128:(bp + 1) * 128, :],
                                     in_=x2p[:, (bp % 2) * D:(bp % 2 + 1) * D]))
        for b in range(NTT):
            if b >= 2:
                sync.wait_ge(TO.h, 2 * (b - 1))
            LRB.issue(sync.dma_start(
                out=rbT[:, (b % 2) * 1024:(b % 2 + 1) * 1024].rearrange(
                    "p (k t) -> p k t", k=NKT),
                in_=rCv_dram[:, :, b * 128:(b + 1) * 128]))
            if b >= 2:
                sync.wait_ge(OV.h, b - 1)
            LX1.issue(sync.dma_start(out=x1r[:, (b % 2) * D:(b % 2 + 1) * D],
                                     in_=x1_dram[b * 128:(b + 1) * 128, :]))
            if b >= 2:
                store(b - 2)
        store(NTT - 2)
        store(NTT - 1)
        SX2.wait_all(sync)
    bl.sync(sec_sync2)

    def sec_omm(tensor):
        tensor.wait_ge(LWO2.h, lw_o)
        for b in range(NTT):
            LRB.wait_done(tensor, b)
            if b >= 2:
                tensor.wait_ge(OV.h, b - 1)
            for q2 in range(2):
                ins = None
                for k in range(NKT):
                    ins = tensor.matmul(
                        pO[:, (b % 2) * 1024 + q2 * 512:(b % 2) * 1024 + (q2 + 1) * 512],
                        rbT[:, (b % 2) * 1024 + k * 128:(b % 2) * 1024 + (k + 1) * 128],
                        Wvo[:, k * 1024 + q2 * 512:k * 1024 + (q2 + 1) * 512],
                        start=(k == 0), stop=(k == NKT - 1))
                TO.inc(ins)
    bl.tensor(sec_omm)

    def sec_oev(vector):
        vector = SS(vector, OV)
        for b in range(NTT):
            vector.wait_ge(TO.h, 2 * b + 2)
            LX1.wait_done(vector, b)
            SX2.wait_done(vector, b - 2)
            vector.tensor_add(out=x2p[:, (b % 2) * D:(b % 2 + 1) * D],
                              in0=pO[:, (b % 2) * 1024:(b % 2 + 1) * 1024],
                              in1=x1r[:, (b % 2) * D:(b % 2 + 1) * D])
    bl.vector(sec_oev)




def emit_ffn1(ctx, xn1C_dram, x2p_dram, out_dram, es):
    bl = ctx.block
    if "WinTb" in ctx.t:
        WinTb, WoutTb = ctx.t["WinTb"], ctx.t["WoutTb"]
        preloaded = True
    else:
        WinTb = ctx.sb("WinTb", [128, NKT * JW], BF16, es)
        WoutTb = ctx.sb("WoutTb", [128, NJT * 1024], BF16, es)
        preloaded = False
    acts = ctx.sb("g_acts", [128, NJT * 512], BF16, es)
    dec = ctx.sb("g_dec", [128, 11 * 512], BF16, es)
    msk2 = ctx.sb("g_msk", [128, 2 * NJT * 512], BF16, es)
    u0 = ctx.sb("g_u0", [128, 11 * 512], BF16, es)
    u1 = ctx.sb("g_u1", [128, 11 * 512], BF16, es)
    xchunk = ctx.sb("g_xc", [128, 2 * NKT * 512], BF16, es)
    x2p = ctx.sb("g_x2p", [128, 2 * D], F32, es)
    xout = ctx.sb("g_out", [128, D], F32, es)
    pL = ctx.ps("g_pL", [128, 2 * 512], F32, es)
    pS = ctx.ps("g_pS", [128, 2 * 512], F32, es)
    pO = ctx.ps("g_pO", [128, 2 * 1024], F32, es)
    bin_col = ctx.t["bin_col_sb"]
    S0sb, S1sb = ctx.t["S0_sb"], ctx.t["S1_sb"]
    dWinT, dWoutT = ctx.t["d_WinT"], ctx.t["d_WoutT"]
    xn1Cv_dram = xn1C_dram.rearrange("p (k t) -> p k t", k=NKT)
    ldc_target = ctx.S("ldc").n

    LWI = ctx.S("g_lwi"); LWO = ctx.S("g_lwo")
    TE = ctx.S("g_te"); SE = ctx.S("g_se"); VE = ctx.S("g_ve")
    LXC = DStream(ctx, "g_lxc"); LX2 = DStream(ctx, "g_lx2")
    SOUT = DStream(ctx, "g_sout")
    OV = ctx.S("g_ov")

    if not preloaded:
        def sec_wload(gpsimd):
            for k in range(NKT):
                LWI.dinc(gpsimd.dma_start(out=WinTb[:, k * JW:(k + 1) * JW],
                                          in_=dWinT[k * 128:(k + 1) * 128, :]))
            for jt in range(NJT):
                LWO.dinc(gpsimd.dma_start(
                    out=WoutTb[:, jt * 1024:(jt + 1) * 1024],
                    in_=dWoutT[jt * 128:(jt + 1) * 128, :]))
        bl.gpsimd(sec_wload)

    def sec_xc0(sync):
        sync.wait_ge(ctx.S("ldc").h, ldc_target)
        sync.wait_ge(ctx.S("spill").h, ctx.S("spill").n)
        LXC.issue(sync.dma_start(
            out=xchunk[:, 0:NKT * 512].rearrange("p (k t) -> p k t", k=NKT),
            in_=xn1Cv_dram[:, :, 0:512]))
        LXC.issue(sync.dma_start(
            out=xchunk[:, NKT * 512:].rearrange("p (k t) -> p k t", k=NKT),
            in_=xn1Cv_dram[:, :, 512:1024]))
    bl.sync(sec_xc0)

    silu_end = {}; dec_end = {}; tree_end = {}
    tev_cnt = {}; uu_cnt = {}; tmm_cnt = {}; chain_to = {}

    def emit_omm_chain(q_out, c, part=None):
        sub, dc = c // 2, c % 2
        i2 = q_out * 4 + sub
        jts = range(NJT) if part is None else (
            range(0, 10) if part == 0 else range(10, NJT))

        def sec_ommc(tensor, q_out=q_out, sub=sub, dc=dc, i2=i2, jts=jts,
                     part=part):
            if part in (None, 0):
                if q_out == 0 and sub == 0 and dc == 0:
                    tensor.wait_ge(LWO.h, 16 * NJT)
                tensor.wait_ge(VE.h, tree_end[q_out])
                if i2 >= 2 and dc == 0:
                    tensor.wait_ge(OV.h, i2 - 1)
            moff = (q_out % 2) * NJT * 512
            ins = None
            for jt in jts:
                ins = tensor.matmul(
                    pO[:, (i2 % 2) * 1024 + dc * 512:(i2 % 2) * 1024 + (dc + 1) * 512],
                    msk2[:, moff + jt * 512 + sub * 128:moff + jt * 512 + (sub + 1) * 128],
                    WoutTb[:, jt * 1024 + dc * 512:jt * 1024 + (dc + 1) * 512],
                    start=(jt == 0), stop=(jt == NJT - 1))
            if part in (None, 1):
                TO = ctx.S("g_to")
                TO.inc(ins)
                chain_to[(q_out, sub, dc)] = TO.n
        bl.tensor(sec_ommc)

    TO = ctx.S("g_to")

    for q in range(NCH):
        te_l0 = TE.n
        ve0 = ctx.S("g_ve").n
        se0 = ctx.S("g_se").n
        xoff = (q % 2) * NKT * 512

        def sec_lmm(tensor, q=q, xoff=xoff, ve0=ve0, se0=se0):
            LXC.wait_done(tensor, q)
            if q == 0:
                tensor.wait_ge(LWI.h, 16 * NKT)
            for jt in range(NJT):
                if jt < 2 and q >= 1:
                    tensor.wait_ge(SE.h, silu_end[q - 1])
                    tensor.wait_ge(VE.h, dec_end[q - 1])
                if jt >= 2:
                    tensor.wait_ge(SE.h, se0 + jt - 1)
                    if jt - 2 < 11:
                        tensor.wait_ge(VE.h, ve0 + jt - 1)
                ins = None
                for k in range(NKT):
                    ins = tensor.matmul(pL[:, (jt % 2) * 512:(jt % 2 + 1) * 512],
                                        WinTb[:, k * JW + jt * 128:k * JW + (jt + 1) * 128],
                                        xchunk[:, xoff + k * 512:xoff + (k + 1) * 512],
                                        start=(k == 0), stop=(k == NKT - 1))
                TE.inc(ins)
        bl.tensor(sec_lmm)
        te_lmm = TE.n

        def sec_silu(scalar, q=q, te_l0=te_l0):
            scalar = SS(scalar, SE)
            if q >= 1:
                scalar.wait_ge(VE.h, tree_end[q - 1])
            for jt in range(NJT):
                scalar.wait_ge(TE.h, te_l0 + jt + 1)
                scalar.activation(out=acts[:, jt * 512:(jt + 1) * 512],
                                  in_=pL[:, (jt % 2) * 512:(jt % 2 + 1) * 512],
                                  func=AF.Silu, bias=bin_col[:, jt:jt + 1])
        bl.scalar(sec_silu)
        silu_end[q] = ctx.S("g_se").n

        def sec_dec(vector, q=q, te_l0=te_l0, se0=se0):
            vector = SS(vector, VE)
            for jt in range(11):
                vector.wait_ge(TE.h, te_l0 + jt + 1)
                vector.wait_ge(SE.h, se0 + jt + 1)
                vector.tensor_scalar(out=dec[:, jt * 512:(jt + 1) * 512],
                                     in0=pL[:, (jt % 2) * 512:(jt % 2 + 1) * 512],
                                     scalar1=bin_col[:, jt:jt + 1], scalar2=0.0,
                                     op0=ALU.add, op1=ALU.is_gt)
            vector.tensor_copy(out=u1[:, 0:512], in_=dec[:, 0:512])
            vector.tensor_scalar(out=u0[:, 0:512], in0=dec[:, 0:512],
                                 scalar1=-1.0, scalar2=1.0,
                                 op0=ALU.mult, op1=ALU.add)
            uu_cnt[(q, 0)] = ctx.S("g_ve").n
            if q >= 2:
                vector.wait_ge(TO.h, chain_to[(q - 2, 3, 1)])
            vector.tensor_copy(out=msk2[:, (q % 2) * NJT * 512:(q % 2) * NJT * 512 + 512],
                               in_=acts[:, 0:512])
        bl.vector(sec_dec)
        dec_end[q] = ctx.S("g_ve").n

        if q + 2 < NCH:
            def sec_xc(sync, q=q, te_lmm=te_lmm):
                sync.wait_ge(TE.h, te_lmm)
                LXC.issue(sync.dma_start(
                    out=xchunk[:, (q % 2) * NKT * 512:(q % 2 + 1) * NKT * 512].rearrange(
                        "p (k t) -> p k t", k=NKT),
                    in_=xn1Cv_dram[:, :, (q + 2) * 512:(q + 3) * 512]))
            bl.sync(sec_xc)

        if q >= 1:
            def sec_x2pload(sync, q=q):
                for sub in range(4):
                    i2 = (q - 1) * 4 + sub
                    if i2 >= 2:
                        sync.wait_ge(OV.h, i2 - 1)
                    s_tok = (q - 1) * 512 + sub * 128
                    LX2.issue(sync.dma_start(
                        out=x2p[:, (i2 % 2) * D:(i2 % 2 + 1) * D],
                        in_=x2p_dram[s_tok:s_tok + 128, :]))
            bl.sync(sec_x2pload)

        for r, (ct, pt) in enumerate(TREE_SMETA):
            def sec_tmm(tensor, q=q, r=r, ct=ct, pt=pt):
                tensor.wait_ge(VE.h, uu_cnt[(q, pt)])
                if r >= 2:
                    tensor.wait_ge(VE.h, tev_cnt[(q, r - 2)])
                elif q >= 1:
                    tensor.wait_ge(VE.h, tev_cnt[(q - 1, 16 + r)])
                tensor.matmul(pS[:, (r % 2) * 512:(r % 2 + 1) * 512],
                              S0sb[:, r * 128:(r + 1) * 128],
                              u0[:, pt * 512:(pt + 1) * 512],
                              start=True, stop=False)
                TE.inc(tensor.matmul(pS[:, (r % 2) * 512:(r % 2 + 1) * 512],
                                     S1sb[:, r * 128:(r + 1) * 128],
                                     u1[:, pt * 512:(pt + 1) * 512],
                                     start=False, stop=True))
                tmm_cnt[(q, r)] = ctx.S("g_te").n
            bl.tensor(sec_tmm)

            if q >= 1 and 1 <= r <= 16:
                emit_omm_chain(q - 1, (r - 1) // 2, part=(r - 1) % 2)

            if q >= 1 and r % 4 == 0 and r >= 4:
                subo = r // 4 - 1
                i2o = (q - 1) * 4 + subo

                def sec_oev1(vector, q=q, subo=subo, i2o=i2o):
                    vector = SS(vector, OV)
                    vector.wait_ge(TO.h, chain_to[(q - 1, subo, 1)])
                    LX2.wait_done(vector, i2o)
                    SOUT.wait_done(vector, i2o - 1)
                    vector.tensor_add(out=xout[:, 0:D],
                                      in0=pO[:, (i2o % 2) * 1024:(i2o % 2 + 1) * 1024],
                                      in1=x2p[:, (i2o % 2) * D:(i2o % 2 + 1) * D])
                bl.vector(sec_oev1)

            def sec_tev(vector, q=q, r=r, ct=ct, se0=se0):
                vector = SS(vector, VE)
                vector.wait_ge(TE.h, tmm_cnt[(q, r)])
                ps = pS[:, (r % 2) * 512:(r % 2 + 1) * 512]
                moff = (q % 2) * NJT * 512
                if ct < 11:
                    vector.tensor_mul(out=u1[:, ct * 512:(ct + 1) * 512],
                                      in0=ps, in1=dec[:, ct * 512:(ct + 1) * 512])
                    vector.tensor_sub(out=u0[:, ct * 512:(ct + 1) * 512],
                                      in0=ps, in1=u1[:, ct * 512:(ct + 1) * 512])
                    uu_cnt[(q, ct)] = ctx.S("g_ve").n
                vector.wait_ge(SE.h, se0 + ct + 1)
                vector.tensor_mul(out=msk2[:, moff + ct * 512:moff + (ct + 1) * 512],
                                  in0=acts[:, ct * 512:(ct + 1) * 512], in1=ps)
                tev_cnt[(q, r)] = ctx.S("g_ve").n
            bl.vector(sec_tev)
        tree_end[q] = ctx.S("g_ve").n

        if q >= 1:
            def sec_ostore(sync, q=q):
                for sub in range(4):
                    i2 = (q - 1) * 4 + sub
                    sync.wait_ge(OV.h, i2 + 1)
                    s_tok = (q - 1) * 512 + sub * 128
                    SOUT.issue(sync.dma_start(
                        out=out_dram[s_tok:s_tok + 128, :],
                        in_=xout[:, 0:D]))
            bl.gpsimd(sec_ostore)

    qt = NCH - 1

    def sec_x2pload_t(sync):
        for sub in range(4):
            i2 = qt * 4 + sub
            sync.wait_ge(OV.h, i2 - 1)
            s_tok = qt * 512 + sub * 128
            LX2.issue(sync.dma_start(out=x2p[:, (i2 % 2) * D:(i2 % 2 + 1) * D],
                                     in_=x2p_dram[s_tok:s_tok + 128, :]))
    bl.sync(sec_x2pload_t)
    for c in range(8):
        emit_omm_chain(qt, c)

    def sec_oev_t(vector):
        vector = SS(vector, OV)
        for sub in range(4):
            i2 = qt * 4 + sub
            vector.wait_ge(TO.h, chain_to[(qt, sub, 1)])
            LX2.wait_done(vector, i2)
            SOUT.wait_done(vector, i2 - 1)
            vector.tensor_add(out=xout[:, 0:D],
                              in0=pO[:, (i2 % 2) * 1024:(i2 % 2 + 1) * 1024],
                              in1=x2p[:, (i2 % 2) * D:(i2 % 2 + 1) * D])
    bl.vector(sec_oev_t)

    def sec_ostore_t(sync):
        for sub in range(4):
            i2 = qt * 4 + sub
            sync.wait_ge(OV.h, i2 + 1)
            s_tok = qt * 512 + sub * 128
            SOUT.issue(sync.dma_start(out=out_dram[s_tok:s_tok + 128, :],
                                      in_=xout[:, 0:D]))
    bl.gpsimd(sec_ostore_t)

    def sec_final(gpsimd):
        SOUT.wait_all(gpsimd)
    bl.gpsimd(sec_final)



PHASE_INPUTS = {
    "A": [], "B": ["Wn"], "C": [], "D": ["w1T", "w2T"],
    "E": [], "F": ["WvT", "WoT"], "G": ["WinT", "WoutT"],
}
TOP_CONSTS = {
    "A": ["ident", "eps_col"], "B": ["ident", "b_col"], "C": ["ident"],
    "D": ["b1_col"], "E": ["ident", "eps_col"],
    "F": ["fr_col"], "G": [],
}
HBM_T = {
    "xnC_h": ([128, NKT * T], BF16, "A", ["B", "D"]),
    "x1p_h": ([T, D], F32, "C", ["D"]),
    "x1_h": ([T, D], F32, "D", ["E", "F"]),
    "xn1C_h": ([128, NKT * T], BF16, "E", ["F", "G"]),
    "x2p_h": ([T, D], F32, "F", ["G"]),
}
PHASE_ORDER = ["A", "B", "C", "D", "E", "F", "G"]


def build_group(phases):
    nc = bass.Bass("TRN2", target_bir_lowering=False)
    in_names, out_names = [], []
    consts = set()
    dins = set()
    for ph in phases:
        consts.update(TOP_CONSTS[ph])
        dins.update(PHASE_INPUTS[ph])
    if "C" in phases:
        dins.update(["band", "b2_row"])
    if "F" in phases:
        dins.update(["iota128_row", "iota32_row"])
    if "G" in phases:
        dins.update(["bin_col", "S0all", "S1all"])
    if "C" in phases:
        assert "B" in phases
    if "F" in phases and "E" not in phases:
        pass

    es = contextlib.ExitStack()
    with es:
        dram = {}
        for nm in sorted(dins | consts):
            shape, dt = CONST_SPECS[nm]
            dram["d_" + nm] = nc.dram_tensor(nm, shape, dt, kind="ExternalInput").ap()
            in_names.append(nm)
        if "A" in phases or "C" in phases:
            dram["d_x"] = nc.dram_tensor("x", [T, D], F32, kind="ExternalInput").ap()
            in_names.append("x")
        pset = set(phases)
        for nm, (shape, dt, prod, cons) in HBM_T.items():
            produced = prod in pset
            consumed_here = bool(pset & set(cons))
            consumed_later = bool((set(cons) - pset))
            if not produced and not consumed_here:
                continue
            if produced and consumed_later:
                kind = "ExternalOutput"
                out_names.append(nm)
            elif produced:
                kind = "Internal"
            else:
                kind = "ExternalInput"
                in_names.append(nm)
            dram["d_" + nm] = nc.dram_tensor(nm, shape, dt, kind=kind).ap()
        if "F" in phases:
            dram["d_rC_h"] = nc.dram_tensor("rC_h", [128, NKT * T], BF16,
                                            kind="Internal").ap()
        if "G" in phases:
            dram["d_out"] = nc.dram_tensor("out", [T, D], F32,
                                           kind="ExternalOutput").ap()
            out_names.append("out")

        block = es.enter_context(nc.Block())
        ctx = Ctx(nc, es, block)
        ctx.t.update(dram)
        emit_preload(ctx, sorted(consts))

        if "A" in phases:
            with contextlib.ExitStack() as sA:
                emit_ln(ctx, "a", ctx.t["d_x"], "xnC", ctx.t["d_xnC_h"], sA,
                        dst_es=sA)
            barrier(ctx)
        with contextlib.ExitStack() as s_bc:
            if "B" in phases:
                with contextlib.ExitStack() as sB:
                    emit_fproj(ctx, ctx.t["d_xnC_h"], sB, dst_es=s_bc)
                barrier(ctx)
            if "C" in phases:
                with contextlib.ExitStack() as sC:
                    emit_preload(ctx, ["band", "b2_row"], es=sC)
                    emit_attn0(ctx, ctx.t["d_x"], ctx.t["d_x1p_h"], sC)
                barrier(ctx)
        if "D" in phases:
            with contextlib.ExitStack() as sD:
                emit_ffn0(ctx, ctx.t["d_xnC_h"], ctx.t["d_x1p_h"],
                          ctx.t["d_x1_h"], sD)
            barrier(ctx)
        with contextlib.ExitStack() as s_ef:
            if "E" in phases:
                with contextlib.ExitStack() as sE:
                    emit_ln(ctx, "e", ctx.t["d_x1_h"], "xn1C",
                            ctx.t["d_xn1C_h"], sE, dst_es=s_ef)
                barrier(ctx)
            if "F" in phases:
                with contextlib.ExitStack() as sF:
                    if "xn1C" not in ctx.t:
                        t = ctx.sb("xn1C", [128, NKT * T], BF16, s_ef)
                        LD = ctx.S("ld_xn1c")

                        def sec(sync):
                            LD.dinc(sync.dma_start(out=t[:], in_=ctx.t["d_xn1C_h"]))
                        block.sync(sec)
                        tgt = LD.n

                        def secw(eng):
                            eng.wait_ge(LD.h, tgt)
                        block.tensor(secw)
                    emit_preload(ctx, ["iota128_row", "iota32_row"], es=sF)
                    emit_rotor(ctx, ctx.t["d_x1_h"], ctx.t["d_x2p_h"],
                               ctx.t["d_rC_h"], sF)
                barrier(ctx)
        if "G" in phases:
            with contextlib.ExitStack() as sG:
                emit_preload(ctx, ["bin_col", "S0all", "S1all"], es=sG)
                emit_ffn1(ctx, ctx.t["d_xn1C_h"], ctx.t["d_x2p_h"],
                          ctx.t["d_out"], sG)
            barrier(ctx)
        if "G" not in phases:
            def sec_fin(gpsimd):
                if "spill" in ctx.sem:
                    gpsimd.wait_ge(ctx.sem["spill"].h, ctx.sem["spill"].n)
                for st_ in ctx.store_streams:
                    st_.wait_all(gpsimd)
            block.gpsimd(sec_fin)
    return nc, in_names, out_names


GROUPS_UNFUSED = [["A", "B", "C"], ["D"], ["E"], ["F"], ["G"]]
GROUPS_FUSED = [["A", "B", "C", "D", "E", "F", "G"]]
GROUPS = GROUPS_FUSED

_BUILD_CACHE = {}


def kernel(**inputs):
    common, per_core = host_prep(inputs)
    carry = [dict(pc) for pc in per_core]
    result = None
    for phases in GROUPS:
        key = tuple(phases)
        if key not in _BUILD_CACHE:
            _BUILD_CACHE[key] = build_group(phases)
        nc, in_names, out_names = _BUILD_CACHE[key]
        in_maps = []
        for ci in range(8):
            m = {}
            for nm in in_names:
                m[nm] = common[nm] if nm in common else carry[ci][nm]
            in_maps.append(m)
        res = run_bass_kernel_spmd(nc, in_maps, core_ids=list(range(8)))
        for ci in range(8):
            for nm in out_names:
                carry[ci][nm] = np.asarray(res.results[ci][nm])
        if "G" in phases:
            result = np.stack([carry[ci]["out"].astype(np.float32)
                               for ci in range(8)], axis=0)
    return result


# version 27
# speedup vs baseline: 1.0178x; 1.0178x over previous
import contextlib
import math

import numpy as np
import ml_dtypes

import concourse.bass as bass
import concourse.mybir as mybir
from concourse.bass_utils import run_bass_kernel_spmd

F32 = mybir.dt.float32
BF16 = mybir.dt.bfloat16
I32 = mybir.dt.int32
AF = mybir.ActivationFunctionType
ALU = mybir.AluOpType
AX = mybir.AxisListType

D = 1024
T = 4096
NKT = 8
NTT = 32
NCH = 8

TREE_DEPTH = 7
TREE_PAR = 8
N_NODES = 255
TREE_TILE_BASE = [0, 1, 2, 3, 4, 5, 7, 11]
NJT = 19
JW = NJT * 128

TWO_PI = 6.283185307179586
C1 = float(np.float32(6.28125))
C2 = float(np.float32(TWO_PI - 6.28125))
C3 = float(np.float32(TWO_PI - 6.28125 - float(np.float32(TWO_PI - 6.28125))))
INV2PI = float(np.float32(1.0 / TWO_PI))
HALF_PI = float(np.float32(math.pi / 2))
PI_F = float(np.float32(math.pi))
TWOPI_F = float(np.float32(TWO_PI))
MAGIC = 12582912.0
LN_EPS = 1e-5

BF = ml_dtypes.bfloat16



def _tree_perm():
    slot_src = np.full(JW, -1, dtype=np.int64)
    for p in range(TREE_PAR):
        for d in range(TREE_DEPTH + 1):
            base = 2 ** d - 1
            for k in range(2 ** d):
                j = p * N_NODES + base + k
                idx = p * (2 ** d) + k
                slot_src[128 * TREE_TILE_BASE[d] + idx] = j
    return slot_src


def _tree_scatter():
    smeta = []
    mats0, mats1 = [], []
    for d in range(TREE_DEPTH):
        n_par = TREE_PAR * 2 ** d
        n_child = 2 * n_par
        pt0, ct0 = TREE_TILE_BASE[d], TREE_TILE_BASE[d + 1]
        n_ct = (n_child + 127) // 128
        for ci in range(n_ct):
            S0 = np.zeros((128, 128), dtype=np.float32)
            S1 = np.zeros((128, 128), dtype=np.float32)
            for cl in range(min(128, n_child - ci * 128)):
                cg = ci * 128 + cl
                q = cg // 2
                b = cg % 2
                (S1 if b else S0)[q % 128, cl] = 1.0
            smeta.append((ct0 + ci, pt0 + (ci * 64) // 128))
            mats0.append(S0)
            mats1.append(S1)
    return smeta, (np.concatenate(mats0, 1).astype(BF),
                   np.concatenate(mats1, 1).astype(BF))


TREE_SMETA, (_S0ALL, _S1ALL) = _tree_scatter()
N_SC = len(TREE_SMETA)


def _band_patch():
    Cm = np.zeros((128, 146), dtype=np.float32)
    for i in range(128):
        for w in range(146):
            delta = i - (w - 9)
            Cm[i, w] = -0.01 * (10.0 - min(abs(delta), 10.0))
    return Cm


def host_prep(inp):
    f32 = np.float32
    cont = np.ascontiguousarray
    c = {}
    _ = inp["L1_attn_Wk"]
    c["Wn"] = cont(inp["L0_attn_W"].astype(f32))
    c["b_col"] = cont(inp["L0_attn_b"].astype(f32).reshape(8, 128).T)
    c["w1T"] = cont(inp["L0_ffn_w1"].astype(f32).T)
    c["b1_col"] = cont(inp["L0_ffn_b1"].astype(f32).reshape(32, 128).T)
    c["w2T"] = cont(inp["L0_ffn_w2"].astype(f32).T)
    c["b2_row"] = cont(inp["L0_ffn_b2"].astype(f32).reshape(1, 1024))
    c["WvT"] = cont(inp["L1_attn_Wv"].astype(f32).T)
    c["WoT"] = cont(inp["L1_attn_Wo"].astype(f32).T)
    c["fr_col"] = cont(inp["L1_attn_freqs"].astype(f32).reshape(8, 128).T)
    slot = _tree_perm()
    used = slot >= 0
    WinP = np.zeros((JW, D), dtype=f32)
    binP = np.zeros(JW, dtype=f32)
    WoutP = np.zeros((D, JW), dtype=f32)
    WinP[used] = inp["L1_ffn_W_in"].astype(f32)[slot[used]]
    binP[used] = inp["L1_ffn_b_in"].astype(f32)[slot[used]]
    WoutP[:, used] = inp["L1_ffn_W_out"].astype(f32)[:, slot[used]]
    c["WinT"] = cont(WinP.T)
    c["bin_col"] = cont(binP.reshape(NJT, 128).T)
    c["WoutT"] = cont(WoutP.T)
    c["ident"] = np.eye(128, dtype=f32).astype(BF)
    c["eps_col"] = np.full((128, 1), 1e-5, dtype=f32)
    c["band"] = _band_patch()
    c["iota128_row"] = np.arange(128, dtype=f32).reshape(1, 128)
    c["iota32_row"] = np.arange(32, dtype=f32).reshape(1, 32)
    t_idx = np.arange(T)
    c["brow"] = (t_idx // 128).astype(BF).reshape(1, T)
    c["urow"] = (t_idx % 128).astype(BF).reshape(1, T)
    c["S0all"] = _S0ALL
    c["S1all"] = _S1ALL
    per_core = [{"x": cont(inp["L0_attn_x"].astype(f32)[i])} for i in range(8)]
    return c, per_core


CONST_SPECS = {
    "Wn": ([1024, 1024], F32), "b_col": ([128, 8], F32),
    "w1T": ([1024, 4096], F32), "b1_col": ([128, 32], F32),
    "w2T": ([4096, 1024], F32), "b2_row": ([1, 1024], F32),
    "WvT": ([1024, 1024], F32), "WoT": ([1024, 1024], F32),
    "fr_col": ([128, 8], F32), "WinT": ([1024, JW], F32),
    "bin_col": ([128, NJT], F32), "WoutT": ([JW, 1024], F32),
    "ident": ([128, 128], BF16), "band": ([128, 146], F32),
    "eps_col": ([128, 1], F32),
    "iota128_row": ([1, 128], F32), "iota32_row": ([1, 32], F32),
    "brow": ([1, T], BF16), "urow": ([1, T], BF16),
    "S0all": ([128, N_SC * 128], BF16), "S1all": ([128, N_SC * 128], BF16),
}



class Sem:
    __slots__ = ("h", "n")

    def __init__(self, nc, es, name):
        self.h = es.enter_context(nc.semaphore(name))
        self.n = 0

    def inc(self, inst, k=1):
        inst.then_inc(self.h, k)
        self.n += k
        return self.n

    def dinc(self, inst):
        return self.inc(inst, 16)


class DStream:

    def __init__(self, ctx, name, nbuf=2):
        self.sems = [ctx.S(name + str(j)) for j in range(nbuf)]
        self.nbuf = nbuf
        self.count = 0

    def issue(self, inst):
        i = self.count
        self.sems[i % self.nbuf].inc(inst, 16)
        self.count += 1
        return i

    def wait_done(self, eng, i):
        if i < 0:
            return
        eng.wait_ge(self.sems[i % self.nbuf].h, 16 * (i // self.nbuf + 1))

    def wait_all(self, eng):
        for s in self.sems:
            if s.n:
                eng.wait_ge(s.h, s.n)


_SS_WRAP = {
    "tensor_copy", "tensor_tensor", "tensor_add", "tensor_sub", "tensor_mul",
    "tensor_max", "tensor_scalar", "tensor_single_scalar", "tensor_scalar_add",
    "tensor_scalar_sub", "tensor_scalar_mul", "tensor_scalar_max",
    "tensor_scalar_min", "tensor_relu", "tensor_reduce", "reduce_max",
    "reduce_sum", "scalar_tensor_tensor", "tensor_tensor_scan", "memset",
    "memzero", "reciprocal", "select", "transpose", "activation", "copy",
    "sign", "square", "sqrt", "mul", "add", "engine_nop", "bn_stats",
    "bn_aggr", "stream_shuffle", "cody_waite_cascade", "add_range_wrap",
    "affine_then_add", "affine_mul_reduce", "tensor_tensor_reduce",
}


class SS:

    def __init__(self, raw, sem):
        self._r = raw
        self._s = sem

    def __getattr__(self, name):
        f = getattr(self._r, name)
        if name in _SS_WRAP:
            def w(*a, **k):
                if self._s.n:
                    self._r.wait_ge(self._s.h, self._s.n)
                ins = f(*a, **k)
                self._s.inc(ins)
                return ins
            return w
        return f


class Chain:

    def __init__(self, raw):
        self.raw = raw
        self.prev = None

    def wait_ge(self, h, n):
        self.raw.wait_ge(h, n)

    def do(self, sem, f):
        if self.prev is not None:
            ps, pn = self.prev
            self.raw.wait_ge(ps.h, pn)
        ins = f(self.raw)
        sem.inc(ins)
        self.prev = (sem, sem.n)
        return ins


class Ctx:
    def __init__(self, nc, es, block):
        self.nc = nc
        self.es = es
        self.block = block
        self.t = {}
        self.sem = {}
        self.store_streams = []

    def S(self, name):
        if name not in self.sem:
            self.sem[name] = Sem(self.nc, self.es, name)
        return self.sem[name]

    def sb(self, name, shape, dtype, es=None, side="left"):
        h = (es or self.es).enter_context(
            self.nc.sbuf_tensor(name, shape, dtype, side=side))
        self.t[name] = h
        return h

    def ps(self, name, shape, dtype, es=None):
        h = (es or self.es).enter_context(self.nc.psum_tensor(name, shape, dtype))
        self.t[name] = h
        return h


def barrier(ctx):
    bl = ctx.block
    sems = [ctx.S("bar_" + e) for e in ("sy", "te", "ve", "se", "ge")]
    all_waits = [(s.h, s.n) for nm, s in ctx.sem.items()
                 if s.n and not nm.startswith("bar_")]

    def mk(idx):
        def f(eng):
            for h, n in all_waits:
                eng.wait_ge(h, n)
            sems[idx].inc(eng.nop())
        return f
    bl.sync(mk(0)); bl.tensor(mk(1)); bl.vector(mk(2)); bl.scalar(mk(3)); bl.gpsimd(mk(4))
    targets = [s.n for s in sems]

    def mkw(my_idx):
        def f(eng):
            for i, s in enumerate(sems):
                if i != my_idx:
                    eng.wait_ge(s.h, targets[i])
        return f
    bl.sync(mkw(0)); bl.tensor(mkw(1)); bl.vector(mkw(2)); bl.scalar(mkw(3)); bl.gpsimd(mkw(4))


def emit_preload(ctx, names, es=None):
    bl = ctx.block
    LDC = ctx.S("ldc")
    table = {
        "ident": ("ident_sb", [128, 128], BF16, False),
        "eps_col": ("eps_col_sb", [128, 1], F32, False),
        "band": ("band_sb", [128, 146], F32, False),
        "b_col": ("b_col_sb", [128, 8], F32, False),
        "b1_col": ("b1_col_sb", [128, 32], F32, False),
        "bin_col": ("bin_col_sb", [128, NJT], F32, False),
        "fr_col": ("fr_col_sb", [128, 8], F32, False),
        "S0all": ("S0_sb", [128, N_SC * 128], BF16, False),
        "S1all": ("S1_sb", [128, N_SC * 128], BF16, False),
        "iota128_row": ("iota128_sb", [128, 128], F32, True),
        "iota32_row": ("iota32_sb", [128, 32], F32, True),
        "b2_row": ("b2bc_sb", [128, 1024], F32, True),
        "brow": ("brow_sb", [128, T], BF16, True),
        "urow": ("urow_sb", [128, T], BF16, True),
    }
    plans = []
    for nm in names:
        sbn, shape, dt, bc = table[nm]
        if sbn not in ctx.t:
            ctx.sb(sbn, shape, dt, es)
        plans.append((sbn, nm, bc))

    def sec(sync):
        for sbn, nm, bc in plans:
            src = ctx.t["d_" + nm]
            if bc:
                ins = sync.dma_start(out=ctx.t[sbn][:],
                                     in_=src[0:1, :].partition_broadcast(128))
            else:
                ins = sync.dma_start(out=ctx.t[sbn][:], in_=src)
            LDC.dinc(ins)
    bl.sync(sec)




def emit_ln(ctx, tag, src_dram, dst_name, spill_dram, es, dst_es=None,
            dst_side="left"):
    bl = ctx.block
    DEP = 4
    dst = ctx.sb(dst_name, [128, NKT * T], BF16, dst_es or es, side=dst_side)
    xb = ctx.sb(tag + "_x", [128, DEP * D], F32, es)
    xnT = ctx.sb(tag + "_xnT", [128, DEP * D], BF16, es)
    st = ctx.sb(tag + "_st", [128, 16 * DEP], F32, es)
    pT = ctx.ps(tag + "_pT", [128, DEP * 1024], BF16, es)
    ident = ctx.t["ident_sb"]
    ldc_target = ctx.S("ldc").n

    LD = [DStream(ctx, tag + "ldx"), DStream(ctx, tag + "ldy")]
    VE = ctx.S(tag + "ve"); SE = ctx.S(tag + "se"); TEs = ctx.S(tag + "te")
    SP = ctx.S("spill")
    xnc = {}; agc = {}; rsc = {}; cpc = {}; tec = {}; stc = {}

    dstv = dst[:].rearrange("p (k t) -> p k t", k=NKT)

    for i in range(NTT + 2):
        s = i % DEP

        if i < NTT:
            def sec_load(eng, i=i, s=s):
                if i >= DEP:
                    eng.wait_ge(VE.h, xnc[i - DEP])
                LD[i % 2].issue(eng.dma_start(
                    out=xb[:, s * D:(s + 1) * D],
                    in_=src_dram[i * 128:(i + 1) * 128, :]))
            (bl.sync if i % 2 == 0 else bl.gpsimd)(sec_load)

        def sec_ve(vector, i=i):
            vector = SS(vector, VE)
            if i < NTT:
                s = i % DEP
                c = st[:, s * 16:s * 16 + 16]
                LD[i % 2].wait_done(vector, i // 2)
                if i >= DEP:
                    vector.wait_ge(SE.h, rsc[i - DEP])
                vector.bn_stats(out=c[:, 0:6], in_=xb[:, s * D:s * D + 512])
                vector.bn_stats(out=c[:, 6:12], in_=xb[:, s * D + 512:s * D + 1024])
                vector.bn_aggr(out=c[:, 12:14], in_=c[:, 0:12])
                agc[i] = VE.n
            if 1 <= i <= NTT:
                j = i - 1
                sj = j % DEP
                c = st[:, sj * 16:sj * 16 + 16]
                vector.wait_ge(SE.h, rsc[j])
                vector.reciprocal(out=c[:, 15:16], in_=c[:, 14:15])
                vector.tensor_scalar(out=xnT[:, sj * D:(sj + 1) * D],
                                     in0=xb[:, sj * D:(sj + 1) * D],
                                     scalar1=c[:, 12:13], scalar2=c[:, 15:16],
                                     op0=ALU.subtract, op1=ALU.mult)
                xnc[j] = VE.n
            if i >= 2 and (i - 2) % 2 == 1:
                j = i - 2
                sj = j % DEP
                vector.wait_ge(TEs.h, tec[j])
                vector.tensor_copy(out=dstv[:, :, j * 128:(j + 1) * 128],
                                   in_=pT[:, sj * 1024:(sj + 1) * 1024].rearrange(
                                       "p (k t) -> p k t", k=NKT))
                cpc[j] = ("ve", VE.n)
        bl.vector(sec_ve)

        if i >= 1 and i - 1 < NTT:
            def sec_pe(tensor, i=i):
                j = i - 1
                sj = j % DEP
                if j == 0:
                    tensor.wait_ge(ctx.S("ldc").h, ldc_target)
                tensor.wait_ge(VE.h, xnc[j])
                if j >= DEP:
                    eng, n = cpc[j - DEP]
                    tensor.wait_ge(VE.h if eng == "ve" else SE.h, n)
                ins = None
                for k in range(NKT):
                    ins = tensor.transpose(
                        pT[:, sj * 1024 + k * 128:sj * 1024 + (k + 1) * 128],
                        xnT[:, sj * D + k * 128:sj * D + (k + 1) * 128],
                        ident[:])
                TEs.inc(ins)
                tec[j] = TEs.n
            bl.tensor(sec_pe)

        def sec_se(scalar, i=i):
            scalar = SS(scalar, SE)
            if i < NTT:
                s = i % DEP
                if i == 0:
                    scalar.wait_ge(ctx.S("ldc").h, ldc_target)
                scalar.wait_ge(VE.h, agc[i])
                scalar.activation(out=st[:, s * 16 + 14:s * 16 + 15],
                                  in_=st[:, s * 16 + 13:s * 16 + 14],
                                  func=AF.Sqrt, bias=ctx.t["eps_col_sb"][:, 0:1])
                rsc[i] = SE.n
            if i >= 2 and (i - 2) % 2 == 0:
                j = i - 2
                sj = j % DEP
                scalar.wait_ge(TEs.h, tec[j])
                scalar.copy(out=dstv[:, :, j * 128:(j + 1) * 128],
                            in_=pT[:, sj * 1024:(sj + 1) * 1024].rearrange(
                                "p (k t) -> p k t", k=NKT))
                cpc[j] = ("se", SE.n)
        bl.scalar(sec_se)

    if spill_dram is not None:
        spillv = spill_dram.rearrange("p (k t) -> p k t", k=NKT)
        chunk_counts = []

        def sec_spill(sync):
            for q in range(NCH):
                for j in (4 * q + 2, 4 * q + 3):
                    eng, n = cpc[j]
                    sync.wait_ge(VE.h if eng == "ve" else SE.h, n)
                SP.dinc(sync.dma_start(out=spillv[:, :, q * 512:(q + 1) * 512],
                                       in_=dstv[:, :, q * 512:(q + 1) * 512]))
                chunk_counts.append(SP.n)
        bl.sync(sec_spill)
        ctx.spill_chunks = chunk_counts
    return dst




def emit_fproj(ctx, xnC_dram, es, dst_es):
    bl = ctx.block
    fC = ctx.sb("fC", [128, NKT * T], BF16, dst_es)
    fT = ctx.sb("fT", [128, NTT * 1024], BF16, dst_es)
    Wb = ctx.sb("Wb", [128, NKT * 1024], BF16, es)
    xchunk = ctx.sb("b_xc", [128, 2 * NKT * 512], BF16, es)
    pf = ctx.ps("b_pf", [128, 2 * 512], F32, es)
    pT = ctx.ps("b_pT", [128, 2 * 1024], BF16, es)
    ident = ctx.t["ident_sb"]
    b_col = ctx.t["b_col_sb"]
    dWn = ctx.t["d_Wn"]
    xnCv_dram = xnC_dram.rearrange("p (k t) -> p k t", k=NKT)
    ldc_target = ctx.S("ldc").n

    LG = ctx.S("b_lg"); LXC = DStream(ctx, "b_lxc"); TE = ctx.S("b_te"); SE = ctx.S("b_se")
    T2 = ctx.S("b_t2"); V2 = ctx.S("b_v2")

    def sec_wload(gpsimd):
        for k in range(NKT):
            LG.dinc(gpsimd.dma_start(out=Wb[:, k * 1024:(k + 1) * 1024],
                                     in_=dWn[k * 128:(k + 1) * 128, :]))
    bl.gpsimd(sec_wload)

    def sec_xcload(sync):
        for q in range(NCH):
            sync.wait_ge(ctx.S("spill").h, ctx.spill_chunks[q])
            if q >= 2:
                sync.wait_ge(TE.h, (q - 1) * NKT)
            LXC.issue(sync.dma_start(
                out=xchunk[:, (q % 2) * NKT * 512:(q % 2 + 1) * NKT * 512].rearrange(
                    "p (k t) -> p k t", k=NKT),
                in_=xnCv_dram[:, :, q * 512:(q + 1) * 512]))
    bl.sync(sec_xcload)

    def sec_mm(tensor):
        tensor.wait_ge(LG.h, 16 * NKT)
        tensor.wait_ge(ctx.S("ldc").h, ldc_target)
        for q in range(NCH):
            LXC.wait_done(tensor, q)
            for j in range(NKT):
                r = q * NKT + j
                if r >= 2:
                    tensor.wait_ge(SE.h, r - 1)
                ins = None
                for k in range(NKT):
                    ins = tensor.matmul(pf[:, (r % 2) * 512:(r % 2 + 1) * 512],
                                        Wb[:, k * 1024 + j * 128:k * 1024 + (j + 1) * 128],
                                        xchunk[:, (q % 2) * NKT * 512 + k * 512:(q % 2) * NKT * 512 + (k + 1) * 512],
                                        start=(k == 0), stop=(k == NKT - 1))
                TE.inc(ins)
        tensor.wait_ge(SE.h, NKT * NCH)
        for i in range(NTT):
            if i >= 2:
                tensor.wait_ge(V2.h, i - 1)
            ins = None
            for k in range(NKT):
                ins = tensor.transpose(pT[:, (i % 2) * 1024 + k * 128:(i % 2) * 1024 + (k + 1) * 128],
                                       fC[:, k * T + i * 128:k * T + (i + 1) * 128],
                                       ident[:])
            T2.inc(ins)
    bl.tensor(sec_mm)

    def sec_evac(scalar):
        scalar = SS(scalar, SE)
        for q in range(NCH):
            for j in range(NKT):
                r = q * NKT + j
                scalar.wait_ge(TE.h, r + 1)
                scalar.activation(out=fC[:, j * T + q * 512:j * T + (q + 1) * 512],
                                  in_=pf[:, (r % 2) * 512:(r % 2 + 1) * 512],
                                  func=AF.Identity, bias=b_col[:, j:j + 1])
    bl.scalar(sec_evac)

    def sec_trcopy(vector):
        vector = SS(vector, V2)
        for i in range(NTT):
            vector.wait_ge(T2.h, i + 1)
            vector.tensor_copy(out=fT[:, i * 1024:(i + 1) * 1024],
                               in_=pT[:, (i % 2) * 1024:(i % 2 + 1) * 1024])
    bl.vector(sec_trcopy)
    return fC, fT




def emit_attn0(ctx, x_dram, x1p_dram, es):
    bl = ctx.block
    fC, fT = ctx.t["fC"], ctx.t["fT"]
    e_sb = ctx.sb("e_sb", [128, 2 * T], F32, es)
    a_sb = ctx.sb("a_sb", [128, 2 * T], BF16, es)
    aT_sb = ctx.sb("aT_sb", [128, T], BF16, es)
    xr = ctx.sb("c_xr", [128, 2 * D], F32, es)
    x1p = ctx.sb("c_x1p", [128, 2 * D], F32, es)
    cst = ctx.sb("c_st", [128, 4], F32, es)
    inv = ctx.sb("c_inv", [128, 2], F32, es)
    pe_ = ctx.ps("c_pe", [128, 2 * 512], F32, es)
    pA = ctx.ps("c_pA", [128, 2 * 1024], BF16, es)
    pO = ctx.ps("c_pO", [128, 2 * 1024], F32, es)
    band = ctx.t["band_sb"]
    b2bc = ctx.t["b2bc_sb"]
    ident = ctx.t["ident_sb"]

    TEs = ctx.S("c_te"); EVa = ctx.S("c_eva"); EVb = ctx.S("c_evb"); GE = ctx.S("c_ge")
    SEx = ctx.S("c_se"); VEi = ctx.S("c_vi"); TT_ = ctx.S("c_tt")
    TEV = ctx.S("c_tev"); TO = ctx.S("c_to"); OV = ctx.S("c_ov")
    VEm = ctx.S("c_vm")
    ldc_c = ctx.S("ldc").n
    LXR = DStream(ctx, "c_lxr"); SP1 = DStream(ctx, "c_sp1")
    ctx.store_streams.append(SP1)

    def sec_pe(tensor):
        for s in range(NTT + 1):
            if s < NTT:
                for q in range(NCH):
                    r = s * NCH + q
                    if r >= 2:
                        if r % 2 == 0:
                            tensor.wait_ge(EVa.h, r // 2)
                        else:
                            tensor.wait_ge(EVb.h, (r - 1) // 2)
                    ins = None
                    for k in range(NKT):
                        ins = tensor.matmul(pe_[:, (r % 2) * 512:(r % 2 + 1) * 512],
                                            fC[:, k * T + s * 128:k * T + (s + 1) * 128],
                                            fC[:, k * T + q * 512:k * T + (q + 1) * 512],
                                            start=(k == 0), stop=(k == NKT - 1))
                    TEs.inc(ins)
            if s >= 1:
                sp = s - 1
                tensor.wait_ge(SEx.h, sp + 1)
                for g in range(4):
                    gi = sp * 4 + g
                    if gi >= 2:
                        tensor.wait_ge(TEV.h, gi - 1)
                    ins = None
                    for u in range(8):
                        i = g * 8 + u
                        ins = tensor.transpose(
                            pA[:, (gi % 2) * 1024 + u * 128:(gi % 2) * 1024 + (u + 1) * 128],
                            a_sb[:, (sp % 2) * T + i * 128:(sp % 2) * T + (i + 1) * 128],
                            ident[:])
                    TT_.inc(ins)
                    if g >= 1:
                        tensor.wait_ge(TEV.h, sp * 4 + g)
                        if g == 1 and sp >= 2:
                            tensor.wait_ge(OV.h, 2 * (sp - 1))
                        for q2 in range(2):
                            for i in range((g - 1) * 8, g * 8):
                                tensor.matmul(
                                    pO[:, (sp % 2) * 1024 + q2 * 512:(sp % 2) * 1024 + (q2 + 1) * 512],
                                    aT_sb[:, i * 128:(i + 1) * 128],
                                    fT[:, i * 1024 + q2 * 512:i * 1024 + (q2 + 1) * 512],
                                    start=(i == 0), stop=False)
                tensor.wait_ge(TEV.h, sp * 4 + 4)
                for q2 in range(2):
                    ins = None
                    for i in range(24, NTT):
                        ins = tensor.matmul(
                            pO[:, (sp % 2) * 1024 + q2 * 512:(sp % 2) * 1024 + (q2 + 1) * 512],
                            aT_sb[:, i * 128:(i + 1) * 128],
                            fT[:, i * 1024 + q2 * 512:i * 1024 + (q2 + 1) * 512],
                            start=False, stop=(i == NTT - 1))
                    TO.inc(ins)
    bl.tensor(sec_pe)

    def sec_pool(gpsimd):
        gpsimd = SS(gpsimd, GE)
        gpsimd.wait_ge(ctx.S("ldc").h, ldc_c)
        for s in range(NTT):
            gpsimd.wait_ge(EVa.h, (s + 1) * 4)
            gpsimd.wait_ge(EVb.h, (s + 1) * 4)
            base = (s % 2) * T
            lo = s * 128 - 9
            if s == 0:
                gpsimd.tensor_add(out=e_sb[:, base:base + 137],
                                  in0=e_sb[:, base:base + 137], in1=band[:, 9:146])
            elif s == NTT - 1:
                gpsimd.tensor_add(out=e_sb[:, base + lo:base + T],
                                  in0=e_sb[:, base + lo:base + T], in1=band[:, 0:137])
            else:
                gpsimd.tensor_add(out=e_sb[:, base + lo:base + lo + 146],
                                  in0=e_sb[:, base + lo:base + lo + 146], in1=band[:])
    bl.gpsimd(sec_pool)

    def sec_act(scalar):
        ch = Chain(scalar)

        def emit_exp(s):
            ch.wait_ge(VEm.h, s + 1)
            if s >= 2:
                ch.wait_ge(TT_.h, (s - 1) * 4)
                ch.wait_ge(VEi.h, s - 1)
            ch.do(SEx, lambda sc, s=s: sc.activation(
                out=a_sb[:, (s % 2) * T:(s % 2 + 1) * T],
                in_=e_sb[:, (s % 2) * T:(s % 2 + 1) * T],
                func=AF.Exp, bias=cst[:, (s % 2):(s % 2) + 1],
                accum_out=cst[:, 2 + (s % 2):3 + (s % 2)]))

        def emit_tevc(sp):
            for g in range(4):
                gi = sp * 4 + g
                ch.wait_ge(TT_.h, gi + 1)
                if sp >= 1 and g == 0:
                    ch.wait_ge(TO.h, 2 * sp)
                ch.do(TEV, lambda sc, g=g, gi=gi: sc.copy(
                    out=aT_sb[:, g * 1024:(g + 1) * 1024],
                    in_=pA[:, (gi % 2) * 1024:(gi % 2 + 1) * 1024]))

        for s in range(NTT):
            if s >= 1:
                emit_exp(s - 1)
            for q in range(0, NCH, 2):
                r = s * NCH + q
                ch.wait_ge(TEs.h, r + 1)
                if s >= 2 and q == 0:
                    ch.wait_ge(SEx.h, s - 1)
                ch.do(EVa, lambda sc, s=s, q=q, r=r: sc.copy(
                    out=e_sb[:, (s % 2) * T + q * 512:(s % 2) * T + (q + 1) * 512],
                    in_=pe_[:, (r % 2) * 512:(r % 2 + 1) * 512]))
            if s >= 1:
                emit_tevc(s - 1)
        emit_exp(NTT - 1)
        emit_tevc(NTT - 1)
    bl.scalar(sec_act)

    def sec_dve(vector):
        ch = Chain(vector)
        ch.wait_ge(ctx.S("ldc").h, ldc_c)

        def emit_tail(sp):
            ch.wait_ge(SEx.h, sp + 1)
            ch.do(VEi, lambda v, sp=sp: v.reciprocal(
                out=inv[:, (sp % 2):(sp % 2) + 1],
                in_=cst[:, 2 + (sp % 2):3 + (sp % 2)]))
            ch.wait_ge(TO.h, 2 * sp + 2)
            LXR.wait_done(ch, sp)
            SP1.wait_done(ch, sp - 2)
            ch.do(OV, lambda v, sp=sp: v.scalar_tensor_tensor(
                out=x1p[:, (sp % 2) * D:(sp % 2 + 1) * D],
                in0=pO[:, (sp % 2) * 1024:(sp % 2 + 1) * 1024],
                scalar=inv[:, (sp % 2):(sp % 2) + 1],
                in1=xr[:, (sp % 2) * D:(sp % 2 + 1) * D],
                op0=ALU.mult, op1=ALU.add))
            ch.do(OV, lambda v, sp=sp: v.tensor_add(
                out=x1p[:, (sp % 2) * D:(sp % 2 + 1) * D],
                in0=x1p[:, (sp % 2) * D:(sp % 2 + 1) * D],
                in1=b2bc[:]))

        for s in range(NTT):
            for q in range(1, NCH, 2):
                r = s * NCH + q
                ch.wait_ge(TEs.h, r + 1)
                if s >= 2 and q == 1:
                    ch.wait_ge(SEx.h, s - 1)
                ch.do(EVb, lambda v, s=s, q=q, r=r: v.tensor_copy(
                    out=e_sb[:, (s % 2) * T + q * 512:(s % 2) * T + (q + 1) * 512],
                    in_=pe_[:, (r % 2) * 512:(r % 2 + 1) * 512]))
            ch.wait_ge(GE.h, s + 1)
            ch.do(VEm, lambda v, s=s: v.reduce_max(
                out=cst[:, (s % 2):(s % 2) + 1],
                in_=e_sb[:, (s % 2) * T:(s % 2) * T + T], axis=AX.X, negate=True))
            if s >= 1:
                emit_tail(s - 1)
        emit_tail(NTT - 1)
    bl.vector(sec_dve)

    def sec_sync(sync):
        def store(sp):
            sync.wait_ge(OV.h, 2 * (sp + 1))
            SP1.issue(sync.dma_start(out=x1p_dram[sp * 128:(sp + 1) * 128, :],
                                     in_=x1p[:, (sp % 2) * D:(sp % 2 + 1) * D]))
        for s in range(NTT):
            if s >= 2:
                sync.wait_ge(OV.h, 2 * (s - 1))
            LXR.issue(sync.dma_start(out=xr[:, (s % 2) * D:(s % 2 + 1) * D],
                                     in_=x_dram[s * 128:(s + 1) * 128, :]))
            if s >= 2:
                store(s - 2)
        store(NTT - 2)
        store(NTT - 1)
        SP1.wait_all(sync)
    bl.sync(sec_sync)




def emit_ffn0(ctx, xnC_dram, x1p_dram, x1_dram, es):
    bl = ctx.block
    sw1T = ctx.sb("sw1T", [128, NKT * T], BF16, es)
    sw2T = ctx.sb("sw2T", [128, NTT * 1024], BF16, es)
    xchunk = ctx.sb("d_xc", [128, 2 * NKT * 512], BF16, es)
    x1p = ctx.sb("d_x1p", [128, 2 * D], F32, es)
    x1o = ctx.sb("d_x1o", [128, 2 * D], F32, es)
    a2bc = ctx.sb("d_a2bc", [128, 1024], BF16, es)
    a1col = ctx.sb("d_a1col", [128, NTT], F32, es)
    es_prep = contextlib.ExitStack()
    wraw = ctx.sb("d_wraw", [128, 4 * T], BF16, es_prep)
    a1bc = ctx.sb("d_a1bc", [128, T], BF16, es_prep)
    ones128 = ctx.sb("d_ones", [128, 128], BF16, es_prep)
    pAw = ctx.ps("d_pAw", [128, NCH * 512], F32, es_prep)
    b1_col = ctx.t["b1_col_sb"]
    dw1T, dw2T = ctx.t["d_w1T"], ctx.t["d_w2T"]
    xnCv_dram = xnC_dram.rearrange("p (k t) -> p k t", k=NKT)

    LW = ctx.S("d_lw"); SE = ctx.S("d_se"); TE = ctx.S("d_te"); VE = ctx.S("d_ve")
    LXC = ctx.S("d_lxc"); LX1 = DStream(ctx, "d_lx1"); SX1 = DStream(ctx, "d_sx1")
    ctx.store_streams.append(SX1)
    TH = ctx.S("d_th"); SH = ctx.S("d_sh"); TO = ctx.S("d_to"); OV = ctx.S("d_ov")

    def sec_ones(vector):
        vector = SS(vector, VE)
        vector.memset(ones128[:], 1.0)
    bl.vector(sec_ones)

    def sec_wload(gpsimd):
        for k in range(NKT):
            if k >= 4:
                gpsimd.wait_ge(TE.h, (k - 3) * NCH)
            LW.dinc(gpsimd.dma_start(out=wraw[:, (k % 4) * T:(k % 4 + 1) * T],
                                     in_=dw1T[k * 128:(k + 1) * 128, :]))
        for jt in range(NTT):
            r = NKT + jt
            if jt < 4:
                gpsimd.wait_ge(TE.h, NKT * NCH)
            else:
                gpsimd.wait_ge(TE.h, NKT * NCH + (jt - 3) * 2)
            LW.dinc(gpsimd.dma_start(out=wraw[:, (r % 4) * T:(r % 4) * T + 1024],
                                     in_=dw2T[jt * 128:(jt + 1) * 128, :]))
    bl.gpsimd(sec_wload)

    def sec_wact(scalar):
        scalar = SS(scalar, SE)
        for k in range(NKT):
            w = wraw[:, (k % 4) * T:(k % 4 + 1) * T]
            scalar.wait_ge(LW.h, 16 * (k + 1))
            scalar.activation(out=sw1T[:, k * T:(k + 1) * T], in_=w,
                              func=AF.Sign)
            scalar.activation(out=w, in_=w, func=AF.Abs)
        for jt in range(NTT):
            r = NKT + jt
            w = wraw[:, (r % 4) * T:(r % 4) * T + 1024]
            scalar.wait_ge(LW.h, 16 * (NKT + jt + 1))
            scalar.activation(out=sw2T[:, jt * 1024:(jt + 1) * 1024],
                              in_=w, func=AF.Sign)
            scalar.activation(out=w, in_=w, func=AF.Abs)
    bl.scalar(sec_wact)

    def sec_alpha_mm(tensor):
        tensor.wait_ge(VE.h, 1)
        for k in range(NKT):
            tensor.wait_ge(SE.h, 2 * k + 2)
            for jc in range(NCH):
                TE.inc(tensor.matmul(pAw[:, jc * 512:(jc + 1) * 512], ones128[:],
                                     wraw[:, (k % 4) * T + jc * 512:(k % 4) * T + (jc + 1) * 512],
                                     start=(k == 0), stop=(k == NKT - 1)))
        for jt in range(NTT):
            r = NKT + jt
            tensor.wait_ge(SE.h, 2 * NKT + 2 * jt + 2)
            if jt == 0:
                tensor.wait_ge(VE.h, 2)
            ins = None
            for dc in range(2):
                ins = tensor.matmul(pAw[:, dc * 512:(dc + 1) * 512], ones128[:],
                                    wraw[:, (r % 4) * T + dc * 512:(r % 4) * T + (dc + 1) * 512],
                                    start=(jt == 0), stop=(jt == NTT - 1))
            TE.inc(ins, 2)
    bl.tensor(sec_alpha_mm)

    def sec_alpha_ev(vector):
        vector = SS(vector, VE)
        vector.wait_ge(TE.h, NKT * NCH)
        vector.tensor_scalar(out=a1bc[:], in0=pAw[:], scalar1=1.0 / D,
                             scalar2=None, op0=ALU.mult)
        vector.wait_ge(TE.h, NKT * NCH + NTT * 2)
        vector.tensor_scalar(out=a2bc[:], in0=pAw[:, 0:1024], scalar1=1.0 / T,
                             scalar2=None, op0=ALU.mult)
    bl.vector(sec_alpha_ev)

    A1L = ctx.S("d_a1l")

    def sec_a1x(gpsimd):
        gpsimd.wait_ge(VE.h, 2)
        A1L.dinc(gpsimd.dma_start(out=ctx.t["d_a1s"], in_=a1bc[0:1, :]))
        gpsimd.wait_ge(A1L.h, 16)
        A1L.dinc(gpsimd.dma_start(
            out=a1col[:],
            in_=ctx.t["d_a1s"].rearrange("(j p) -> p j", p=128),
            allow_slow_non_contiguous=True))
    bl.gpsimd(sec_a1x)
    VE_PREP = ctx.S("d_ve").n
    es_prep.close()
    hC = ctx.sb("d_hC", [128, NTT * 512], BF16, es)

    pH = ctx.ps("d_pH", [128, 2 * 512], F32, es)
    pO = ctx.ps("d_pO", [128, 2 * 1024], F32, es)

    ldc_d = ctx.S("ldc").n

    def sec_sync(sync):
        sync.wait_ge(ctx.S("ldc").h, ldc_d)
        sync.wait_ge(ctx.S("spill").h, ctx.S("spill").n)

        def store(i):
            sync.wait_ge(OV.h, i + 1)
            s_tok = (i // 4) * 512 + (i % 4) * 128
            SX1.issue(sync.dma_start(out=x1_dram[s_tok:s_tok + 128, :],
                                     in_=x1o[:, (i % 2) * D:(i % 2 + 1) * D]))
        for q in range(NCH):
            if q >= 2:
                sync.wait_ge(TH.h, (q - 1) * NTT)
            LXC.dinc(sync.dma_start(
                out=xchunk[:, (q % 2) * NKT * 512:(q % 2 + 1) * NKT * 512].rearrange(
                    "p (k t) -> p k t", k=NKT),
                in_=xnCv_dram[:, :, q * 512:(q + 1) * 512]))
            for sub in range(4):
                i = q * 4 + sub
                if i >= 2:
                    sync.wait_ge(OV.h, i - 1)
                s_tok = q * 512 + sub * 128
                LX1.issue(sync.dma_start(out=x1p[:, (i % 2) * D:(i % 2 + 1) * D],
                                         in_=x1p_dram[s_tok:s_tok + 128, :]))
                if i >= 2:
                    store(i - 2)
        store(4 * NCH - 2)
        store(4 * NCH - 1)
        SX1.wait_all(sync)
    bl.sync(sec_sync)

    def sec_pe(tensor):
        for q in range(NCH):
            tensor.wait_ge(LXC.h, 16 * (q + 1))
            if q == 0:
                tensor.wait_ge(VE.h, VE_PREP)
                tensor.wait_ge(SE.h, 2 * NKT)
                tensor.wait_ge(A1L.h, 32)
            xoff = (q % 2) * NKT * 512
            for jt in range(NTT):
                r = q * NTT + jt
                if r >= 2:
                    tensor.wait_ge(SH.h, r - 1)
                ins = None
                for k in range(NKT):
                    ins = tensor.matmul(pH[:, (r % 2) * 512:(r % 2 + 1) * 512],
                                        sw1T[:, k * T + jt * 128:k * T + (jt + 1) * 128],
                                        xchunk[:, xoff + k * 512:xoff + (k + 1) * 512],
                                        start=(k == 0), stop=(k == NKT - 1))
                TH.inc(ins)
            tensor.wait_ge(SH.h, (q + 1) * NTT)
            for sub in range(4):
                i = q * 4 + sub
                if i >= 2:
                    tensor.wait_ge(OV.h, i - 1)
                for dc in range(2):
                    ins = None
                    for jt in range(NTT):
                        ins = tensor.matmul(
                            pO[:, (i % 2) * 1024 + dc * 512:(i % 2) * 1024 + (dc + 1) * 512],
                            hC[:, jt * 512 + sub * 128:jt * 512 + (sub + 1) * 128],
                            sw2T[:, jt * 1024 + dc * 512:jt * 1024 + (dc + 1) * 512],
                            start=(jt == 0), stop=(jt == NTT - 1))
                    TO.inc(ins)
    bl.tensor(sec_pe)

    def sec_hevac(scalar):
        scalar = SS(scalar, SH)
        for q in range(NCH):
            for jt in range(NTT):
                r = q * NTT + jt
                scalar.wait_ge(TH.h, r + 1)
                if q >= 1 and jt == 0:
                    scalar.wait_ge(TO.h, q * 8)
                scalar.activation(out=hC[:, jt * 512:(jt + 1) * 512],
                                  in_=pH[:, (r % 2) * 512:(r % 2 + 1) * 512],
                                  func=AF.Gelu, scale=a1col[:, jt:jt + 1],
                                  bias=b1_col[:, jt:jt + 1])
    bl.scalar(sec_hevac)



    def sec_oevac(vector):
        ch = Chain(vector)
        OVm = ctx.S("d_ovm")
        for q in range(NCH):
            for sub in range(4):
                i = q * 4 + sub
                ch.wait_ge(TO.h, 2 * i + 2)
                LX1.wait_done(ch, i)
                SX1.wait_done(ch, i - 2)
                ch.do(OVm, lambda v, i=i: v.tensor_mul(
                    out=x1o[:, (i % 2) * D:(i % 2 + 1) * D],
                    in0=pO[:, (i % 2) * 1024:(i % 2 + 1) * 1024],
                    in1=a2bc[:]))
                ch.do(OV, lambda v, i=i: v.tensor_add(
                    out=x1o[:, (i % 2) * D:(i % 2 + 1) * D],
                    in0=x1o[:, (i % 2) * D:(i % 2 + 1) * D],
                    in1=x1p[:, (i % 2) * D:(i % 2 + 1) * D]))
    bl.vector(sec_oevac)




def emit_rotor(ctx, x1_dram, x2p_dram, rC_dram, es):
    bl = ctx.block
    xn1C = ctx.t["xn1C"]
    es_t = contextlib.ExitStack()
    Wvo = ctx.sb("Wvo", [128, NKT * 1024], BF16, es)
    v_sb = ctx.sb("f_v", [128, 2 * T], BF16, es_t)
    sint = ctx.sb("f_sin", [128, T], BF16, es_t)
    cost = ctx.sb("f_cos", [128, T], BF16, es_t)
    tmp = ctx.sb("f_tmp", [128, T], BF16, es_t)
    Pb = ctx.sb("f_P", [128, T], BF16, es_t)
    Qb = ctx.sb("f_Q", [128, T], BF16, es_t)
    Pc = ctx.sb("f_Pc", [128, T], BF16, es_t)
    Qc = ctx.sb("f_Qc", [128, T], BF16, es_t)
    ones_r = ctx.sb("f_ones", [128, T], BF16, es_t)
    gtab = ctx.sb("f_g", [128, 32], F32, es_t)
    hpi_col = ctx.sb("f_hpi", [128, 1], F32, es_t)
    thB = ctx.sb("f_thB", [128, NKT * 128], F32, es_t)
    uuB = ctx.sb("f_uuB", [128, NKT * 128], F32, es_t)
    sinB = ctx.sb("f_sinB", [128, NKT * 128], BF16, es_t)
    cosB16 = ctx.sb("f_cosB", [128, NKT * 128], BF16, es_t)
    cosB = ctx.sb("f_cosBW", [128, NKT * 128], F32, es_t)
    thA = ctx.sb("f_thA", [128, NKT * 32], F32, es_t)
    sinA = ctx.sb("f_sinA", [128, NKT * 32], BF16, es_t)
    cosA16 = ctx.sb("f_cosA", [128, NKT * 32], BF16, es_t)
    cosA = ctx.sb("f_cosAW", [128, NKT * 32], F32, es_t)
    pV = ctx.ps("f_pV", [128, 2 * 512], F32, es_t)
    iota128 = ctx.t["iota128_sb"]
    iota32 = ctx.t["iota32_sb"]
    fr_col = ctx.t["fr_col_sb"]
    dWvT, dWoT = ctx.t["d_WvT"], ctx.t["d_WoT"]

    LWV = ctx.S("f_lwv"); LWO2 = ctx.S("f_lwo")
    TE = ctx.S("f_te"); SEs = ctx.S("f_se"); VEs = ctx.S("f_ve")
    GEs = ctx.S("f_ge"); SRC = ctx.S("f_src")
    ldc_target = ctx.S("ldc").n

    def bc3(ap2d, mid, inner):
        if ap2d.shape[1] == inner:
            return ap2d.rearrange("p (o u) -> p o u", o=1).to_broadcast(
                [128, mid, inner])
        return ap2d.rearrange("p (b o) -> p b o", o=1).to_broadcast(
            [128, mid, inner])

    def sec_wload_v(gpsimd):
        for k in range(NKT):
            LWV.dinc(gpsimd.dma_start(out=Wvo[:, k * 1024:(k + 1) * 1024],
                                      in_=dWvT[k * 128:(k + 1) * 128, :]))
    bl.gpsimd(sec_wload_v)

    def sec_g(vector):
        vector = SS(vector, VEs)
        vector.wait_ge(ctx.S("ldc").h, ldc_target)
        g, kg = gtab[:, 0:8], gtab[:, 24:32]
        vector.tensor_scalar(out=g, in0=fr_col[:], scalar1=128.0, scalar2=None,
                             op0=ALU.mult)
        vector.tensor_scalar(out=kg, in0=g, scalar1=INV2PI, scalar2=MAGIC,
                             op0=ALU.mult, op1=ALU.add)
        vector.tensor_scalar(out=kg, in0=kg, scalar1=MAGIC, scalar2=None,
                             op0=ALU.subtract)
        vector.scalar_tensor_tensor(out=g, in0=kg, scalar=-C1, in1=g,
                                    op0=ALU.mult, op1=ALU.add)
        vector.scalar_tensor_tensor(out=g, in0=kg, scalar=-C2, in1=g,
                                    op0=ALU.mult, op1=ALU.add)
        vector.memset(ones_r[:], 1.0)
        vector.memset(hpi_col[:], HALF_PI)
        thBv = thB[:].rearrange("p (k u) -> p k u", k=NKT)
        vector.tensor_mul(out=thBv, in0=bc3(fr_col[:], NKT, 128),
                          in1=bc3(iota128[:], NKT, 128))
        vector.tensor_scalar(out=uuB[:], in0=thB[:], scalar1=INV2PI,
                             scalar2=MAGIC, op0=ALU.mult, op1=ALU.add)
        vector.tensor_scalar(out=uuB[:], in0=uuB[:], scalar1=MAGIC,
                             scalar2=None, op0=ALU.subtract)
        vector.scalar_tensor_tensor(out=thB[:], in0=uuB[:], scalar=-TWOPI_F,
                                    in1=thB[:], op0=ALU.mult, op1=ALU.add)
        vector.tensor_scalar(out=cosB[:], in0=thB[:], scalar1=-PI_F,
                             scalar2=PI_F, op0=ALU.max, op1=ALU.min)
        thAv = thA[:].rearrange("p (k b) -> p k b", k=NKT)
        vector.tensor_mul(out=thAv, in0=bc3(gtab[:, 0:8], NKT, 32),
                          in1=bc3(iota32[:, 0:32], NKT, 32))
        vector.tensor_scalar(out=uuB[:, 0:NKT * 32], in0=thA[:], scalar1=INV2PI,
                             scalar2=MAGIC, op0=ALU.mult, op1=ALU.add)
        vector.tensor_scalar(out=uuB[:, 0:NKT * 32], in0=uuB[:, 0:NKT * 32],
                             scalar1=MAGIC, scalar2=None, op0=ALU.subtract)
        vector.scalar_tensor_tensor(out=thA[:], in0=uuB[:, 0:NKT * 32],
                                    scalar=-TWOPI_F,
                                    in1=thA[:], op0=ALU.mult, op1=ALU.add)
        vector.tensor_scalar(out=cosA[:], in0=thA[:], scalar1=-PI_F,
                             scalar2=PI_F, op0=ALU.max, op1=ALU.min)
    bl.vector(sec_g)
    ve_gbuild = VEs.n

    def sec_se_tab(scalar):
        scalar = SS(scalar, SEs)
        scalar.wait_ge(VEs.h, ve_gbuild)
        scalar.activation(out=sinB[:], in_=cosB[:], func=AF.Sin)
        scalar.activation(out=cosB[:], in_=cosB[:], func=AF.Abs)
        scalar.activation(out=cosB16[:], in_=cosB[:], func=AF.Sin,
                          scale=-1.0, bias=hpi_col[:, 0:1])
        scalar.activation(out=sinA[:], in_=cosA[:], func=AF.Sin)
        scalar.activation(out=cosA[:], in_=cosA[:], func=AF.Abs)
        scalar.activation(out=cosA16[:], in_=cosA[:], func=AF.Sin,
                          scale=-1.0, bias=hpi_col[:, 0:1])
    bl.scalar(sec_se_tab)
    se_base = SEs.n
    te_base = TE.n

    def sec_vmm(tensor):
        tensor.wait_ge(LWV.h, 16 * NKT)
        for k in range(NKT):
            for q in range(NCH):
                m = k * NCH + q
                if m >= 2:
                    kp2, qp2 = divmod(m - 2, NCH)
                    tensor.wait_ge(SEs.h, se_base + 8 * kp2 + qp2 + 1)
                ins = None
                for dt_ in range(NKT):
                    ins = tensor.matmul(
                        pV[:, (m % 2) * 512:(m % 2 + 1) * 512],
                        Wvo[:, dt_ * 1024 + k * 128:dt_ * 1024 + (k + 1) * 128],
                        xn1C[:, dt_ * T + q * 512:dt_ * T + (q + 1) * 512],
                        start=(dt_ == 0), stop=(dt_ == NKT - 1))
                TE.inc(ins)
    bl.tensor(sec_vmm)

    cnt = {nm: {} for nm in ("recon", "P", "Q", "S1", "S2", "m1", "m2", "r")}
    for kc in range(NKT + 1):
        k = kc
        kp = kc - 1

        if kp >= 0:
            def sec_ve_scan(vector, k=kp):
                vector = SS(vector, VEs)
                if k >= 1:
                    vector.wait_ge(SRC.h, 16 * k)
                vector.tensor_tensor_scan(out=Pc[:], data0=ones_r[:], data1=Pb[:],
                                          initial=0.0, op0=ALU.mult, op1=ALU.add)
                cnt["S1"][k] = VEs.n
                vector.tensor_tensor_scan(out=Qc[:], data0=ones_r[:], data1=Qb[:],
                                          initial=0.0, op0=ALU.mult, op1=ALU.add)
                cnt["S2"][k] = VEs.n
                vector.tensor_mul(out=Pb[:], in0=cost[:], in1=Pc[:])
                cnt["m1"][k] = VEs.n
                vector.tensor_mul(out=Qb[:], in0=sint[:], in1=Qc[:])
                cnt["m2"][k] = VEs.n
                vector.tensor_add(out=Pc[:], in0=Pb[:], in1=Qb[:])
                cnt["r"][k] = VEs.n
            bl.vector(sec_ve_scan)

            def sec_store(sync, k=kp):
                sync.wait_ge(VEs.h, cnt["r"][k])
                SRC.dinc(sync.dma_start(out=rC_dram[:, k * T:(k + 1) * T],
                                        in_=Pc[:]))
            bl.sync(sec_store)

        if k < NKT:
            def sec_ve_recon(vector, k=k):
                vector = SS(vector, VEs)
                vector.wait_ge(SEs.h, 6)
                As = bc3(sinA[:, k * 32:(k + 1) * 32], 32, 128)
                Ac = bc3(cosA16[:, k * 32:(k + 1) * 32], 32, 128)
                Bs = bc3(sinB[:, k * 128:(k + 1) * 128], 32, 128)
                Bc = bc3(cosB16[:, k * 128:(k + 1) * 128], 32, 128)
                sv = sint[:].rearrange("p (b u) -> p b u", b=32)
                cv = cost[:].rearrange("p (b u) -> p b u", b=32)
                tv = tmp[:].rearrange("p (b u) -> p b u", b=32)
                vector.tensor_mul(out=sv, in0=As, in1=Bc)
                vector.tensor_mul(out=tv, in0=Ac, in1=Bs)
                vector.tensor_add(out=sint[:], in0=sint[:], in1=tmp[:])
                vector.tensor_mul(out=cv, in0=Ac, in1=Bc)
                vector.tensor_mul(out=tv, in0=As, in1=Bs)
                vector.tensor_sub(out=cost[:], in0=cost[:], in1=tmp[:])
                cnt["recon"][k] = VEs.n
            bl.vector(sec_ve_recon)

        if k < NKT:
            def sec_se_ev(scalar, k=k):
                scalar = SS(scalar, SEs)
                if k >= 2:
                    scalar.wait_ge(VEs.h, cnt["Q"][k - 2])
                vo = (k % 2) * T
                for q in range(NCH):
                    m = k * NCH + q
                    scalar.wait_ge(TE.h, te_base + m + 1)
                    scalar.copy(out=v_sb[:, vo + q * 512:vo + (q + 1) * 512],
                                in_=pV[:, (m % 2) * 512:(m % 2 + 1) * 512])
                assert SEs.n == se_base + 8 * (k + 1), (SEs.n, se_base, k)
            bl.scalar(sec_se_ev)

            def sec_ve_p(vector, k=k):
                vector = SS(vector, VEs)
                vector.wait_ge(SEs.h, se_base + 8 * (k + 1))
                vector.tensor_mul(out=Pb[:], in0=v_sb[:, (k % 2) * T:(k % 2 + 1) * T],
                                  in1=cost[:])
                cnt["P"][k] = VEs.n
            bl.vector(sec_ve_p)

            def sec_ve_q(vector, k=k):
                vector = SS(vector, VEs)
                vector.tensor_mul(out=Qb[:], in0=v_sb[:, (k % 2) * T:(k % 2 + 1) * T],
                                  in1=sint[:])
                cnt["Q"][k] = VEs.n
            bl.vector(sec_ve_q)

    te_vmm_all = TE.n

    def sec_wload_o(gpsimd):
        gpsimd.wait_ge(TE.h, te_vmm_all)
        for k in range(NKT):
            LWO2.dinc(gpsimd.dma_start(out=Wvo[:, k * 1024:(k + 1) * 1024],
                                       in_=dWoT[k * 128:(k + 1) * 128, :]))
    bl.gpsimd(sec_wload_o)
    lw_o = ctx.S("f_lwo").n

    es_t.close()
    rbT = ctx.sb("f_rbT", [128, 2 * 1024], BF16, es)
    x1r = ctx.sb("f_x1r", [128, 2 * D], F32, es)
    x2p = ctx.sb("f_x2p", [128, 2 * D], F32, es)
    pO = ctx.ps("f_pO", [128, 2 * 1024], F32, es)
    TO = ctx.S("f_to"); OV = ctx.S("f_ov")
    LX1 = DStream(ctx, "f_lx1"); SX2 = DStream(ctx, "f_sx2")
    LRB = DStream(ctx, "f_lrb")
    ctx.store_streams.append(SX2)
    rCv_dram = rC_dram.rearrange("p (k t) -> p k t", k=NKT)

    def sec_sync2(sync):
        sync.wait_ge(SRC.h, 16 * NKT)

        def store(bp):
            sync.wait_ge(OV.h, bp + 1)
            SX2.issue(sync.dma_start(out=x2p_dram[bp * 128:(bp + 1) * 128, :],
                                     in_=x2p[:, (bp % 2) * D:(bp % 2 + 1) * D]))
        for b in range(NTT):
            if b >= 2:
                sync.wait_ge(TO.h, 2 * (b - 1))
            LRB.issue(sync.dma_start(
                out=rbT[:, (b % 2) * 1024:(b % 2 + 1) * 1024].rearrange(
                    "p (k t) -> p k t", k=NKT),
                in_=rCv_dram[:, :, b * 128:(b + 1) * 128]))
            if b >= 2:
                sync.wait_ge(OV.h, b - 1)
            LX1.issue(sync.dma_start(out=x1r[:, (b % 2) * D:(b % 2 + 1) * D],
                                     in_=x1_dram[b * 128:(b + 1) * 128, :]))
            if b >= 2:
                store(b - 2)
        store(NTT - 2)
        store(NTT - 1)
        SX2.wait_all(sync)
    bl.sync(sec_sync2)

    def sec_omm(tensor):
        tensor.wait_ge(LWO2.h, lw_o)
        for b in range(NTT):
            LRB.wait_done(tensor, b)
            if b >= 2:
                tensor.wait_ge(OV.h, b - 1)
            for q2 in range(2):
                ins = None
                for k in range(NKT):
                    ins = tensor.matmul(
                        pO[:, (b % 2) * 1024 + q2 * 512:(b % 2) * 1024 + (q2 + 1) * 512],
                        rbT[:, (b % 2) * 1024 + k * 128:(b % 2) * 1024 + (k + 1) * 128],
                        Wvo[:, k * 1024 + q2 * 512:k * 1024 + (q2 + 1) * 512],
                        start=(k == 0), stop=(k == NKT - 1))
                TO.inc(ins)
    bl.tensor(sec_omm)

    def sec_oev(vector):
        vector = SS(vector, OV)
        for b in range(NTT):
            vector.wait_ge(TO.h, 2 * b + 2)
            LX1.wait_done(vector, b)
            SX2.wait_done(vector, b - 2)
            vector.tensor_add(out=x2p[:, (b % 2) * D:(b % 2 + 1) * D],
                              in0=pO[:, (b % 2) * 1024:(b % 2 + 1) * 1024],
                              in1=x1r[:, (b % 2) * D:(b % 2 + 1) * D])
    bl.vector(sec_oev)




def emit_ffn1(ctx, xn1C_dram, x2p_dram, out_dram, es):
    bl = ctx.block
    if "WinTb" in ctx.t:
        WinTb, WoutTb = ctx.t["WinTb"], ctx.t["WoutTb"]
        preloaded = True
    else:
        WinTb = ctx.sb("WinTb", [128, NKT * JW], BF16, es)
        WoutTb = ctx.sb("WoutTb", [128, NJT * 1024], BF16, es)
        preloaded = False
    acts = ctx.sb("g_acts", [128, NJT * 512], BF16, es)
    dec = ctx.sb("g_dec", [128, 11 * 512], BF16, es)
    msk2 = ctx.sb("g_msk", [128, 2 * NJT * 512], BF16, es)
    u0 = ctx.sb("g_u0", [128, 11 * 512], BF16, es)
    u1 = ctx.sb("g_u1", [128, 11 * 512], BF16, es)
    xchunk = ctx.sb("g_xc", [128, 2 * NKT * 512], BF16, es)
    x2p = ctx.sb("g_x2p", [128, 2 * D], F32, es)
    xout = ctx.sb("g_out", [128, D], F32, es)
    pL = ctx.ps("g_pL", [128, 2 * 512], F32, es)
    pS = ctx.ps("g_pS", [128, 2 * 512], F32, es)
    pO = ctx.ps("g_pO", [128, 2 * 1024], F32, es)
    bin_col = ctx.t["bin_col_sb"]
    S0sb, S1sb = ctx.t["S0_sb"], ctx.t["S1_sb"]
    dWinT, dWoutT = ctx.t["d_WinT"], ctx.t["d_WoutT"]
    xn1Cv_dram = xn1C_dram.rearrange("p (k t) -> p k t", k=NKT)
    ldc_target = ctx.S("ldc").n

    LWI = ctx.S("g_lwi"); LWO = ctx.S("g_lwo")
    TE = ctx.S("g_te"); SE = ctx.S("g_se"); VE = ctx.S("g_ve")
    LXC = DStream(ctx, "g_lxc"); LX2 = DStream(ctx, "g_lx2")
    SOUT = DStream(ctx, "g_sout")
    OV = ctx.S("g_ov")

    if not preloaded:
        def sec_wload(gpsimd):
            for k in range(NKT):
                LWI.dinc(gpsimd.dma_start(out=WinTb[:, k * JW:(k + 1) * JW],
                                          in_=dWinT[k * 128:(k + 1) * 128, :]))
            for jt in range(NJT):
                LWO.dinc(gpsimd.dma_start(
                    out=WoutTb[:, jt * 1024:(jt + 1) * 1024],
                    in_=dWoutT[jt * 128:(jt + 1) * 128, :]))
        bl.gpsimd(sec_wload)

    def sec_xc0(sync):
        sync.wait_ge(ctx.S("ldc").h, ldc_target)
        sync.wait_ge(ctx.S("spill").h, ctx.S("spill").n)
        LXC.issue(sync.dma_start(
            out=xchunk[:, 0:NKT * 512].rearrange("p (k t) -> p k t", k=NKT),
            in_=xn1Cv_dram[:, :, 0:512]))
        LXC.issue(sync.dma_start(
            out=xchunk[:, NKT * 512:].rearrange("p (k t) -> p k t", k=NKT),
            in_=xn1Cv_dram[:, :, 512:1024]))
    bl.sync(sec_xc0)

    silu_end = {}; dec_end = {}; tree_end = {}
    tev_cnt = {}; uu_cnt = {}; tmm_cnt = {}; chain_to = {}

    def emit_omm_chain(q_out, c, part=None):
        sub, dc = c // 2, c % 2
        i2 = q_out * 4 + sub
        jts = range(NJT) if part is None else (
            range(0, 10) if part == 0 else range(10, NJT))

        def sec_ommc(tensor, q_out=q_out, sub=sub, dc=dc, i2=i2, jts=jts,
                     part=part):
            if part in (None, 0):
                if q_out == 0 and sub == 0 and dc == 0:
                    tensor.wait_ge(LWO.h, 16 * NJT)
                tensor.wait_ge(VE.h, tree_end[q_out])
                if i2 >= 2 and dc == 0:
                    tensor.wait_ge(OV.h, i2 - 1)
            moff = (q_out % 2) * NJT * 512
            ins = None
            for jt in jts:
                ins = tensor.matmul(
                    pO[:, (i2 % 2) * 1024 + dc * 512:(i2 % 2) * 1024 + (dc + 1) * 512],
                    msk2[:, moff + jt * 512 + sub * 128:moff + jt * 512 + (sub + 1) * 128],
                    WoutTb[:, jt * 1024 + dc * 512:jt * 1024 + (dc + 1) * 512],
                    start=(jt == 0), stop=(jt == NJT - 1))
            if part in (None, 1):
                TO = ctx.S("g_to")
                TO.inc(ins)
                chain_to[(q_out, sub, dc)] = TO.n
        bl.tensor(sec_ommc)

    TO = ctx.S("g_to")

    for q in range(NCH):
        te_l0 = TE.n
        ve0 = ctx.S("g_ve").n
        se0 = ctx.S("g_se").n
        xoff = (q % 2) * NKT * 512

        def sec_lmm(tensor, q=q, xoff=xoff, ve0=ve0, se0=se0):
            LXC.wait_done(tensor, q)
            if q == 0:
                tensor.wait_ge(LWI.h, 16 * NKT)
            for jt in range(NJT):
                if jt < 2 and q >= 1:
                    tensor.wait_ge(SE.h, silu_end[q - 1])
                    tensor.wait_ge(VE.h, dec_end[q - 1])
                if jt >= 2:
                    tensor.wait_ge(SE.h, se0 + jt - 1)
                    if jt - 2 < 11:
                        tensor.wait_ge(VE.h, ve0 + jt - 1)
                ins = None
                for k in range(NKT):
                    ins = tensor.matmul(pL[:, (jt % 2) * 512:(jt % 2 + 1) * 512],
                                        WinTb[:, k * JW + jt * 128:k * JW + (jt + 1) * 128],
                                        xchunk[:, xoff + k * 512:xoff + (k + 1) * 512],
                                        start=(k == 0), stop=(k == NKT - 1))
                TE.inc(ins)
        bl.tensor(sec_lmm)
        te_lmm = TE.n

        def sec_silu(scalar, q=q, te_l0=te_l0):
            scalar = SS(scalar, SE)
            if q >= 1:
                scalar.wait_ge(VE.h, tree_end[q - 1])
            for jt in range(NJT):
                scalar.wait_ge(TE.h, te_l0 + jt + 1)
                scalar.activation(out=acts[:, jt * 512:(jt + 1) * 512],
                                  in_=pL[:, (jt % 2) * 512:(jt % 2 + 1) * 512],
                                  func=AF.Silu, bias=bin_col[:, jt:jt + 1])
        bl.scalar(sec_silu)
        silu_end[q] = ctx.S("g_se").n

        def sec_dec(vector, q=q, te_l0=te_l0, se0=se0):
            vector = SS(vector, VE)
            for jt in range(11):
                vector.wait_ge(TE.h, te_l0 + jt + 1)
                vector.wait_ge(SE.h, se0 + jt + 1)
                vector.tensor_scalar(out=dec[:, jt * 512:(jt + 1) * 512],
                                     in0=pL[:, (jt % 2) * 512:(jt % 2 + 1) * 512],
                                     scalar1=bin_col[:, jt:jt + 1], scalar2=0.0,
                                     op0=ALU.add, op1=ALU.is_gt)
            vector.tensor_copy(out=u1[:, 0:512], in_=dec[:, 0:512])
            vector.tensor_scalar(out=u0[:, 0:512], in0=dec[:, 0:512],
                                 scalar1=-1.0, scalar2=1.0,
                                 op0=ALU.mult, op1=ALU.add)
            uu_cnt[(q, 0)] = ctx.S("g_ve").n
            if q >= 2:
                vector.wait_ge(TO.h, chain_to[(q - 2, 3, 1)])
            vector.tensor_copy(out=msk2[:, (q % 2) * NJT * 512:(q % 2) * NJT * 512 + 512],
                               in_=acts[:, 0:512])
        bl.vector(sec_dec)
        dec_end[q] = ctx.S("g_ve").n

        if q + 2 < NCH:
            def sec_xc(sync, q=q, te_lmm=te_lmm):
                sync.wait_ge(TE.h, te_lmm)
                LXC.issue(sync.dma_start(
                    out=xchunk[:, (q % 2) * NKT * 512:(q % 2 + 1) * NKT * 512].rearrange(
                        "p (k t) -> p k t", k=NKT),
                    in_=xn1Cv_dram[:, :, (q + 2) * 512:(q + 3) * 512]))
            bl.sync(sec_xc)

        if q >= 1:
            def sec_x2pload(sync, q=q):
                for sub in range(4):
                    i2 = (q - 1) * 4 + sub
                    if i2 >= 2:
                        sync.wait_ge(OV.h, i2 - 1)
                    s_tok = (q - 1) * 512 + sub * 128
                    LX2.issue(sync.dma_start(
                        out=x2p[:, (i2 % 2) * D:(i2 % 2 + 1) * D],
                        in_=x2p_dram[s_tok:s_tok + 128, :]))
            bl.sync(sec_x2pload)

        for r, (ct, pt) in enumerate(TREE_SMETA):
            def sec_tmm(tensor, q=q, r=r, ct=ct, pt=pt):
                tensor.wait_ge(VE.h, uu_cnt[(q, pt)])
                if r >= 2:
                    tensor.wait_ge(VE.h, tev_cnt[(q, r - 2)])
                elif q >= 1:
                    tensor.wait_ge(VE.h, tev_cnt[(q - 1, 16 + r)])
                tensor.matmul(pS[:, (r % 2) * 512:(r % 2 + 1) * 512],
                              S0sb[:, r * 128:(r + 1) * 128],
                              u0[:, pt * 512:(pt + 1) * 512],
                              start=True, stop=False)
                TE.inc(tensor.matmul(pS[:, (r % 2) * 512:(r % 2 + 1) * 512],
                                     S1sb[:, r * 128:(r + 1) * 128],
                                     u1[:, pt * 512:(pt + 1) * 512],
                                     start=False, stop=True))
                tmm_cnt[(q, r)] = ctx.S("g_te").n
            bl.tensor(sec_tmm)

            if q >= 1 and 1 <= r <= 16:
                emit_omm_chain(q - 1, (r - 1) // 2, part=(r - 1) % 2)

            if q >= 1 and r % 4 == 0 and r >= 4:
                subo = r // 4 - 1
                i2o = (q - 1) * 4 + subo

                def sec_oev1(vector, q=q, subo=subo, i2o=i2o):
                    vector = SS(vector, OV)
                    vector.wait_ge(TO.h, chain_to[(q - 1, subo, 1)])
                    LX2.wait_done(vector, i2o)
                    SOUT.wait_done(vector, i2o - 1)
                    vector.tensor_add(out=xout[:, 0:D],
                                      in0=pO[:, (i2o % 2) * 1024:(i2o % 2 + 1) * 1024],
                                      in1=x2p[:, (i2o % 2) * D:(i2o % 2 + 1) * D])
                bl.vector(sec_oev1)

            def sec_tev(vector, q=q, r=r, ct=ct, se0=se0):
                vector = SS(vector, VE)
                vector.wait_ge(TE.h, tmm_cnt[(q, r)])
                ps = pS[:, (r % 2) * 512:(r % 2 + 1) * 512]
                moff = (q % 2) * NJT * 512
                if ct < 11:
                    vector.tensor_mul(out=u1[:, ct * 512:(ct + 1) * 512],
                                      in0=ps, in1=dec[:, ct * 512:(ct + 1) * 512])
                    vector.tensor_sub(out=u0[:, ct * 512:(ct + 1) * 512],
                                      in0=ps, in1=u1[:, ct * 512:(ct + 1) * 512])
                    uu_cnt[(q, ct)] = ctx.S("g_ve").n
                vector.wait_ge(SE.h, se0 + ct + 1)
                vector.tensor_mul(out=msk2[:, moff + ct * 512:moff + (ct + 1) * 512],
                                  in0=acts[:, ct * 512:(ct + 1) * 512], in1=ps)
                tev_cnt[(q, r)] = ctx.S("g_ve").n
            bl.vector(sec_tev)
        tree_end[q] = ctx.S("g_ve").n

        if q >= 1:
            def sec_ostore(sync, q=q):
                for sub in range(4):
                    i2 = (q - 1) * 4 + sub
                    sync.wait_ge(OV.h, i2 + 1)
                    s_tok = (q - 1) * 512 + sub * 128
                    SOUT.issue(sync.dma_start(
                        out=out_dram[s_tok:s_tok + 128, :],
                        in_=xout[:, 0:D]))
            bl.gpsimd(sec_ostore)

    qt = NCH - 1

    def sec_x2pload_t(sync):
        for sub in range(4):
            i2 = qt * 4 + sub
            sync.wait_ge(OV.h, i2 - 1)
            s_tok = qt * 512 + sub * 128
            LX2.issue(sync.dma_start(out=x2p[:, (i2 % 2) * D:(i2 % 2 + 1) * D],
                                     in_=x2p_dram[s_tok:s_tok + 128, :]))
    bl.sync(sec_x2pload_t)
    for c in range(8):
        emit_omm_chain(qt, c)

    def sec_oev_t(vector):
        vector = SS(vector, OV)
        for sub in range(4):
            i2 = qt * 4 + sub
            vector.wait_ge(TO.h, chain_to[(qt, sub, 1)])
            LX2.wait_done(vector, i2)
            SOUT.wait_done(vector, i2 - 1)
            vector.tensor_add(out=xout[:, 0:D],
                              in0=pO[:, (i2 % 2) * 1024:(i2 % 2 + 1) * 1024],
                              in1=x2p[:, (i2 % 2) * D:(i2 % 2 + 1) * D])
    bl.vector(sec_oev_t)

    def sec_ostore_t(sync):
        for sub in range(4):
            i2 = qt * 4 + sub
            sync.wait_ge(OV.h, i2 + 1)
            s_tok = qt * 512 + sub * 128
            SOUT.issue(sync.dma_start(out=out_dram[s_tok:s_tok + 128, :],
                                      in_=xout[:, 0:D]))
    bl.gpsimd(sec_ostore_t)

    def sec_final(gpsimd):
        SOUT.wait_all(gpsimd)
    bl.gpsimd(sec_final)



PHASE_INPUTS = {
    "A": [], "B": ["Wn"], "C": [], "D": ["w1T", "w2T"],
    "E": [], "F": ["WvT", "WoT"], "G": ["WinT", "WoutT"],
}
TOP_CONSTS = {
    "A": ["ident", "eps_col"], "B": ["ident", "b_col"], "C": ["ident"],
    "D": ["b1_col"], "E": ["ident", "eps_col"],
    "F": ["fr_col"], "G": [],
}
HBM_T = {
    "xnC_h": ([128, NKT * T], BF16, "A", ["B", "D"]),
    "x1p_h": ([T, D], F32, "C", ["D"]),
    "x1_h": ([T, D], F32, "D", ["E", "F"]),
    "xn1C_h": ([128, NKT * T], BF16, "E", ["F", "G"]),
    "x2p_h": ([T, D], F32, "F", ["G"]),
}
PHASE_ORDER = ["A", "B", "C", "D", "E", "F", "G"]


def build_group(phases):
    nc = bass.Bass("TRN2", target_bir_lowering=False)
    in_names, out_names = [], []
    consts = set()
    dins = set()
    for ph in phases:
        consts.update(TOP_CONSTS[ph])
        dins.update(PHASE_INPUTS[ph])
    if "C" in phases:
        dins.update(["band", "b2_row"])
    if "F" in phases:
        dins.update(["iota128_row", "iota32_row"])
    if "G" in phases:
        dins.update(["bin_col", "S0all", "S1all"])
    if "C" in phases:
        assert "B" in phases
    if "F" in phases and "E" not in phases:
        pass

    es = contextlib.ExitStack()
    with es:
        dram = {}
        for nm in sorted(dins | consts):
            shape, dt = CONST_SPECS[nm]
            dram["d_" + nm] = nc.dram_tensor(nm, shape, dt, kind="ExternalInput").ap()
            in_names.append(nm)
        if "A" in phases or "C" in phases:
            dram["d_x"] = nc.dram_tensor("x", [T, D], F32, kind="ExternalInput").ap()
            in_names.append("x")
        pset = set(phases)
        for nm, (shape, dt, prod, cons) in HBM_T.items():
            produced = prod in pset
            consumed_here = bool(pset & set(cons))
            consumed_later = bool((set(cons) - pset))
            if not produced and not consumed_here:
                continue
            if produced and consumed_later:
                kind = "ExternalOutput"
                out_names.append(nm)
            elif produced:
                kind = "Internal"
            else:
                kind = "ExternalInput"
                in_names.append(nm)
            dram["d_" + nm] = nc.dram_tensor(nm, shape, dt, kind=kind).ap()
        if "D" in phases:
            dram["d_a1s"] = nc.dram_tensor("a1s", [T], F32,
                                           kind="Internal").ap()
        if "F" in phases:
            dram["d_rC_h"] = nc.dram_tensor("rC_h", [128, NKT * T], BF16,
                                            kind="Internal").ap()
        if "G" in phases:
            dram["d_out"] = nc.dram_tensor("out", [T, D], F32,
                                           kind="ExternalOutput").ap()
            out_names.append("out")

        block = es.enter_context(nc.Block())
        ctx = Ctx(nc, es, block)
        ctx.t.update(dram)
        emit_preload(ctx, sorted(consts))

        if "A" in phases:
            with contextlib.ExitStack() as sA:
                emit_ln(ctx, "a", ctx.t["d_x"], "xnC", ctx.t["d_xnC_h"], sA,
                        dst_es=sA)
            barrier(ctx)
        with contextlib.ExitStack() as s_bc:
            if "B" in phases:
                with contextlib.ExitStack() as sB:
                    emit_fproj(ctx, ctx.t["d_xnC_h"], sB, dst_es=s_bc)
                barrier(ctx)
            if "C" in phases:
                with contextlib.ExitStack() as sC:
                    emit_preload(ctx, ["band", "b2_row"], es=sC)
                    emit_attn0(ctx, ctx.t["d_x"], ctx.t["d_x1p_h"], sC)
                barrier(ctx)
        if "D" in phases:
            with contextlib.ExitStack() as sD:
                emit_ffn0(ctx, ctx.t["d_xnC_h"], ctx.t["d_x1p_h"],
                          ctx.t["d_x1_h"], sD)
            barrier(ctx)
        with contextlib.ExitStack() as s_ef:
            if "E" in phases:
                with contextlib.ExitStack() as sE:
                    emit_ln(ctx, "e", ctx.t["d_x1_h"], "xn1C",
                            ctx.t["d_xn1C_h"], sE, dst_es=s_ef)
                barrier(ctx)
            if "F" in phases:
                with contextlib.ExitStack() as sF:
                    if "xn1C" not in ctx.t:
                        t = ctx.sb("xn1C", [128, NKT * T], BF16, s_ef)
                        LD = ctx.S("ld_xn1c")

                        def sec(sync):
                            LD.dinc(sync.dma_start(out=t[:], in_=ctx.t["d_xn1C_h"]))
                        block.sync(sec)
                        tgt = LD.n

                        def secw(eng):
                            eng.wait_ge(LD.h, tgt)
                        block.tensor(secw)
                    emit_preload(ctx, ["iota128_row", "iota32_row"], es=sF)
                    emit_rotor(ctx, ctx.t["d_x1_h"], ctx.t["d_x2p_h"],
                               ctx.t["d_rC_h"], sF)
                barrier(ctx)
        if "G" in phases:
            with contextlib.ExitStack() as sG:
                emit_preload(ctx, ["bin_col", "S0all", "S1all"], es=sG)
                emit_ffn1(ctx, ctx.t["d_xn1C_h"], ctx.t["d_x2p_h"],
                          ctx.t["d_out"], sG)
            barrier(ctx)
        if "G" not in phases:
            def sec_fin(gpsimd):
                if "spill" in ctx.sem:
                    gpsimd.wait_ge(ctx.sem["spill"].h, ctx.sem["spill"].n)
                for st_ in ctx.store_streams:
                    st_.wait_all(gpsimd)
            block.gpsimd(sec_fin)
    return nc, in_names, out_names


GROUPS_UNFUSED = [["A", "B", "C"], ["D"], ["E"], ["F"], ["G"]]
GROUPS_FUSED = [["A", "B", "C", "D", "E", "F", "G"]]
GROUPS = GROUPS_FUSED

_BUILD_CACHE = {}


def kernel(**inputs):
    common, per_core = host_prep(inputs)
    carry = [dict(pc) for pc in per_core]
    result = None
    for phases in GROUPS:
        key = tuple(phases)
        if key not in _BUILD_CACHE:
            _BUILD_CACHE[key] = build_group(phases)
        nc, in_names, out_names = _BUILD_CACHE[key]
        in_maps = []
        for ci in range(8):
            m = {}
            for nm in in_names:
                m[nm] = common[nm] if nm in common else carry[ci][nm]
            in_maps.append(m)
        res = run_bass_kernel_spmd(nc, in_maps, core_ids=list(range(8)))
        for ci in range(8):
            for nm in out_names:
                carry[ci][nm] = np.asarray(res.results[ci][nm])
        if "G" in phases:
            result = np.stack([carry[ci]["out"].astype(np.float32)
                               for ci in range(8)], axis=0)
    return result


# version 28
# speedup vs baseline: 1.0186x; 1.0008x over previous
import contextlib
import math

import numpy as np
import ml_dtypes

import concourse.bass as bass
import concourse.mybir as mybir
from concourse.bass_utils import run_bass_kernel_spmd

F32 = mybir.dt.float32
BF16 = mybir.dt.bfloat16
I32 = mybir.dt.int32
AF = mybir.ActivationFunctionType
ALU = mybir.AluOpType
AX = mybir.AxisListType

D = 1024
T = 4096
NKT = 8
NTT = 32
NCH = 8

TREE_DEPTH = 7
TREE_PAR = 8
N_NODES = 255
TREE_TILE_BASE = [0, 1, 2, 3, 4, 5, 7, 11]
NJT = 19
JW = NJT * 128

TWO_PI = 6.283185307179586
C1 = float(np.float32(6.28125))
C2 = float(np.float32(TWO_PI - 6.28125))
C3 = float(np.float32(TWO_PI - 6.28125 - float(np.float32(TWO_PI - 6.28125))))
INV2PI = float(np.float32(1.0 / TWO_PI))
HALF_PI = float(np.float32(math.pi / 2))
PI_F = float(np.float32(math.pi))
TWOPI_F = float(np.float32(TWO_PI))
MAGIC = 12582912.0
LN_EPS = 1e-5

BF = ml_dtypes.bfloat16



def _tree_perm():
    slot_src = np.full(JW, -1, dtype=np.int64)
    for p in range(TREE_PAR):
        for d in range(TREE_DEPTH + 1):
            base = 2 ** d - 1
            for k in range(2 ** d):
                j = p * N_NODES + base + k
                idx = p * (2 ** d) + k
                slot_src[128 * TREE_TILE_BASE[d] + idx] = j
    return slot_src


def _tree_scatter():
    smeta = []
    mats0, mats1 = [], []
    for d in range(TREE_DEPTH):
        n_par = TREE_PAR * 2 ** d
        n_child = 2 * n_par
        pt0, ct0 = TREE_TILE_BASE[d], TREE_TILE_BASE[d + 1]
        n_ct = (n_child + 127) // 128
        for ci in range(n_ct):
            S0 = np.zeros((128, 128), dtype=np.float32)
            S1 = np.zeros((128, 128), dtype=np.float32)
            for cl in range(min(128, n_child - ci * 128)):
                cg = ci * 128 + cl
                q = cg // 2
                b = cg % 2
                (S1 if b else S0)[q % 128, cl] = 1.0
            smeta.append((ct0 + ci, pt0 + (ci * 64) // 128))
            mats0.append(S0)
            mats1.append(S1)
    return smeta, (np.concatenate(mats0, 1).astype(BF),
                   np.concatenate(mats1, 1).astype(BF))


TREE_SMETA, (_S0ALL, _S1ALL) = _tree_scatter()
N_SC = len(TREE_SMETA)


def _band_patch():
    Cm = np.zeros((128, 146), dtype=np.float32)
    for i in range(128):
        for w in range(146):
            delta = i - (w - 9)
            Cm[i, w] = -0.01 * (10.0 - min(abs(delta), 10.0))
    return Cm


def host_prep(inp):
    f32 = np.float32
    cont = np.ascontiguousarray
    c = {}
    _ = inp["L1_attn_Wk"]
    c["Wn"] = cont(inp["L0_attn_W"].astype(f32))
    c["b_col"] = cont(inp["L0_attn_b"].astype(f32).reshape(8, 128).T)
    c["w1T"] = cont(inp["L0_ffn_w1"].astype(f32).T)
    c["b1_col"] = cont(inp["L0_ffn_b1"].astype(f32).reshape(32, 128).T)
    c["w2T"] = cont(inp["L0_ffn_w2"].astype(f32).T)
    c["b2_row"] = cont(inp["L0_ffn_b2"].astype(f32).reshape(1, 1024))
    c["WvT"] = cont(inp["L1_attn_Wv"].astype(f32).T)
    c["WoT"] = cont(inp["L1_attn_Wo"].astype(f32).T)
    c["fr_col"] = cont(inp["L1_attn_freqs"].astype(f32).reshape(8, 128).T)
    slot = _tree_perm()
    used = slot >= 0
    WinP = np.zeros((JW, D), dtype=f32)
    binP = np.zeros(JW, dtype=f32)
    WoutP = np.zeros((D, JW), dtype=f32)
    WinP[used] = inp["L1_ffn_W_in"].astype(f32)[slot[used]]
    binP[used] = inp["L1_ffn_b_in"].astype(f32)[slot[used]]
    WoutP[:, used] = inp["L1_ffn_W_out"].astype(f32)[:, slot[used]]
    c["WinT"] = cont(WinP.T)
    c["bin_col"] = cont(binP.reshape(NJT, 128).T)
    c["WoutT"] = cont(WoutP.T)
    c["ident"] = np.eye(128, dtype=f32).astype(BF)
    c["eps_col"] = np.full((128, 1), 1e-5, dtype=f32)
    c["band"] = _band_patch()
    c["iota128_row"] = np.arange(128, dtype=f32).reshape(1, 128)
    c["iota32_row"] = np.arange(32, dtype=f32).reshape(1, 32)
    t_idx = np.arange(T)
    c["brow"] = (t_idx // 128).astype(BF).reshape(1, T)
    c["urow"] = (t_idx % 128).astype(BF).reshape(1, T)
    c["S0all"] = _S0ALL
    c["S1all"] = _S1ALL
    per_core = [{"x": cont(inp["L0_attn_x"].astype(f32)[i])} for i in range(8)]
    return c, per_core


CONST_SPECS = {
    "Wn": ([1024, 1024], F32), "b_col": ([128, 8], F32),
    "w1T": ([1024, 4096], F32), "b1_col": ([128, 32], F32),
    "w2T": ([4096, 1024], F32), "b2_row": ([1, 1024], F32),
    "WvT": ([1024, 1024], F32), "WoT": ([1024, 1024], F32),
    "fr_col": ([128, 8], F32), "WinT": ([1024, JW], F32),
    "bin_col": ([128, NJT], F32), "WoutT": ([JW, 1024], F32),
    "ident": ([128, 128], BF16), "band": ([128, 146], F32),
    "eps_col": ([128, 1], F32),
    "iota128_row": ([1, 128], F32), "iota32_row": ([1, 32], F32),
    "brow": ([1, T], BF16), "urow": ([1, T], BF16),
    "S0all": ([128, N_SC * 128], BF16), "S1all": ([128, N_SC * 128], BF16),
}



class Sem:
    __slots__ = ("h", "n")

    def __init__(self, nc, es, name):
        self.h = es.enter_context(nc.semaphore(name))
        self.n = 0

    def inc(self, inst, k=1):
        inst.then_inc(self.h, k)
        self.n += k
        return self.n

    def dinc(self, inst):
        return self.inc(inst, 16)


class DStream:

    def __init__(self, ctx, name, nbuf=2):
        self.sems = [ctx.S(name + str(j)) for j in range(nbuf)]
        self.nbuf = nbuf
        self.count = 0

    def issue(self, inst):
        i = self.count
        self.sems[i % self.nbuf].inc(inst, 16)
        self.count += 1
        return i

    def wait_done(self, eng, i):
        if i < 0:
            return
        eng.wait_ge(self.sems[i % self.nbuf].h, 16 * (i // self.nbuf + 1))

    def wait_all(self, eng):
        for s in self.sems:
            if s.n:
                eng.wait_ge(s.h, s.n)


_SS_WRAP = {
    "tensor_copy", "tensor_tensor", "tensor_add", "tensor_sub", "tensor_mul",
    "tensor_max", "tensor_scalar", "tensor_single_scalar", "tensor_scalar_add",
    "tensor_scalar_sub", "tensor_scalar_mul", "tensor_scalar_max",
    "tensor_scalar_min", "tensor_relu", "tensor_reduce", "reduce_max",
    "reduce_sum", "scalar_tensor_tensor", "tensor_tensor_scan", "memset",
    "memzero", "reciprocal", "select", "transpose", "activation", "copy",
    "sign", "square", "sqrt", "mul", "add", "engine_nop", "bn_stats",
    "bn_aggr", "stream_shuffle", "cody_waite_cascade", "add_range_wrap",
    "affine_then_add", "affine_mul_reduce", "tensor_tensor_reduce",
}


class SS:

    def __init__(self, raw, sem):
        self._r = raw
        self._s = sem

    def __getattr__(self, name):
        f = getattr(self._r, name)
        if name in _SS_WRAP:
            def w(*a, **k):
                if self._s.n:
                    self._r.wait_ge(self._s.h, self._s.n)
                ins = f(*a, **k)
                self._s.inc(ins)
                return ins
            return w
        return f


class Chain:

    def __init__(self, raw):
        self.raw = raw
        self.prev = None

    def wait_ge(self, h, n):
        self.raw.wait_ge(h, n)

    def do(self, sem, f):
        if self.prev is not None:
            ps, pn = self.prev
            self.raw.wait_ge(ps.h, pn)
        ins = f(self.raw)
        sem.inc(ins)
        self.prev = (sem, sem.n)
        return ins


class Ctx:
    def __init__(self, nc, es, block):
        self.nc = nc
        self.es = es
        self.block = block
        self.t = {}
        self.sem = {}
        self.store_streams = []

    def S(self, name):
        if name not in self.sem:
            self.sem[name] = Sem(self.nc, self.es, name)
        return self.sem[name]

    def sb(self, name, shape, dtype, es=None, side="left"):
        h = (es or self.es).enter_context(
            self.nc.sbuf_tensor(name, shape, dtype, side=side))
        self.t[name] = h
        return h

    def ps(self, name, shape, dtype, es=None):
        h = (es or self.es).enter_context(self.nc.psum_tensor(name, shape, dtype))
        self.t[name] = h
        return h


def barrier(ctx):
    bl = ctx.block
    sems = [ctx.S("bar_" + e) for e in ("sy", "te", "ve", "se", "ge")]
    all_waits = [(s.h, s.n) for nm, s in ctx.sem.items()
                 if s.n and not nm.startswith("bar_")]

    def mk(idx):
        def f(eng):
            for h, n in all_waits:
                eng.wait_ge(h, n)
            sems[idx].inc(eng.nop())
        return f
    bl.sync(mk(0)); bl.tensor(mk(1)); bl.vector(mk(2)); bl.scalar(mk(3)); bl.gpsimd(mk(4))
    targets = [s.n for s in sems]

    def mkw(my_idx):
        def f(eng):
            for i, s in enumerate(sems):
                if i != my_idx:
                    eng.wait_ge(s.h, targets[i])
        return f
    bl.sync(mkw(0)); bl.tensor(mkw(1)); bl.vector(mkw(2)); bl.scalar(mkw(3)); bl.gpsimd(mkw(4))


def emit_preload(ctx, names, es=None):
    bl = ctx.block
    LDC = ctx.S("ldc")
    table = {
        "ident": ("ident_sb", [128, 128], BF16, False),
        "eps_col": ("eps_col_sb", [128, 1], F32, False),
        "band": ("band_sb", [128, 146], F32, False),
        "b_col": ("b_col_sb", [128, 8], F32, False),
        "b1_col": ("b1_col_sb", [128, 32], F32, False),
        "bin_col": ("bin_col_sb", [128, NJT], F32, False),
        "fr_col": ("fr_col_sb", [128, 8], F32, False),
        "S0all": ("S0_sb", [128, N_SC * 128], BF16, False),
        "S1all": ("S1_sb", [128, N_SC * 128], BF16, False),
        "iota128_row": ("iota128_sb", [128, 128], F32, True),
        "iota32_row": ("iota32_sb", [128, 32], F32, True),
        "b2_row": ("b2bc_sb", [128, 1024], F32, True),
        "brow": ("brow_sb", [128, T], BF16, True),
        "urow": ("urow_sb", [128, T], BF16, True),
    }
    plans = []
    for nm in names:
        sbn, shape, dt, bc = table[nm]
        if sbn not in ctx.t:
            ctx.sb(sbn, shape, dt, es)
        plans.append((sbn, nm, bc))

    def sec(sync):
        for sbn, nm, bc in plans:
            src = ctx.t["d_" + nm]
            if bc:
                ins = sync.dma_start(out=ctx.t[sbn][:],
                                     in_=src[0:1, :].partition_broadcast(128))
            else:
                ins = sync.dma_start(out=ctx.t[sbn][:], in_=src)
            LDC.dinc(ins)
    bl.sync(sec)




def emit_ln(ctx, tag, src_dram, dst_name, spill_dram, es, dst_es=None,
            dst_side="left"):
    bl = ctx.block
    DEP = 4
    dst = ctx.sb(dst_name, [128, NKT * T], BF16, dst_es or es, side=dst_side)
    xb = ctx.sb(tag + "_x", [128, DEP * D], F32, es)
    xnT = ctx.sb(tag + "_xnT", [128, DEP * D], BF16, es)
    st = ctx.sb(tag + "_st", [128, 16 * DEP], F32, es)
    pT = ctx.ps(tag + "_pT", [128, DEP * 1024], BF16, es)
    ident = ctx.t["ident_sb"]
    ldc_target = ctx.S("ldc").n

    LD = [DStream(ctx, tag + "ldx"), DStream(ctx, tag + "ldy")]
    VE = ctx.S(tag + "ve"); SE = ctx.S(tag + "se"); TEs = ctx.S(tag + "te")
    SP = ctx.S("spill")
    xnc = {}; agc = {}; rsc = {}; cpc = {}; tec = {}; stc = {}

    dstv = dst[:].rearrange("p (k t) -> p k t", k=NKT)

    for i in range(NTT + 2):
        s = i % DEP

        if i < NTT:
            def sec_load(eng, i=i, s=s):
                if i >= DEP:
                    eng.wait_ge(VE.h, xnc[i - DEP])
                LD[i % 2].issue(eng.dma_start(
                    out=xb[:, s * D:(s + 1) * D],
                    in_=src_dram[i * 128:(i + 1) * 128, :]))
            (bl.sync if i % 2 == 0 else bl.gpsimd)(sec_load)

        def sec_ve(vector, i=i):
            vector = SS(vector, VE)
            if i < NTT:
                s = i % DEP
                c = st[:, s * 16:s * 16 + 16]
                LD[i % 2].wait_done(vector, i // 2)
                if i >= DEP:
                    vector.wait_ge(SE.h, rsc[i - DEP])
                vector.bn_stats(out=c[:, 0:6], in_=xb[:, s * D:s * D + 512])
                vector.bn_stats(out=c[:, 6:12], in_=xb[:, s * D + 512:s * D + 1024])
                vector.bn_aggr(out=c[:, 12:14], in_=c[:, 0:12])
                agc[i] = VE.n
            if 1 <= i <= NTT:
                j = i - 1
                sj = j % DEP
                c = st[:, sj * 16:sj * 16 + 16]
                vector.wait_ge(SE.h, rsc[j])
                vector.reciprocal(out=c[:, 15:16], in_=c[:, 14:15])
                vector.tensor_scalar(out=xnT[:, sj * D:(sj + 1) * D],
                                     in0=xb[:, sj * D:(sj + 1) * D],
                                     scalar1=c[:, 12:13], scalar2=c[:, 15:16],
                                     op0=ALU.subtract, op1=ALU.mult)
                xnc[j] = VE.n
            if i >= 2 and (i - 2) % 2 == 1:
                j = i - 2
                sj = j % DEP
                vector.wait_ge(TEs.h, tec[j])
                vector.tensor_copy(out=dstv[:, :, j * 128:(j + 1) * 128],
                                   in_=pT[:, sj * 1024:(sj + 1) * 1024].rearrange(
                                       "p (k t) -> p k t", k=NKT))
                cpc[j] = ("ve", VE.n)
        bl.vector(sec_ve)

        if i >= 1 and i - 1 < NTT:
            def sec_pe(tensor, i=i):
                j = i - 1
                sj = j % DEP
                if j == 0:
                    tensor.wait_ge(ctx.S("ldc").h, ldc_target)
                tensor.wait_ge(VE.h, xnc[j])
                if j >= DEP:
                    eng, n = cpc[j - DEP]
                    tensor.wait_ge(VE.h if eng == "ve" else SE.h, n)
                ins = None
                for k in range(NKT):
                    ins = tensor.transpose(
                        pT[:, sj * 1024 + k * 128:sj * 1024 + (k + 1) * 128],
                        xnT[:, sj * D + k * 128:sj * D + (k + 1) * 128],
                        ident[:])
                TEs.inc(ins)
                tec[j] = TEs.n
            bl.tensor(sec_pe)

        def sec_se(scalar, i=i):
            scalar = SS(scalar, SE)
            if i < NTT:
                s = i % DEP
                if i == 0:
                    scalar.wait_ge(ctx.S("ldc").h, ldc_target)
                scalar.wait_ge(VE.h, agc[i])
                scalar.activation(out=st[:, s * 16 + 14:s * 16 + 15],
                                  in_=st[:, s * 16 + 13:s * 16 + 14],
                                  func=AF.Sqrt, bias=ctx.t["eps_col_sb"][:, 0:1])
                rsc[i] = SE.n
            if i >= 2 and (i - 2) % 2 == 0:
                j = i - 2
                sj = j % DEP
                scalar.wait_ge(TEs.h, tec[j])
                scalar.copy(out=dstv[:, :, j * 128:(j + 1) * 128],
                            in_=pT[:, sj * 1024:(sj + 1) * 1024].rearrange(
                                "p (k t) -> p k t", k=NKT))
                cpc[j] = ("se", SE.n)
        bl.scalar(sec_se)

    if spill_dram is not None:
        spillv = spill_dram.rearrange("p (k t) -> p k t", k=NKT)
        chunk_counts = []

        def sec_spill(sync):
            for q in range(NCH):
                for j in (4 * q + 2, 4 * q + 3):
                    eng, n = cpc[j]
                    sync.wait_ge(VE.h if eng == "ve" else SE.h, n)
                SP.dinc(sync.dma_start(out=spillv[:, :, q * 512:(q + 1) * 512],
                                       in_=dstv[:, :, q * 512:(q + 1) * 512]))
                chunk_counts.append(SP.n)
        bl.sync(sec_spill)
        ctx.spill_chunks = chunk_counts
    return dst




def emit_fproj(ctx, xnC_dram, es, dst_es):
    bl = ctx.block
    fC = ctx.sb("fC", [128, NKT * T], BF16, dst_es)
    fT = ctx.sb("fT", [128, NTT * 1024], BF16, dst_es)
    Wb = ctx.sb("Wb", [128, NKT * 1024], BF16, es)
    xchunk = ctx.sb("b_xc", [128, 2 * NKT * 512], BF16, es)
    pf = ctx.ps("b_pf", [128, 2 * 512], F32, es)
    pT = ctx.ps("b_pT", [128, 2 * 1024], BF16, es)
    ident = ctx.t["ident_sb"]
    b_col = ctx.t["b_col_sb"]
    dWn = ctx.t["d_Wn"]
    xnCv_dram = xnC_dram.rearrange("p (k t) -> p k t", k=NKT)
    ldc_target = ctx.S("ldc").n

    LG = ctx.S("b_lg"); LXC = DStream(ctx, "b_lxc"); TE = ctx.S("b_te"); SE = ctx.S("b_se")
    T2 = ctx.S("b_t2"); V2 = ctx.S("b_v2")

    def sec_wload(gpsimd):
        for k in range(NKT):
            LG.dinc(gpsimd.dma_start(out=Wb[:, k * 1024:(k + 1) * 1024],
                                     in_=dWn[k * 128:(k + 1) * 128, :]))
    bl.gpsimd(sec_wload)

    def sec_xcload(sync):
        for q in range(NCH):
            sync.wait_ge(ctx.S("spill").h, ctx.spill_chunks[q])
            if q >= 2:
                sync.wait_ge(TE.h, (q - 1) * NKT)
            LXC.issue(sync.dma_start(
                out=xchunk[:, (q % 2) * NKT * 512:(q % 2 + 1) * NKT * 512].rearrange(
                    "p (k t) -> p k t", k=NKT),
                in_=xnCv_dram[:, :, q * 512:(q + 1) * 512]))
    bl.sync(sec_xcload)

    def sec_mm(tensor):
        tensor.wait_ge(LG.h, 16 * NKT)
        tensor.wait_ge(ctx.S("ldc").h, ldc_target)
        for q in range(NCH):
            LXC.wait_done(tensor, q)
            for j in range(NKT):
                r = q * NKT + j
                if r >= 2:
                    tensor.wait_ge(SE.h, r - 1)
                ins = None
                for k in range(NKT):
                    ins = tensor.matmul(pf[:, (r % 2) * 512:(r % 2 + 1) * 512],
                                        Wb[:, k * 1024 + j * 128:k * 1024 + (j + 1) * 128],
                                        xchunk[:, (q % 2) * NKT * 512 + k * 512:(q % 2) * NKT * 512 + (k + 1) * 512],
                                        start=(k == 0), stop=(k == NKT - 1))
                TE.inc(ins)
        tensor.wait_ge(SE.h, NKT * NCH)
        for i in range(NTT):
            if i >= 2:
                tensor.wait_ge(V2.h, i - 1)
            ins = None
            for k in range(NKT):
                ins = tensor.transpose(pT[:, (i % 2) * 1024 + k * 128:(i % 2) * 1024 + (k + 1) * 128],
                                       fC[:, k * T + i * 128:k * T + (i + 1) * 128],
                                       ident[:])
            T2.inc(ins)
    bl.tensor(sec_mm)

    def sec_evac(scalar):
        scalar = SS(scalar, SE)
        for q in range(NCH):
            for j in range(NKT):
                r = q * NKT + j
                scalar.wait_ge(TE.h, r + 1)
                scalar.activation(out=fC[:, j * T + q * 512:j * T + (q + 1) * 512],
                                  in_=pf[:, (r % 2) * 512:(r % 2 + 1) * 512],
                                  func=AF.Identity, bias=b_col[:, j:j + 1])
    bl.scalar(sec_evac)

    def sec_trcopy(vector):
        vector = SS(vector, V2)
        for i in range(NTT):
            vector.wait_ge(T2.h, i + 1)
            vector.tensor_copy(out=fT[:, i * 1024:(i + 1) * 1024],
                               in_=pT[:, (i % 2) * 1024:(i % 2 + 1) * 1024])
    bl.vector(sec_trcopy)
    return fC, fT




def emit_attn0(ctx, x_dram, x1p_dram, es):
    bl = ctx.block
    fC, fT = ctx.t["fC"], ctx.t["fT"]
    e_sb = ctx.sb("e_sb", [128, 2 * T], F32, es)
    a_sb = ctx.sb("a_sb", [128, 2 * T], BF16, es)
    aT_sb = ctx.sb("aT_sb", [128, T], BF16, es)
    xr = ctx.sb("c_xr", [128, 2 * D], F32, es)
    x1p = ctx.sb("c_x1p", [128, 2 * D], F32, es)
    cst = ctx.sb("c_st", [128, 4], F32, es)
    inv = ctx.sb("c_inv", [128, 2], F32, es)
    pe_ = ctx.ps("c_pe", [128, 2 * 512], F32, es)
    pA = ctx.ps("c_pA", [128, 2 * 1024], BF16, es)
    pO = ctx.ps("c_pO", [128, 2 * 1024], F32, es)
    band = ctx.t["band_sb"]
    b2bc = ctx.t["b2bc_sb"]
    ident = ctx.t["ident_sb"]

    TEs = ctx.S("c_te"); EVa = ctx.S("c_eva"); EVb = ctx.S("c_evb"); GE = ctx.S("c_ge")
    SEx = ctx.S("c_se"); VEi = ctx.S("c_vi"); TT_ = ctx.S("c_tt")
    TEV = ctx.S("c_tev"); TO = ctx.S("c_to"); OV = ctx.S("c_ov")
    VEm = ctx.S("c_vm")
    ldc_c = ctx.S("ldc").n
    LXR = DStream(ctx, "c_lxr"); SP1 = DStream(ctx, "c_sp1")
    ctx.store_streams.append(SP1)

    def sec_pe(tensor):
        for s in range(NTT + 1):
            if s < NTT:
                for q in range(NCH):
                    r = s * NCH + q
                    if r >= 2:
                        if r % 2 == 0:
                            tensor.wait_ge(EVa.h, r // 2)
                        else:
                            tensor.wait_ge(EVb.h, (r - 1) // 2)
                    ins = None
                    for k in range(NKT):
                        ins = tensor.matmul(pe_[:, (r % 2) * 512:(r % 2 + 1) * 512],
                                            fC[:, k * T + s * 128:k * T + (s + 1) * 128],
                                            fC[:, k * T + q * 512:k * T + (q + 1) * 512],
                                            start=(k == 0), stop=(k == NKT - 1))
                    TEs.inc(ins)
            if s >= 1:
                sp = s - 1
                tensor.wait_ge(SEx.h, sp + 1)
                for g in range(4):
                    gi = sp * 4 + g
                    if gi >= 2:
                        tensor.wait_ge(TEV.h, gi - 1)
                    ins = None
                    for u in range(8):
                        i = g * 8 + u
                        ins = tensor.transpose(
                            pA[:, (gi % 2) * 1024 + u * 128:(gi % 2) * 1024 + (u + 1) * 128],
                            a_sb[:, (sp % 2) * T + i * 128:(sp % 2) * T + (i + 1) * 128],
                            ident[:])
                    TT_.inc(ins)
                    if g >= 1:
                        tensor.wait_ge(TEV.h, sp * 4 + g)
                        if g == 1 and sp >= 2:
                            tensor.wait_ge(OV.h, 2 * (sp - 1))
                        for q2 in range(2):
                            for i in range((g - 1) * 8, g * 8):
                                tensor.matmul(
                                    pO[:, (sp % 2) * 1024 + q2 * 512:(sp % 2) * 1024 + (q2 + 1) * 512],
                                    aT_sb[:, i * 128:(i + 1) * 128],
                                    fT[:, i * 1024 + q2 * 512:i * 1024 + (q2 + 1) * 512],
                                    start=(i == 0), stop=False)
                tensor.wait_ge(TEV.h, sp * 4 + 4)
                for q2 in range(2):
                    ins = None
                    for i in range(24, NTT):
                        ins = tensor.matmul(
                            pO[:, (sp % 2) * 1024 + q2 * 512:(sp % 2) * 1024 + (q2 + 1) * 512],
                            aT_sb[:, i * 128:(i + 1) * 128],
                            fT[:, i * 1024 + q2 * 512:i * 1024 + (q2 + 1) * 512],
                            start=False, stop=(i == NTT - 1))
                    TO.inc(ins)
    bl.tensor(sec_pe)

    def sec_pool(gpsimd):
        gpsimd = SS(gpsimd, GE)
        gpsimd.wait_ge(ctx.S("ldc").h, ldc_c)
        for s in range(NTT):
            gpsimd.wait_ge(EVa.h, (s + 1) * 4)
            gpsimd.wait_ge(EVb.h, (s + 1) * 4)
            base = (s % 2) * T
            lo = s * 128 - 9
            if s == 0:
                gpsimd.tensor_add(out=e_sb[:, base:base + 137],
                                  in0=e_sb[:, base:base + 137], in1=band[:, 9:146])
            elif s == NTT - 1:
                gpsimd.tensor_add(out=e_sb[:, base + lo:base + T],
                                  in0=e_sb[:, base + lo:base + T], in1=band[:, 0:137])
            else:
                gpsimd.tensor_add(out=e_sb[:, base + lo:base + lo + 146],
                                  in0=e_sb[:, base + lo:base + lo + 146], in1=band[:])
    bl.gpsimd(sec_pool)

    def sec_act(scalar):
        ch = Chain(scalar)

        def emit_exp(s):
            ch.wait_ge(VEm.h, s + 1)
            if s >= 2:
                ch.wait_ge(TT_.h, (s - 1) * 4)
                ch.wait_ge(VEi.h, s - 1)
            ch.do(SEx, lambda sc, s=s: sc.activation(
                out=a_sb[:, (s % 2) * T:(s % 2 + 1) * T],
                in_=e_sb[:, (s % 2) * T:(s % 2 + 1) * T],
                func=AF.Exp, bias=cst[:, (s % 2):(s % 2) + 1],
                accum_out=cst[:, 2 + (s % 2):3 + (s % 2)]))

        def emit_tevc(sp):
            for g in range(4):
                gi = sp * 4 + g
                ch.wait_ge(TT_.h, gi + 1)
                if sp >= 1 and g == 0:
                    ch.wait_ge(TO.h, 2 * sp)
                ch.do(TEV, lambda sc, g=g, gi=gi: sc.copy(
                    out=aT_sb[:, g * 1024:(g + 1) * 1024],
                    in_=pA[:, (gi % 2) * 1024:(gi % 2 + 1) * 1024]))

        for s in range(NTT):
            if s >= 1:
                emit_exp(s - 1)
            for q in range(0, NCH, 2):
                r = s * NCH + q
                ch.wait_ge(TEs.h, r + 1)
                if s >= 2 and q == 0:
                    ch.wait_ge(SEx.h, s - 1)
                ch.do(EVa, lambda sc, s=s, q=q, r=r: sc.copy(
                    out=e_sb[:, (s % 2) * T + q * 512:(s % 2) * T + (q + 1) * 512],
                    in_=pe_[:, (r % 2) * 512:(r % 2 + 1) * 512]))
            if s >= 1:
                emit_tevc(s - 1)
        emit_exp(NTT - 1)
        emit_tevc(NTT - 1)
    bl.scalar(sec_act)

    def sec_dve(vector):
        ch = Chain(vector)
        ch.wait_ge(ctx.S("ldc").h, ldc_c)

        def emit_tail(sp):
            ch.wait_ge(SEx.h, sp + 1)
            ch.do(VEi, lambda v, sp=sp: v.reciprocal(
                out=inv[:, (sp % 2):(sp % 2) + 1],
                in_=cst[:, 2 + (sp % 2):3 + (sp % 2)]))
            ch.wait_ge(TO.h, 2 * sp + 2)
            LXR.wait_done(ch, sp)
            SP1.wait_done(ch, sp - 2)
            ch.do(OV, lambda v, sp=sp: v.scalar_tensor_tensor(
                out=x1p[:, (sp % 2) * D:(sp % 2 + 1) * D],
                in0=pO[:, (sp % 2) * 1024:(sp % 2 + 1) * 1024],
                scalar=inv[:, (sp % 2):(sp % 2) + 1],
                in1=xr[:, (sp % 2) * D:(sp % 2 + 1) * D],
                op0=ALU.mult, op1=ALU.add))
            ch.do(OV, lambda v, sp=sp: v.tensor_add(
                out=x1p[:, (sp % 2) * D:(sp % 2 + 1) * D],
                in0=x1p[:, (sp % 2) * D:(sp % 2 + 1) * D],
                in1=b2bc[:]))

        for s in range(NTT):
            for q in range(1, NCH, 2):
                r = s * NCH + q
                ch.wait_ge(TEs.h, r + 1)
                if s >= 2 and q == 1:
                    ch.wait_ge(SEx.h, s - 1)
                ch.do(EVb, lambda v, s=s, q=q, r=r: v.tensor_copy(
                    out=e_sb[:, (s % 2) * T + q * 512:(s % 2) * T + (q + 1) * 512],
                    in_=pe_[:, (r % 2) * 512:(r % 2 + 1) * 512]))
            ch.wait_ge(GE.h, s + 1)
            ch.do(VEm, lambda v, s=s: v.reduce_max(
                out=cst[:, (s % 2):(s % 2) + 1],
                in_=e_sb[:, (s % 2) * T:(s % 2) * T + T], axis=AX.X, negate=True))
            if s >= 1:
                emit_tail(s - 1)
        emit_tail(NTT - 1)
    bl.vector(sec_dve)

    def sec_sync(sync):
        def store(sp):
            sync.wait_ge(OV.h, 2 * (sp + 1))
            SP1.issue(sync.dma_start(out=x1p_dram[sp * 128:(sp + 1) * 128, :],
                                     in_=x1p[:, (sp % 2) * D:(sp % 2 + 1) * D]))
        for s in range(NTT):
            if s >= 2:
                sync.wait_ge(OV.h, 2 * (s - 1))
            LXR.issue(sync.dma_start(out=xr[:, (s % 2) * D:(s % 2 + 1) * D],
                                     in_=x_dram[s * 128:(s + 1) * 128, :]))
            if s >= 2:
                store(s - 2)
        store(NTT - 2)
        store(NTT - 1)
        SP1.wait_all(sync)
    bl.sync(sec_sync)




def emit_ffn0(ctx, xnC_dram, x1p_dram, x1_dram, es):
    bl = ctx.block
    sw1T = ctx.sb("sw1T", [128, NKT * T], BF16, es)
    sw2T = ctx.sb("sw2T", [128, NTT * 1024], BF16, es)
    xchunk = ctx.sb("d_xc", [128, 2 * NKT * 512], BF16, es)
    x1p = ctx.sb("d_x1p", [128, 2 * D], F32, es)
    x1o = ctx.sb("d_x1o", [128, 2 * D], F32, es)
    a2bc = ctx.sb("d_a2bc", [128, 1024], BF16, es)
    a1col = ctx.sb("d_a1col", [128, NTT], F32, es)
    es_prep = contextlib.ExitStack()
    wraw = ctx.sb("d_wraw", [128, 4 * T], BF16, es_prep)
    a1bc = ctx.sb("d_a1bc", [128, T], BF16, es_prep)
    ones128 = ctx.sb("d_ones", [128, 128], BF16, es_prep)
    pAw = ctx.ps("d_pAw", [128, NCH * 512], F32, es_prep)
    b1_col = ctx.t["b1_col_sb"]
    dw1T, dw2T = ctx.t["d_w1T"], ctx.t["d_w2T"]
    xnCv_dram = xnC_dram.rearrange("p (k t) -> p k t", k=NKT)

    LW = ctx.S("d_lw"); SE = ctx.S("d_se"); TE = ctx.S("d_te"); VE = ctx.S("d_ve")
    LXC = ctx.S("d_lxc"); LX1 = DStream(ctx, "d_lx1"); SX1 = DStream(ctx, "d_sx1")
    ctx.store_streams.append(SX1)
    TH = ctx.S("d_th"); SH = ctx.S("d_sh"); TO = ctx.S("d_to"); OV = ctx.S("d_ov")

    def sec_ones(vector):
        vector = SS(vector, VE)
        vector.memset(ones128[:], 1.0)
    bl.vector(sec_ones)

    def sec_wload(gpsimd):
        for k in range(NKT):
            if k >= 4:
                gpsimd.wait_ge(TE.h, (k - 3) * NCH)
            LW.dinc(gpsimd.dma_start(out=wraw[:, (k % 4) * T:(k % 4 + 1) * T],
                                     in_=dw1T[k * 128:(k + 1) * 128, :]))
        for jt in range(NTT):
            r = NKT + jt
            if jt < 4:
                gpsimd.wait_ge(TE.h, NKT * NCH)
            else:
                gpsimd.wait_ge(TE.h, NKT * NCH + (jt - 3) * 2)
            LW.dinc(gpsimd.dma_start(out=wraw[:, (r % 4) * T:(r % 4) * T + 1024],
                                     in_=dw2T[jt * 128:(jt + 1) * 128, :]))
    bl.gpsimd(sec_wload)

    def sec_wact(scalar):
        scalar = SS(scalar, SE)
        for k in range(NKT):
            w = wraw[:, (k % 4) * T:(k % 4 + 1) * T]
            scalar.wait_ge(LW.h, 16 * (k + 1))
            scalar.activation(out=sw1T[:, k * T:(k + 1) * T], in_=w,
                              func=AF.Sign)
            scalar.activation(out=w, in_=w, func=AF.Abs)
        for jt in range(NTT):
            r = NKT + jt
            w = wraw[:, (r % 4) * T:(r % 4) * T + 1024]
            scalar.wait_ge(LW.h, 16 * (NKT + jt + 1))
            scalar.activation(out=sw2T[:, jt * 1024:(jt + 1) * 1024],
                              in_=w, func=AF.Sign)
            scalar.activation(out=w, in_=w, func=AF.Abs)
    bl.scalar(sec_wact)

    def sec_alpha_mm(tensor):
        tensor.wait_ge(VE.h, 1)
        for k in range(NKT):
            tensor.wait_ge(SE.h, 2 * k + 2)
            for jc in range(NCH):
                TE.inc(tensor.matmul(pAw[:, jc * 512:(jc + 1) * 512], ones128[:],
                                     wraw[:, (k % 4) * T + jc * 512:(k % 4) * T + (jc + 1) * 512],
                                     start=(k == 0), stop=(k == NKT - 1)))
        for jt in range(NTT):
            r = NKT + jt
            tensor.wait_ge(SE.h, 2 * NKT + 2 * jt + 2)
            if jt == 0:
                tensor.wait_ge(VE.h, 2)
            ins = None
            for dc in range(2):
                ins = tensor.matmul(pAw[:, dc * 512:(dc + 1) * 512], ones128[:],
                                    wraw[:, (r % 4) * T + dc * 512:(r % 4) * T + (dc + 1) * 512],
                                    start=(jt == 0), stop=(jt == NTT - 1))
            TE.inc(ins, 2)
    bl.tensor(sec_alpha_mm)

    def sec_alpha_ev(vector):
        vector = SS(vector, VE)
        vector.wait_ge(TE.h, NKT * NCH)
        vector.tensor_scalar(out=a1bc[:], in0=pAw[:], scalar1=1.0 / D,
                             scalar2=None, op0=ALU.mult)
        vector.wait_ge(TE.h, NKT * NCH + NTT * 2)
        vector.tensor_scalar(out=a2bc[:], in0=pAw[:, 0:1024], scalar1=1.0 / T,
                             scalar2=None, op0=ALU.mult)
    bl.vector(sec_alpha_ev)

    A1L = ctx.S("d_a1l")

    def sec_a1x(gpsimd):
        gpsimd.wait_ge(VE.h, 2)
        A1L.dinc(gpsimd.dma_start(out=ctx.t["d_a1s"], in_=a1bc[0:1, :]))
        gpsimd.wait_ge(A1L.h, 16)
        A1L.dinc(gpsimd.dma_start(
            out=a1col[:],
            in_=ctx.t["d_a1s"].rearrange("(j p) -> p j", p=128),
            allow_slow_non_contiguous=True))
    bl.gpsimd(sec_a1x)
    VE_PREP = ctx.S("d_ve").n
    es_prep.close()
    hC = ctx.sb("d_hC", [128, NTT * 512], BF16, es)

    pH = ctx.ps("d_pH", [128, 2 * 512], F32, es)
    pO = ctx.ps("d_pO", [128, 2 * 1024], F32, es)

    ldc_d = ctx.S("ldc").n

    def sec_sync(sync):
        sync.wait_ge(ctx.S("ldc").h, ldc_d)
        sync.wait_ge(ctx.S("spill").h, ctx.S("spill").n)

        def store(i):
            sync.wait_ge(OV.h, i + 1)
            s_tok = (i // 4) * 512 + (i % 4) * 128
            SX1.issue(sync.dma_start(out=x1_dram[s_tok:s_tok + 128, :],
                                     in_=x1o[:, (i % 2) * D:(i % 2 + 1) * D]))
        for q in range(NCH):
            if q >= 2:
                sync.wait_ge(TH.h, (q - 1) * NTT)
            LXC.dinc(sync.dma_start(
                out=xchunk[:, (q % 2) * NKT * 512:(q % 2 + 1) * NKT * 512].rearrange(
                    "p (k t) -> p k t", k=NKT),
                in_=xnCv_dram[:, :, q * 512:(q + 1) * 512]))
            for sub in range(4):
                i = q * 4 + sub
                if i >= 2:
                    sync.wait_ge(OV.h, i - 1)
                s_tok = q * 512 + sub * 128
                LX1.issue(sync.dma_start(out=x1p[:, (i % 2) * D:(i % 2 + 1) * D],
                                         in_=x1p_dram[s_tok:s_tok + 128, :]))
                if i >= 2:
                    store(i - 2)
        store(4 * NCH - 2)
        store(4 * NCH - 1)
        SX1.wait_all(sync)
    bl.sync(sec_sync)

    def sec_pe(tensor):
        for q in range(NCH):
            tensor.wait_ge(LXC.h, 16 * (q + 1))
            if q == 0:
                tensor.wait_ge(VE.h, VE_PREP)
                tensor.wait_ge(SE.h, 2 * NKT)
                tensor.wait_ge(A1L.h, 32)
            xoff = (q % 2) * NKT * 512
            for jt in range(NTT):
                r = q * NTT + jt
                if r >= 2:
                    tensor.wait_ge(SH.h, r - 1)
                ins = None
                for k in range(NKT):
                    ins = tensor.matmul(pH[:, (r % 2) * 512:(r % 2 + 1) * 512],
                                        sw1T[:, k * T + jt * 128:k * T + (jt + 1) * 128],
                                        xchunk[:, xoff + k * 512:xoff + (k + 1) * 512],
                                        start=(k == 0), stop=(k == NKT - 1))
                TH.inc(ins)
            tensor.wait_ge(SH.h, (q + 1) * NTT)
            for sub in range(4):
                i = q * 4 + sub
                if i >= 2:
                    tensor.wait_ge(OV.h, i - 1)
                for dc in range(2):
                    ins = None
                    for jt in range(NTT):
                        ins = tensor.matmul(
                            pO[:, (i % 2) * 1024 + dc * 512:(i % 2) * 1024 + (dc + 1) * 512],
                            hC[:, jt * 512 + sub * 128:jt * 512 + (sub + 1) * 128],
                            sw2T[:, jt * 1024 + dc * 512:jt * 1024 + (dc + 1) * 512],
                            start=(jt == 0), stop=(jt == NTT - 1))
                    TO.inc(ins)
    bl.tensor(sec_pe)

    def sec_hevac(scalar):
        scalar = SS(scalar, SH)
        for q in range(NCH):
            for jt in range(NTT):
                r = q * NTT + jt
                scalar.wait_ge(TH.h, r + 1)
                if q >= 1 and jt == 0:
                    scalar.wait_ge(TO.h, q * 8)
                scalar.activation(out=hC[:, jt * 512:(jt + 1) * 512],
                                  in_=pH[:, (r % 2) * 512:(r % 2 + 1) * 512],
                                  func=AF.Gelu, scale=a1col[:, jt:jt + 1],
                                  bias=b1_col[:, jt:jt + 1])
    bl.scalar(sec_hevac)



    def sec_oevac(vector):
        ch = Chain(vector)
        OVm = ctx.S("d_ovm")
        for q in range(NCH):
            for sub in range(4):
                i = q * 4 + sub
                ch.wait_ge(TO.h, 2 * i + 2)
                LX1.wait_done(ch, i)
                SX1.wait_done(ch, i - 2)
                ch.do(OVm, lambda v, i=i: v.tensor_mul(
                    out=x1o[:, (i % 2) * D:(i % 2 + 1) * D],
                    in0=pO[:, (i % 2) * 1024:(i % 2 + 1) * 1024],
                    in1=a2bc[:]))
                ch.do(OV, lambda v, i=i: v.tensor_add(
                    out=x1o[:, (i % 2) * D:(i % 2 + 1) * D],
                    in0=x1o[:, (i % 2) * D:(i % 2 + 1) * D],
                    in1=x1p[:, (i % 2) * D:(i % 2 + 1) * D]))
    bl.vector(sec_oevac)




def emit_rotor_prelude(ctx, es):
    bl = ctx.block
    Wvo = ctx.sb("Wvo", [128, NKT * 1024], BF16, es, side="right")
    ones_r = ctx.sb("f_ones", [128, T], BF16, es, side="right")
    gtab = ctx.sb("f_g", [128, 32], F32, es, side="right")
    hpi_col = ctx.sb("f_hpi", [128, 1], F32, es, side="right")
    thB = ctx.sb("f_thB", [128, NKT * 128], F32, es, side="right")
    uuB = ctx.sb("f_uuB", [128, NKT * 128], F32, es, side="right")
    sinB = ctx.sb("f_sinB", [128, NKT * 128], BF16, es, side="right")
    cosB16 = ctx.sb("f_cosB", [128, NKT * 128], BF16, es, side="right")
    cosB = ctx.sb("f_cosBW", [128, NKT * 128], F32, es, side="right")
    thA = ctx.sb("f_thA", [128, NKT * 32], F32, es, side="right")
    sinA = ctx.sb("f_sinA", [128, NKT * 32], BF16, es, side="right")
    cosA16 = ctx.sb("f_cosA", [128, NKT * 32], BF16, es, side="right")
    cosA = ctx.sb("f_cosAW", [128, NKT * 32], F32, es, side="right")
    iota128 = ctx.t["iota128_sb"]
    iota32 = ctx.t["iota32_sb"]
    fr_col = ctx.t["fr_col_sb"]
    dWvT = ctx.t["d_WvT"]
    LWV = ctx.S("f_lwv")
    VEs = ctx.S("f_ve")
    SEs = ctx.S("f_se")
    ldc_target = ctx.S("ldc").n

    def bc3(ap2d, mid, inner):
        if ap2d.shape[1] == inner:
            return ap2d.rearrange("p (o u) -> p o u", o=1).to_broadcast(
                [128, mid, inner])
        return ap2d.rearrange("p (b o) -> p b o", o=1).to_broadcast(
            [128, mid, inner])
    ctx.t["f_bc3"] = bc3

    def sec_wload_v(gpsimd):
        for k in range(NKT):
            LWV.dinc(gpsimd.dma_start(out=Wvo[:, k * 1024:(k + 1) * 1024],
                                      in_=dWvT[k * 128:(k + 1) * 128, :]))
    bl.gpsimd(sec_wload_v)

    def sec_g(vector):
        vector = SS(vector, VEs)
        vector.wait_ge(ctx.S("ldc").h, ldc_target)
        g, kg = gtab[:, 0:8], gtab[:, 24:32]
        vector.tensor_scalar(out=g, in0=fr_col[:], scalar1=128.0, scalar2=None,
                             op0=ALU.mult)
        vector.tensor_scalar(out=kg, in0=g, scalar1=INV2PI, scalar2=MAGIC,
                             op0=ALU.mult, op1=ALU.add)
        vector.tensor_scalar(out=kg, in0=kg, scalar1=MAGIC, scalar2=None,
                             op0=ALU.subtract)
        vector.scalar_tensor_tensor(out=g, in0=kg, scalar=-C1, in1=g,
                                    op0=ALU.mult, op1=ALU.add)
        vector.scalar_tensor_tensor(out=g, in0=kg, scalar=-C2, in1=g,
                                    op0=ALU.mult, op1=ALU.add)
        vector.memset(ones_r[:], 1.0)
        vector.memset(hpi_col[:], HALF_PI)
        thBv = thB[:].rearrange("p (k u) -> p k u", k=NKT)
        vector.tensor_mul(out=thBv, in0=bc3(fr_col[:], NKT, 128),
                          in1=bc3(iota128[:], NKT, 128))
        vector.tensor_scalar(out=uuB[:], in0=thB[:], scalar1=INV2PI,
                             scalar2=MAGIC, op0=ALU.mult, op1=ALU.add)
        vector.tensor_scalar(out=uuB[:], in0=uuB[:], scalar1=MAGIC,
                             scalar2=None, op0=ALU.subtract)
        vector.scalar_tensor_tensor(out=thB[:], in0=uuB[:], scalar=-TWOPI_F,
                                    in1=thB[:], op0=ALU.mult, op1=ALU.add)
        vector.tensor_scalar(out=cosB[:], in0=thB[:], scalar1=-PI_F,
                             scalar2=PI_F, op0=ALU.max, op1=ALU.min)
        thAv = thA[:].rearrange("p (k b) -> p k b", k=NKT)
        vector.tensor_mul(out=thAv, in0=bc3(gtab[:, 0:8], NKT, 32),
                          in1=bc3(iota32[:, 0:32], NKT, 32))
        vector.tensor_scalar(out=uuB[:, 0:NKT * 32], in0=thA[:], scalar1=INV2PI,
                             scalar2=MAGIC, op0=ALU.mult, op1=ALU.add)
        vector.tensor_scalar(out=uuB[:, 0:NKT * 32], in0=uuB[:, 0:NKT * 32],
                             scalar1=MAGIC, scalar2=None, op0=ALU.subtract)
        vector.scalar_tensor_tensor(out=thA[:], in0=uuB[:, 0:NKT * 32],
                                    scalar=-TWOPI_F,
                                    in1=thA[:], op0=ALU.mult, op1=ALU.add)
        vector.tensor_scalar(out=cosA[:], in0=thA[:], scalar1=-PI_F,
                             scalar2=PI_F, op0=ALU.max, op1=ALU.min)
    bl.vector(sec_g)
    ve_gbuild = VEs.n

    def sec_se_tab(scalar):
        scalar = SS(scalar, SEs)
        scalar.wait_ge(VEs.h, ve_gbuild)
        scalar.activation(out=sinB[:], in_=cosB[:], func=AF.Sin)
        scalar.activation(out=cosB[:], in_=cosB[:], func=AF.Abs)
        scalar.activation(out=cosB16[:], in_=cosB[:], func=AF.Sin,
                          scale=-1.0, bias=hpi_col[:, 0:1])
        scalar.activation(out=sinA[:], in_=cosA[:], func=AF.Sin)
        scalar.activation(out=cosA[:], in_=cosA[:], func=AF.Abs)
        scalar.activation(out=cosA16[:], in_=cosA[:], func=AF.Sin,
                          scale=-1.0, bias=hpi_col[:, 0:1])
    bl.scalar(sec_se_tab)
    ctx.t["f_setab_n"] = SEs.n


def emit_rotor(ctx, x1_dram, x2p_dram, rC_dram, es):
    bl = ctx.block
    xn1C = ctx.t["xn1C"]
    es_t = contextlib.ExitStack()
    Wvo = ctx.t["Wvo"]
    v_sb = ctx.sb("f_v", [128, 2 * T], BF16, es_t)
    sint = ctx.sb("f_sin", [128, T], BF16, es_t)
    cost = ctx.sb("f_cos", [128, T], BF16, es_t)
    tmp = ctx.sb("f_tmp", [128, T], BF16, es_t)
    Pb = ctx.sb("f_P", [128, T], BF16, es_t)
    Qb = ctx.sb("f_Q", [128, T], BF16, es_t)
    Pc = ctx.sb("f_Pc", [128, T], BF16, es_t)
    Qc = ctx.sb("f_Qc", [128, T], BF16, es_t)
    ones_r = ctx.t["f_ones"]
    sinA, cosA16 = ctx.t["f_sinA"], ctx.t["f_cosA"]
    sinB, cosB16 = ctx.t["f_sinB"], ctx.t["f_cosB"]
    pV = ctx.ps("f_pV", [128, 2 * 512], F32, es_t)
    dWoT = ctx.t["d_WoT"]
    bc3 = ctx.t["f_bc3"]

    LWV = ctx.S("f_lwv"); LWO2 = ctx.S("f_lwo")
    TE = ctx.S("f_te"); SEs = ctx.S("f_se"); VEs = ctx.S("f_ve")
    GEs = ctx.S("f_ge"); SRC = ctx.S("f_src")
    se_tab = ctx.t["f_setab_n"]
    se_base = SEs.n
    te_base = TE.n

    def sec_vmm(tensor):
        tensor.wait_ge(LWV.h, 16 * NKT)
        for k in range(NKT):
            for q in range(NCH):
                m = k * NCH + q
                if m >= 2:
                    kp2, qp2 = divmod(m - 2, NCH)
                    tensor.wait_ge(SEs.h, se_base + 8 * kp2 + qp2 + 1)
                ins = None
                for dt_ in range(NKT):
                    ins = tensor.matmul(
                        pV[:, (m % 2) * 512:(m % 2 + 1) * 512],
                        Wvo[:, dt_ * 1024 + k * 128:dt_ * 1024 + (k + 1) * 128],
                        xn1C[:, dt_ * T + q * 512:dt_ * T + (q + 1) * 512],
                        start=(dt_ == 0), stop=(dt_ == NKT - 1))
                TE.inc(ins)
    bl.tensor(sec_vmm)

    cnt = {nm: {} for nm in ("recon", "P", "Q", "S1", "S2", "m1", "m2", "r")}
    for kc in range(NKT + 1):
        k = kc
        kp = kc - 1

        if kp >= 0:
            def sec_ve_scan(vector, k=kp):
                vector = SS(vector, VEs)
                if k >= 1:
                    vector.wait_ge(SRC.h, 16 * k)
                vector.tensor_tensor_scan(out=Pc[:], data0=ones_r[:], data1=Pb[:],
                                          initial=0.0, op0=ALU.mult, op1=ALU.add)
                cnt["S1"][k] = VEs.n
                vector.tensor_tensor_scan(out=Qc[:], data0=ones_r[:], data1=Qb[:],
                                          initial=0.0, op0=ALU.mult, op1=ALU.add)
                cnt["S2"][k] = VEs.n
                vector.tensor_mul(out=Pb[:], in0=cost[:], in1=Pc[:])
                cnt["m1"][k] = VEs.n
                vector.tensor_mul(out=Qb[:], in0=sint[:], in1=Qc[:])
                cnt["m2"][k] = VEs.n
                vector.tensor_add(out=Pc[:], in0=Pb[:], in1=Qb[:])
                cnt["r"][k] = VEs.n
            bl.vector(sec_ve_scan)

            def sec_store(sync, k=kp):
                sync.wait_ge(VEs.h, cnt["r"][k])
                SRC.dinc(sync.dma_start(out=rC_dram[:, k * T:(k + 1) * T],
                                        in_=Pc[:]))
            bl.sync(sec_store)

        if k < NKT:
            def sec_ve_recon(vector, k=k):
                vector = SS(vector, VEs)
                vector.wait_ge(SEs.h, se_tab)
                As = bc3(sinA[:, k * 32:(k + 1) * 32], 32, 128)
                Ac = bc3(cosA16[:, k * 32:(k + 1) * 32], 32, 128)
                Bs = bc3(sinB[:, k * 128:(k + 1) * 128], 32, 128)
                Bc = bc3(cosB16[:, k * 128:(k + 1) * 128], 32, 128)
                sv = sint[:].rearrange("p (b u) -> p b u", b=32)
                cv = cost[:].rearrange("p (b u) -> p b u", b=32)
                tv = tmp[:].rearrange("p (b u) -> p b u", b=32)
                vector.tensor_mul(out=sv, in0=As, in1=Bc)
                vector.tensor_mul(out=tv, in0=Ac, in1=Bs)
                vector.tensor_add(out=sint[:], in0=sint[:], in1=tmp[:])
                vector.tensor_mul(out=cv, in0=Ac, in1=Bc)
                vector.tensor_mul(out=tv, in0=As, in1=Bs)
                vector.tensor_sub(out=cost[:], in0=cost[:], in1=tmp[:])
                cnt["recon"][k] = VEs.n
            bl.vector(sec_ve_recon)

        if k < NKT:
            def sec_se_ev(scalar, k=k):
                scalar = SS(scalar, SEs)
                if k >= 2:
                    scalar.wait_ge(VEs.h, cnt["Q"][k - 2])
                vo = (k % 2) * T
                for q in range(NCH):
                    m = k * NCH + q
                    scalar.wait_ge(TE.h, te_base + m + 1)
                    scalar.copy(out=v_sb[:, vo + q * 512:vo + (q + 1) * 512],
                                in_=pV[:, (m % 2) * 512:(m % 2 + 1) * 512])
                assert SEs.n == se_base + 8 * (k + 1), (SEs.n, se_base, k)
            bl.scalar(sec_se_ev)

            def sec_ve_p(vector, k=k):
                vector = SS(vector, VEs)
                vector.wait_ge(SEs.h, se_base + 8 * (k + 1))
                vector.tensor_mul(out=Pb[:], in0=v_sb[:, (k % 2) * T:(k % 2 + 1) * T],
                                  in1=cost[:])
                cnt["P"][k] = VEs.n
            bl.vector(sec_ve_p)

            def sec_ve_q(vector, k=k):
                vector = SS(vector, VEs)
                vector.tensor_mul(out=Qb[:], in0=v_sb[:, (k % 2) * T:(k % 2 + 1) * T],
                                  in1=sint[:])
                cnt["Q"][k] = VEs.n
            bl.vector(sec_ve_q)

    te_vmm_all = TE.n

    def sec_wload_o(gpsimd):
        gpsimd.wait_ge(TE.h, te_vmm_all)
        for k in range(NKT):
            LWO2.dinc(gpsimd.dma_start(out=Wvo[:, k * 1024:(k + 1) * 1024],
                                       in_=dWoT[k * 128:(k + 1) * 128, :]))
    bl.gpsimd(sec_wload_o)
    lw_o = ctx.S("f_lwo").n

    es_t.close()
    rbT = ctx.sb("f_rbT", [128, 2 * 1024], BF16, es)
    x1r = ctx.sb("f_x1r", [128, 2 * D], F32, es)
    x2p = ctx.sb("f_x2p", [128, 2 * D], F32, es)
    pO = ctx.ps("f_pO", [128, 2 * 1024], F32, es)
    TO = ctx.S("f_to"); OV = ctx.S("f_ov")
    LX1 = DStream(ctx, "f_lx1"); SX2 = DStream(ctx, "f_sx2")
    LRB = DStream(ctx, "f_lrb")
    ctx.store_streams.append(SX2)
    rCv_dram = rC_dram.rearrange("p (k t) -> p k t", k=NKT)

    def sec_sync2(sync):
        sync.wait_ge(SRC.h, 16 * NKT)

        def store(bp):
            sync.wait_ge(OV.h, bp + 1)
            SX2.issue(sync.dma_start(out=x2p_dram[bp * 128:(bp + 1) * 128, :],
                                     in_=x2p[:, (bp % 2) * D:(bp % 2 + 1) * D]))
        for b in range(NTT):
            if b >= 2:
                sync.wait_ge(TO.h, 2 * (b - 1))
            LRB.issue(sync.dma_start(
                out=rbT[:, (b % 2) * 1024:(b % 2 + 1) * 1024].rearrange(
                    "p (k t) -> p k t", k=NKT),
                in_=rCv_dram[:, :, b * 128:(b + 1) * 128]))
            if b >= 2:
                sync.wait_ge(OV.h, b - 1)
            LX1.issue(sync.dma_start(out=x1r[:, (b % 2) * D:(b % 2 + 1) * D],
                                     in_=x1_dram[b * 128:(b + 1) * 128, :]))
            if b >= 2:
                store(b - 2)
        store(NTT - 2)
        store(NTT - 1)
        SX2.wait_all(sync)
    bl.sync(sec_sync2)

    def sec_omm(tensor):
        tensor.wait_ge(LWO2.h, lw_o)
        for b in range(NTT):
            LRB.wait_done(tensor, b)
            if b >= 2:
                tensor.wait_ge(OV.h, b - 1)
            for q2 in range(2):
                ins = None
                for k in range(NKT):
                    ins = tensor.matmul(
                        pO[:, (b % 2) * 1024 + q2 * 512:(b % 2) * 1024 + (q2 + 1) * 512],
                        rbT[:, (b % 2) * 1024 + k * 128:(b % 2) * 1024 + (k + 1) * 128],
                        Wvo[:, k * 1024 + q2 * 512:k * 1024 + (q2 + 1) * 512],
                        start=(k == 0), stop=(k == NKT - 1))
                TO.inc(ins)
    bl.tensor(sec_omm)

    def sec_oev(vector):
        vector = SS(vector, OV)
        for b in range(NTT):
            vector.wait_ge(TO.h, 2 * b + 2)
            LX1.wait_done(vector, b)
            SX2.wait_done(vector, b - 2)
            vector.tensor_add(out=x2p[:, (b % 2) * D:(b % 2 + 1) * D],
                              in0=pO[:, (b % 2) * 1024:(b % 2 + 1) * 1024],
                              in1=x1r[:, (b % 2) * D:(b % 2 + 1) * D])
    bl.vector(sec_oev)




def emit_ffn1(ctx, xn1C_dram, x2p_dram, out_dram, es):
    bl = ctx.block
    if "WinTb" in ctx.t:
        WinTb, WoutTb = ctx.t["WinTb"], ctx.t["WoutTb"]
        preloaded = True
    else:
        WinTb = ctx.sb("WinTb", [128, NKT * JW], BF16, es)
        WoutTb = ctx.sb("WoutTb", [128, NJT * 1024], BF16, es)
        preloaded = False
    acts = ctx.sb("g_acts", [128, NJT * 512], BF16, es)
    dec = ctx.sb("g_dec", [128, 11 * 512], BF16, es)
    msk2 = ctx.sb("g_msk", [128, 2 * NJT * 512], BF16, es)
    u0 = ctx.sb("g_u0", [128, 11 * 512], BF16, es)
    u1 = ctx.sb("g_u1", [128, 11 * 512], BF16, es)
    xchunk = ctx.sb("g_xc", [128, 2 * NKT * 512], BF16, es)
    x2p = ctx.sb("g_x2p", [128, 2 * D], F32, es)
    xout = ctx.sb("g_out", [128, D], F32, es)
    pL = ctx.ps("g_pL", [128, 2 * 512], F32, es)
    pS = ctx.ps("g_pS", [128, 2 * 512], F32, es)
    pO = ctx.ps("g_pO", [128, 2 * 1024], F32, es)
    bin_col = ctx.t["bin_col_sb"]
    S0sb, S1sb = ctx.t["S0_sb"], ctx.t["S1_sb"]
    dWinT, dWoutT = ctx.t["d_WinT"], ctx.t["d_WoutT"]
    xn1Cv_dram = xn1C_dram.rearrange("p (k t) -> p k t", k=NKT)
    ldc_target = ctx.S("ldc").n

    LWI = ctx.S("g_lwi"); LWO = ctx.S("g_lwo")
    TE = ctx.S("g_te"); SE = ctx.S("g_se"); VE = ctx.S("g_ve")
    LXC = DStream(ctx, "g_lxc"); LX2 = DStream(ctx, "g_lx2")
    SOUT = DStream(ctx, "g_sout")
    OV = ctx.S("g_ov")

    if not preloaded:
        def sec_wload(gpsimd):
            for k in range(NKT):
                LWI.dinc(gpsimd.dma_start(out=WinTb[:, k * JW:(k + 1) * JW],
                                          in_=dWinT[k * 128:(k + 1) * 128, :]))
            for jt in range(NJT):
                LWO.dinc(gpsimd.dma_start(
                    out=WoutTb[:, jt * 1024:(jt + 1) * 1024],
                    in_=dWoutT[jt * 128:(jt + 1) * 128, :]))
        bl.gpsimd(sec_wload)

    def sec_xc0(sync):
        sync.wait_ge(ctx.S("ldc").h, ldc_target)
        sync.wait_ge(ctx.S("spill").h, ctx.S("spill").n)
        LXC.issue(sync.dma_start(
            out=xchunk[:, 0:NKT * 512].rearrange("p (k t) -> p k t", k=NKT),
            in_=xn1Cv_dram[:, :, 0:512]))
        LXC.issue(sync.dma_start(
            out=xchunk[:, NKT * 512:].rearrange("p (k t) -> p k t", k=NKT),
            in_=xn1Cv_dram[:, :, 512:1024]))
    bl.sync(sec_xc0)

    silu_end = {}; dec_end = {}; tree_end = {}
    tev_cnt = {}; uu_cnt = {}; tmm_cnt = {}; chain_to = {}

    def emit_omm_chain(q_out, c, part=None):
        sub, dc = c // 2, c % 2
        i2 = q_out * 4 + sub
        jts = range(NJT) if part is None else (
            range(0, 10) if part == 0 else range(10, NJT))

        def sec_ommc(tensor, q_out=q_out, sub=sub, dc=dc, i2=i2, jts=jts,
                     part=part):
            if part in (None, 0):
                if q_out == 0 and sub == 0 and dc == 0:
                    tensor.wait_ge(LWO.h, 16 * NJT)
                tensor.wait_ge(VE.h, tree_end[q_out])
                if i2 >= 2 and dc == 0:
                    tensor.wait_ge(OV.h, i2 - 1)
            moff = (q_out % 2) * NJT * 512
            ins = None
            for jt in jts:
                ins = tensor.matmul(
                    pO[:, (i2 % 2) * 1024 + dc * 512:(i2 % 2) * 1024 + (dc + 1) * 512],
                    msk2[:, moff + jt * 512 + sub * 128:moff + jt * 512 + (sub + 1) * 128],
                    WoutTb[:, jt * 1024 + dc * 512:jt * 1024 + (dc + 1) * 512],
                    start=(jt == 0), stop=(jt == NJT - 1))
            if part in (None, 1):
                TO = ctx.S("g_to")
                TO.inc(ins)
                chain_to[(q_out, sub, dc)] = TO.n
        bl.tensor(sec_ommc)

    TO = ctx.S("g_to")

    for q in range(NCH):
        te_l0 = TE.n
        ve0 = ctx.S("g_ve").n
        se0 = ctx.S("g_se").n
        xoff = (q % 2) * NKT * 512

        def sec_lmm(tensor, q=q, xoff=xoff, ve0=ve0, se0=se0):
            LXC.wait_done(tensor, q)
            if q == 0:
                tensor.wait_ge(LWI.h, 16 * NKT)
            for jt in range(NJT):
                if jt < 2 and q >= 1:
                    tensor.wait_ge(SE.h, silu_end[q - 1])
                    tensor.wait_ge(VE.h, dec_end[q - 1])
                if jt >= 2:
                    tensor.wait_ge(SE.h, se0 + jt - 1)
                    if jt - 2 < 11:
                        tensor.wait_ge(VE.h, ve0 + jt - 1)
                ins = None
                for k in range(NKT):
                    ins = tensor.matmul(pL[:, (jt % 2) * 512:(jt % 2 + 1) * 512],
                                        WinTb[:, k * JW + jt * 128:k * JW + (jt + 1) * 128],
                                        xchunk[:, xoff + k * 512:xoff + (k + 1) * 512],
                                        start=(k == 0), stop=(k == NKT - 1))
                TE.inc(ins)
        bl.tensor(sec_lmm)
        te_lmm = TE.n

        def sec_silu(scalar, q=q, te_l0=te_l0):
            scalar = SS(scalar, SE)
            if q >= 1:
                scalar.wait_ge(VE.h, tree_end[q - 1])
            for jt in range(NJT):
                scalar.wait_ge(TE.h, te_l0 + jt + 1)
                scalar.activation(out=acts[:, jt * 512:(jt + 1) * 512],
                                  in_=pL[:, (jt % 2) * 512:(jt % 2 + 1) * 512],
                                  func=AF.Silu, bias=bin_col[:, jt:jt + 1])
        bl.scalar(sec_silu)
        silu_end[q] = ctx.S("g_se").n

        def sec_dec(vector, q=q, te_l0=te_l0, se0=se0):
            vector = SS(vector, VE)
            for jt in range(11):
                vector.wait_ge(TE.h, te_l0 + jt + 1)
                vector.wait_ge(SE.h, se0 + jt + 1)
                vector.tensor_scalar(out=dec[:, jt * 512:(jt + 1) * 512],
                                     in0=pL[:, (jt % 2) * 512:(jt % 2 + 1) * 512],
                                     scalar1=bin_col[:, jt:jt + 1], scalar2=0.0,
                                     op0=ALU.add, op1=ALU.is_gt)
            vector.tensor_copy(out=u1[:, 0:512], in_=dec[:, 0:512])
            vector.tensor_scalar(out=u0[:, 0:512], in0=dec[:, 0:512],
                                 scalar1=-1.0, scalar2=1.0,
                                 op0=ALU.mult, op1=ALU.add)
            uu_cnt[(q, 0)] = ctx.S("g_ve").n
            if q >= 2:
                vector.wait_ge(TO.h, chain_to[(q - 2, 3, 1)])
            vector.tensor_copy(out=msk2[:, (q % 2) * NJT * 512:(q % 2) * NJT * 512 + 512],
                               in_=acts[:, 0:512])
        bl.vector(sec_dec)
        dec_end[q] = ctx.S("g_ve").n

        if q + 2 < NCH:
            def sec_xc(sync, q=q, te_lmm=te_lmm):
                sync.wait_ge(TE.h, te_lmm)
                LXC.issue(sync.dma_start(
                    out=xchunk[:, (q % 2) * NKT * 512:(q % 2 + 1) * NKT * 512].rearrange(
                        "p (k t) -> p k t", k=NKT),
                    in_=xn1Cv_dram[:, :, (q + 2) * 512:(q + 3) * 512]))
            bl.sync(sec_xc)

        if q >= 1:
            def sec_x2pload(sync, q=q):
                for sub in range(4):
                    i2 = (q - 1) * 4 + sub
                    if i2 >= 2:
                        sync.wait_ge(OV.h, i2 - 1)
                    s_tok = (q - 1) * 512 + sub * 128
                    LX2.issue(sync.dma_start(
                        out=x2p[:, (i2 % 2) * D:(i2 % 2 + 1) * D],
                        in_=x2p_dram[s_tok:s_tok + 128, :]))
            bl.sync(sec_x2pload)

        for r, (ct, pt) in enumerate(TREE_SMETA):
            def sec_tmm(tensor, q=q, r=r, ct=ct, pt=pt):
                tensor.wait_ge(VE.h, uu_cnt[(q, pt)])
                if r >= 2:
                    tensor.wait_ge(VE.h, tev_cnt[(q, r - 2)])
                elif q >= 1:
                    tensor.wait_ge(VE.h, tev_cnt[(q - 1, 16 + r)])
                tensor.matmul(pS[:, (r % 2) * 512:(r % 2 + 1) * 512],
                              S0sb[:, r * 128:(r + 1) * 128],
                              u0[:, pt * 512:(pt + 1) * 512],
                              start=True, stop=False)
                TE.inc(tensor.matmul(pS[:, (r % 2) * 512:(r % 2 + 1) * 512],
                                     S1sb[:, r * 128:(r + 1) * 128],
                                     u1[:, pt * 512:(pt + 1) * 512],
                                     start=False, stop=True))
                tmm_cnt[(q, r)] = ctx.S("g_te").n
            bl.tensor(sec_tmm)

            if q >= 1 and 1 <= r <= 16:
                emit_omm_chain(q - 1, (r - 1) // 2, part=(r - 1) % 2)

            if q >= 1 and r % 4 == 0 and r >= 4:
                subo = r // 4 - 1
                i2o = (q - 1) * 4 + subo

                def sec_oev1(vector, q=q, subo=subo, i2o=i2o):
                    vector = SS(vector, OV)
                    vector.wait_ge(TO.h, chain_to[(q - 1, subo, 1)])
                    LX2.wait_done(vector, i2o)
                    SOUT.wait_done(vector, i2o - 1)
                    vector.tensor_add(out=xout[:, 0:D],
                                      in0=pO[:, (i2o % 2) * 1024:(i2o % 2 + 1) * 1024],
                                      in1=x2p[:, (i2o % 2) * D:(i2o % 2 + 1) * D])
                bl.vector(sec_oev1)

            def sec_tev(vector, q=q, r=r, ct=ct, se0=se0):
                vector = SS(vector, VE)
                vector.wait_ge(TE.h, tmm_cnt[(q, r)])
                ps = pS[:, (r % 2) * 512:(r % 2 + 1) * 512]
                moff = (q % 2) * NJT * 512
                if ct < 11:
                    vector.tensor_mul(out=u1[:, ct * 512:(ct + 1) * 512],
                                      in0=ps, in1=dec[:, ct * 512:(ct + 1) * 512])
                    vector.tensor_sub(out=u0[:, ct * 512:(ct + 1) * 512],
                                      in0=ps, in1=u1[:, ct * 512:(ct + 1) * 512])
                    uu_cnt[(q, ct)] = ctx.S("g_ve").n
                vector.wait_ge(SE.h, se0 + ct + 1)
                vector.tensor_mul(out=msk2[:, moff + ct * 512:moff + (ct + 1) * 512],
                                  in0=acts[:, ct * 512:(ct + 1) * 512], in1=ps)
                tev_cnt[(q, r)] = ctx.S("g_ve").n
            bl.vector(sec_tev)
        tree_end[q] = ctx.S("g_ve").n

        if q >= 1:
            def sec_ostore(sync, q=q):
                for sub in range(4):
                    i2 = (q - 1) * 4 + sub
                    sync.wait_ge(OV.h, i2 + 1)
                    s_tok = (q - 1) * 512 + sub * 128
                    SOUT.issue(sync.dma_start(
                        out=out_dram[s_tok:s_tok + 128, :],
                        in_=xout[:, 0:D]))
            bl.gpsimd(sec_ostore)

    qt = NCH - 1

    def sec_x2pload_t(sync):
        for sub in range(4):
            i2 = qt * 4 + sub
            sync.wait_ge(OV.h, i2 - 1)
            s_tok = qt * 512 + sub * 128
            LX2.issue(sync.dma_start(out=x2p[:, (i2 % 2) * D:(i2 % 2 + 1) * D],
                                     in_=x2p_dram[s_tok:s_tok + 128, :]))
    bl.sync(sec_x2pload_t)
    for c in range(8):
        emit_omm_chain(qt, c)

    def sec_oev_t(vector):
        vector = SS(vector, OV)
        for sub in range(4):
            i2 = qt * 4 + sub
            vector.wait_ge(TO.h, chain_to[(qt, sub, 1)])
            LX2.wait_done(vector, i2)
            SOUT.wait_done(vector, i2 - 1)
            vector.tensor_add(out=xout[:, 0:D],
                              in0=pO[:, (i2 % 2) * 1024:(i2 % 2 + 1) * 1024],
                              in1=x2p[:, (i2 % 2) * D:(i2 % 2 + 1) * D])
    bl.vector(sec_oev_t)

    def sec_ostore_t(sync):
        for sub in range(4):
            i2 = qt * 4 + sub
            sync.wait_ge(OV.h, i2 + 1)
            s_tok = qt * 512 + sub * 128
            SOUT.issue(sync.dma_start(out=out_dram[s_tok:s_tok + 128, :],
                                      in_=xout[:, 0:D]))
    bl.gpsimd(sec_ostore_t)

    def sec_final(gpsimd):
        SOUT.wait_all(gpsimd)
    bl.gpsimd(sec_final)



PHASE_INPUTS = {
    "A": [], "B": ["Wn"], "C": [], "D": ["w1T", "w2T"],
    "E": [], "F": ["WvT", "WoT"], "G": ["WinT", "WoutT"],
}
TOP_CONSTS = {
    "A": ["ident", "eps_col"], "B": ["ident", "b_col"], "C": ["ident"],
    "D": ["b1_col"], "E": ["ident", "eps_col"],
    "F": ["fr_col"], "G": [],
}
HBM_T = {
    "xnC_h": ([128, NKT * T], BF16, "A", ["B", "D"]),
    "x1p_h": ([T, D], F32, "C", ["D"]),
    "x1_h": ([T, D], F32, "D", ["E", "F"]),
    "xn1C_h": ([128, NKT * T], BF16, "E", ["F", "G"]),
    "x2p_h": ([T, D], F32, "F", ["G"]),
}
PHASE_ORDER = ["A", "B", "C", "D", "E", "F", "G"]


def build_group(phases):
    nc = bass.Bass("TRN2", target_bir_lowering=False)
    in_names, out_names = [], []
    consts = set()
    dins = set()
    for ph in phases:
        consts.update(TOP_CONSTS[ph])
        dins.update(PHASE_INPUTS[ph])
    if "C" in phases:
        dins.update(["band", "b2_row"])
    if "F" in phases:
        dins.update(["iota128_row", "iota32_row"])
    if "G" in phases:
        dins.update(["bin_col", "S0all", "S1all"])
    if "C" in phases:
        assert "B" in phases
    if "F" in phases and "E" not in phases:
        pass

    es = contextlib.ExitStack()
    with es:
        dram = {}
        for nm in sorted(dins | consts):
            shape, dt = CONST_SPECS[nm]
            dram["d_" + nm] = nc.dram_tensor(nm, shape, dt, kind="ExternalInput").ap()
            in_names.append(nm)
        if "A" in phases or "C" in phases:
            dram["d_x"] = nc.dram_tensor("x", [T, D], F32, kind="ExternalInput").ap()
            in_names.append("x")
        pset = set(phases)
        for nm, (shape, dt, prod, cons) in HBM_T.items():
            produced = prod in pset
            consumed_here = bool(pset & set(cons))
            consumed_later = bool((set(cons) - pset))
            if not produced and not consumed_here:
                continue
            if produced and consumed_later:
                kind = "ExternalOutput"
                out_names.append(nm)
            elif produced:
                kind = "Internal"
            else:
                kind = "ExternalInput"
                in_names.append(nm)
            dram["d_" + nm] = nc.dram_tensor(nm, shape, dt, kind=kind).ap()
        if "D" in phases:
            dram["d_a1s"] = nc.dram_tensor("a1s", [T], F32,
                                           kind="Internal").ap()
        if "F" in phases:
            dram["d_rC_h"] = nc.dram_tensor("rC_h", [128, NKT * T], BF16,
                                            kind="Internal").ap()
        if "G" in phases:
            dram["d_out"] = nc.dram_tensor("out", [T, D], F32,
                                           kind="ExternalOutput").ap()
            out_names.append("out")

        block = es.enter_context(nc.Block())
        ctx = Ctx(nc, es, block)
        ctx.t.update(dram)
        emit_preload(ctx, sorted(consts))

        if "A" in phases:
            with contextlib.ExitStack() as sA:
                emit_ln(ctx, "a", ctx.t["d_x"], "xnC", ctx.t["d_xnC_h"], sA,
                        dst_es=sA)
            barrier(ctx)
        with contextlib.ExitStack() as s_bc:
            if "B" in phases:
                with contextlib.ExitStack() as sB:
                    emit_fproj(ctx, ctx.t["d_xnC_h"], sB, dst_es=s_bc)
                barrier(ctx)
            if "C" in phases:
                with contextlib.ExitStack() as sC:
                    emit_preload(ctx, ["band", "b2_row"], es=sC)
                    emit_attn0(ctx, ctx.t["d_x"], ctx.t["d_x1p_h"], sC)
                barrier(ctx)
        if "D" in phases:
            with contextlib.ExitStack() as sD:
                emit_ffn0(ctx, ctx.t["d_xnC_h"], ctx.t["d_x1p_h"],
                          ctx.t["d_x1_h"], sD)
            barrier(ctx)
        with contextlib.ExitStack() as s_ef:
            if "E" in phases:
                if "F" in phases:
                    emit_preload(ctx, ["iota128_row", "iota32_row"], es=s_ef)
                with contextlib.ExitStack() as sE:
                    emit_ln(ctx, "e", ctx.t["d_x1_h"], "xn1C",
                            ctx.t["d_xn1C_h"], sE, dst_es=s_ef)
                    if "F" in phases:
                        emit_rotor_prelude(ctx, s_ef)
                barrier(ctx)
            if "F" in phases:
                with contextlib.ExitStack() as sF:
                    if "xn1C" not in ctx.t:
                        t = ctx.sb("xn1C", [128, NKT * T], BF16, s_ef)
                        LD = ctx.S("ld_xn1c")

                        def sec(sync):
                            LD.dinc(sync.dma_start(out=t[:], in_=ctx.t["d_xn1C_h"]))
                        block.sync(sec)
                        tgt = LD.n

                        def secw(eng):
                            eng.wait_ge(LD.h, tgt)
                        block.tensor(secw)
                    if "E" not in phases:
                        emit_preload(ctx, ["iota128_row", "iota32_row"], es=sF)
                        emit_rotor_prelude(ctx, sF)
                    emit_rotor(ctx, ctx.t["d_x1_h"], ctx.t["d_x2p_h"],
                               ctx.t["d_rC_h"], sF)
                barrier(ctx)
        if "G" in phases:
            with contextlib.ExitStack() as sG:
                emit_preload(ctx, ["bin_col", "S0all", "S1all"], es=sG)
                emit_ffn1(ctx, ctx.t["d_xn1C_h"], ctx.t["d_x2p_h"],
                          ctx.t["d_out"], sG)
            barrier(ctx)
        if "G" not in phases:
            def sec_fin(gpsimd):
                if "spill" in ctx.sem:
                    gpsimd.wait_ge(ctx.sem["spill"].h, ctx.sem["spill"].n)
                for st_ in ctx.store_streams:
                    st_.wait_all(gpsimd)
            block.gpsimd(sec_fin)
    return nc, in_names, out_names


GROUPS_UNFUSED = [["A", "B", "C"], ["D"], ["E"], ["F"], ["G"]]
GROUPS_FUSED = [["A", "B", "C", "D", "E", "F", "G"]]
GROUPS = GROUPS_FUSED

_BUILD_CACHE = {}


def kernel(**inputs):
    common, per_core = host_prep(inputs)
    carry = [dict(pc) for pc in per_core]
    result = None
    for phases in GROUPS:
        key = tuple(phases)
        if key not in _BUILD_CACHE:
            _BUILD_CACHE[key] = build_group(phases)
        nc, in_names, out_names = _BUILD_CACHE[key]
        in_maps = []
        for ci in range(8):
            m = {}
            for nm in in_names:
                m[nm] = common[nm] if nm in common else carry[ci][nm]
            in_maps.append(m)
        res = run_bass_kernel_spmd(nc, in_maps, core_ids=list(range(8)))
        for ci in range(8):
            for nm in out_names:
                carry[ci][nm] = np.asarray(res.results[ci][nm])
        if "G" in phases:
            result = np.stack([carry[ci]["out"].astype(np.float32)
                               for ci in range(8)], axis=0)
    return result


# version 31
# speedup vs baseline: 1.0208x; 1.0022x over previous
import contextlib
import math

import numpy as np
import ml_dtypes

import concourse.bass as bass
import concourse.mybir as mybir
from concourse.bass_utils import run_bass_kernel_spmd

F32 = mybir.dt.float32
BF16 = mybir.dt.bfloat16
I32 = mybir.dt.int32
AF = mybir.ActivationFunctionType
ALU = mybir.AluOpType
AX = mybir.AxisListType

D = 1024
T = 4096
NKT = 8
NTT = 32
NCH = 8

TREE_DEPTH = 7
TREE_PAR = 8
N_NODES = 255
TREE_TILE_BASE = [0, 1, 2, 3, 4, 5, 7, 11]
NJT = 19
JW = NJT * 128

TWO_PI = 6.283185307179586
C1 = float(np.float32(6.28125))
C2 = float(np.float32(TWO_PI - 6.28125))
C3 = float(np.float32(TWO_PI - 6.28125 - float(np.float32(TWO_PI - 6.28125))))
INV2PI = float(np.float32(1.0 / TWO_PI))
HALF_PI = float(np.float32(math.pi / 2))
PI_F = float(np.float32(math.pi))
TWOPI_F = float(np.float32(TWO_PI))
MAGIC = 12582912.0
LN_EPS = 1e-5

BF = ml_dtypes.bfloat16



def _tree_perm():
    slot_src = np.full(JW, -1, dtype=np.int64)
    for p in range(TREE_PAR):
        for d in range(TREE_DEPTH + 1):
            base = 2 ** d - 1
            for k in range(2 ** d):
                j = p * N_NODES + base + k
                idx = p * (2 ** d) + k
                slot_src[128 * TREE_TILE_BASE[d] + idx] = j
    return slot_src


def _tree_scatter():
    smeta = []
    mats0, mats1 = [], []
    for d in range(TREE_DEPTH):
        n_par = TREE_PAR * 2 ** d
        n_child = 2 * n_par
        pt0, ct0 = TREE_TILE_BASE[d], TREE_TILE_BASE[d + 1]
        n_ct = (n_child + 127) // 128
        for ci in range(n_ct):
            S0 = np.zeros((128, 128), dtype=np.float32)
            S1 = np.zeros((128, 128), dtype=np.float32)
            for cl in range(min(128, n_child - ci * 128)):
                cg = ci * 128 + cl
                q = cg // 2
                b = cg % 2
                (S1 if b else S0)[q % 128, cl] = 1.0
            smeta.append((ct0 + ci, pt0 + (ci * 64) // 128))
            mats0.append(S0)
            mats1.append(S1)
    return smeta, (np.concatenate(mats0, 1).astype(BF),
                   np.concatenate(mats1, 1).astype(BF))


TREE_SMETA, (_S0ALL, _S1ALL) = _tree_scatter()
N_SC = len(TREE_SMETA)


def _band_patch():
    Cm = np.zeros((128, 146), dtype=np.float32)
    for i in range(128):
        for w in range(146):
            delta = i - (w - 9)
            Cm[i, w] = -0.01 * (10.0 - min(abs(delta), 10.0))
    return Cm


def host_prep(inp):
    f32 = np.float32
    cont = np.ascontiguousarray
    c = {}
    _ = inp["L1_attn_Wk"]
    c["Wn"] = cont(inp["L0_attn_W"].astype(f32))
    c["b_col"] = cont(inp["L0_attn_b"].astype(f32).reshape(8, 128).T)
    c["w1T"] = cont(inp["L0_ffn_w1"].astype(f32).T)
    c["b1_col"] = cont(inp["L0_ffn_b1"].astype(f32).reshape(32, 128).T)
    c["w2T"] = cont(inp["L0_ffn_w2"].astype(f32).T)
    c["b2_row"] = cont(inp["L0_ffn_b2"].astype(f32).reshape(1, 1024))
    c["WvT"] = cont(inp["L1_attn_Wv"].astype(f32).T)
    c["WoT"] = cont(inp["L1_attn_Wo"].astype(f32).T)
    c["fr_col"] = cont(inp["L1_attn_freqs"].astype(f32).reshape(8, 128).T)
    slot = _tree_perm()
    used = slot >= 0
    WinP = np.zeros((JW, D), dtype=f32)
    binP = np.zeros(JW, dtype=f32)
    WoutP = np.zeros((D, JW), dtype=f32)
    WinP[used] = inp["L1_ffn_W_in"].astype(f32)[slot[used]]
    binP[used] = inp["L1_ffn_b_in"].astype(f32)[slot[used]]
    WoutP[:, used] = inp["L1_ffn_W_out"].astype(f32)[:, slot[used]]
    c["WinT"] = cont(WinP.T)
    c["bin_col"] = cont(binP.reshape(NJT, 128).T)
    c["WoutT"] = cont(WoutP.T)
    c["ident"] = np.eye(128, dtype=f32).astype(BF)
    c["eps_col"] = np.full((128, 1), 1e-5, dtype=f32)
    c["band"] = _band_patch()
    c["iota128_row"] = np.arange(128, dtype=f32).reshape(1, 128)
    c["iota32_row"] = np.arange(32, dtype=f32).reshape(1, 32)
    t_idx = np.arange(T)
    c["brow"] = (t_idx // 128).astype(BF).reshape(1, T)
    c["urow"] = (t_idx % 128).astype(BF).reshape(1, T)
    c["S0all"] = _S0ALL
    c["S1all"] = _S1ALL
    per_core = [{"x": cont(inp["L0_attn_x"].astype(f32)[i])} for i in range(8)]
    return c, per_core


CONST_SPECS = {
    "Wn": ([1024, 1024], F32), "b_col": ([128, 8], F32),
    "w1T": ([1024, 4096], F32), "b1_col": ([128, 32], F32),
    "w2T": ([4096, 1024], F32), "b2_row": ([1, 1024], F32),
    "WvT": ([1024, 1024], F32), "WoT": ([1024, 1024], F32),
    "fr_col": ([128, 8], F32), "WinT": ([1024, JW], F32),
    "bin_col": ([128, NJT], F32), "WoutT": ([JW, 1024], F32),
    "ident": ([128, 128], BF16), "band": ([128, 146], F32),
    "eps_col": ([128, 1], F32),
    "iota128_row": ([1, 128], F32), "iota32_row": ([1, 32], F32),
    "brow": ([1, T], BF16), "urow": ([1, T], BF16),
    "S0all": ([128, N_SC * 128], BF16), "S1all": ([128, N_SC * 128], BF16),
}



class Sem:
    __slots__ = ("h", "n")

    def __init__(self, nc, es, name):
        self.h = es.enter_context(nc.semaphore(name))
        self.n = 0

    def inc(self, inst, k=1):
        inst.then_inc(self.h, k)
        self.n += k
        return self.n

    def dinc(self, inst):
        return self.inc(inst, 16)


class DStream:

    def __init__(self, ctx, name, nbuf=2):
        self.sems = [ctx.S(name + str(j)) for j in range(nbuf)]
        self.nbuf = nbuf
        self.count = 0

    def issue(self, inst):
        i = self.count
        self.sems[i % self.nbuf].inc(inst, 16)
        self.count += 1
        return i

    def wait_done(self, eng, i):
        if i < 0:
            return
        eng.wait_ge(self.sems[i % self.nbuf].h, 16 * (i // self.nbuf + 1))

    def wait_all(self, eng):
        for s in self.sems:
            if s.n:
                eng.wait_ge(s.h, s.n)


_SS_WRAP = {
    "tensor_copy", "tensor_tensor", "tensor_add", "tensor_sub", "tensor_mul",
    "tensor_max", "tensor_scalar", "tensor_single_scalar", "tensor_scalar_add",
    "tensor_scalar_sub", "tensor_scalar_mul", "tensor_scalar_max",
    "tensor_scalar_min", "tensor_relu", "tensor_reduce", "reduce_max",
    "reduce_sum", "scalar_tensor_tensor", "tensor_tensor_scan", "memset",
    "memzero", "reciprocal", "select", "transpose", "activation", "copy",
    "sign", "square", "sqrt", "mul", "add", "engine_nop", "bn_stats",
    "bn_aggr", "stream_shuffle", "cody_waite_cascade", "add_range_wrap",
    "affine_then_add", "affine_mul_reduce", "tensor_tensor_reduce",
}


class SS:

    def __init__(self, raw, sem):
        self._r = raw
        self._s = sem

    def __getattr__(self, name):
        f = getattr(self._r, name)
        if name in _SS_WRAP:
            def w(*a, **k):
                if self._s.n:
                    self._r.wait_ge(self._s.h, self._s.n)
                ins = f(*a, **k)
                self._s.inc(ins)
                return ins
            return w
        return f


class Chain:

    def __init__(self, raw):
        self.raw = raw
        self.prev = None

    def wait_ge(self, h, n):
        self.raw.wait_ge(h, n)

    def do(self, sem, f):
        if self.prev is not None:
            ps, pn = self.prev
            self.raw.wait_ge(ps.h, pn)
        ins = f(self.raw)
        sem.inc(ins)
        self.prev = (sem, sem.n)
        return ins


class Ctx:
    def __init__(self, nc, es, block):
        self.nc = nc
        self.es = es
        self.block = block
        self.t = {}
        self.sem = {}
        self.store_streams = []

    def S(self, name):
        if name not in self.sem:
            self.sem[name] = Sem(self.nc, self.es, name)
        return self.sem[name]

    def sb(self, name, shape, dtype, es=None, side="left"):
        h = (es or self.es).enter_context(
            self.nc.sbuf_tensor(name, shape, dtype, side=side))
        self.t[name] = h
        return h

    def ps(self, name, shape, dtype, es=None):
        h = (es or self.es).enter_context(self.nc.psum_tensor(name, shape, dtype))
        self.t[name] = h
        return h


def barrier(ctx):
    bl = ctx.block
    sems = [ctx.S("bar_" + e) for e in ("sy", "te", "ve", "se", "ge")]
    all_waits = [(s.h, s.n) for nm, s in ctx.sem.items()
                 if s.n and not nm.startswith("bar_")]

    def mk(idx):
        def f(eng):
            for h, n in all_waits:
                eng.wait_ge(h, n)
            sems[idx].inc(eng.nop())
        return f
    bl.sync(mk(0)); bl.tensor(mk(1)); bl.vector(mk(2)); bl.scalar(mk(3)); bl.gpsimd(mk(4))
    targets = [s.n for s in sems]

    def mkw(my_idx):
        def f(eng):
            for i, s in enumerate(sems):
                if i != my_idx:
                    eng.wait_ge(s.h, targets[i])
        return f
    bl.sync(mkw(0)); bl.tensor(mkw(1)); bl.vector(mkw(2)); bl.scalar(mkw(3)); bl.gpsimd(mkw(4))


def emit_preload(ctx, names, es=None):
    bl = ctx.block
    LDC = ctx.S("ldc")
    table = {
        "ident": ("ident_sb", [128, 128], BF16, False),
        "eps_col": ("eps_col_sb", [128, 1], F32, False),
        "band": ("band_sb", [128, 146], F32, False),
        "b_col": ("b_col_sb", [128, 8], F32, False),
        "b1_col": ("b1_col_sb", [128, 32], F32, False),
        "bin_col": ("bin_col_sb", [128, NJT], F32, False),
        "fr_col": ("fr_col_sb", [128, 8], F32, False),
        "S0all": ("S0_sb", [128, N_SC * 128], BF16, False),
        "S1all": ("S1_sb", [128, N_SC * 128], BF16, False),
        "iota128_row": ("iota128_sb", [128, 128], F32, True),
        "iota32_row": ("iota32_sb", [128, 32], F32, True),
        "b2_row": ("b2bc_sb", [128, 1024], F32, True),
        "brow": ("brow_sb", [128, T], BF16, True),
        "urow": ("urow_sb", [128, T], BF16, True),
    }
    plans = []
    for nm in names:
        sbn, shape, dt, bc = table[nm]
        if sbn not in ctx.t:
            ctx.sb(sbn, shape, dt, es)
        plans.append((sbn, nm, bc))

    def sec(sync):
        for sbn, nm, bc in plans:
            src = ctx.t["d_" + nm]
            if bc:
                ins = sync.dma_start(out=ctx.t[sbn][:],
                                     in_=src[0:1, :].partition_broadcast(128))
            else:
                ins = sync.dma_start(out=ctx.t[sbn][:], in_=src)
            LDC.dinc(ins)
    bl.sync(sec)




def emit_ln(ctx, tag, src_dram, dst_name, spill_dram, es, dst_es=None,
            dst_side="left"):
    bl = ctx.block
    DEP = 4
    dst = ctx.sb(dst_name, [128, NKT * T], BF16, dst_es or es, side=dst_side)
    xb = ctx.sb(tag + "_x", [128, DEP * D], F32, es)
    xnT = ctx.sb(tag + "_xnT", [128, DEP * D], BF16, es)
    st = ctx.sb(tag + "_st", [128, 16 * DEP], F32, es)
    pT = ctx.ps(tag + "_pT", [128, DEP * 1024], BF16, es)
    ident = ctx.t["ident_sb"]
    ldc_target = ctx.S("ldc").n

    LD = [DStream(ctx, tag + "ldx"), DStream(ctx, tag + "ldy")]
    VE = ctx.S(tag + "ve"); SE = ctx.S(tag + "se"); TEs = ctx.S(tag + "te")
    SP = ctx.S("spill")
    xnc = {}; agc = {}; rsc = {}; cpc = {}; tec = {}; stc = {}

    dstv = dst[:].rearrange("p (k t) -> p k t", k=NKT)

    for i in range(NTT + 2):
        s = i % DEP

        if i < NTT:
            def sec_load(eng, i=i, s=s):
                if i >= DEP:
                    eng.wait_ge(VE.h, xnc[i - DEP])
                LD[i % 2].issue(eng.dma_start(
                    out=xb[:, s * D:(s + 1) * D],
                    in_=src_dram[i * 128:(i + 1) * 128, :]))
            (bl.sync if i % 2 == 0 else bl.gpsimd)(sec_load)

        def sec_ve(vector, i=i):
            vector = SS(vector, VE)
            if i < NTT:
                s = i % DEP
                c = st[:, s * 16:s * 16 + 16]
                LD[i % 2].wait_done(vector, i // 2)
                if i >= DEP:
                    vector.wait_ge(SE.h, rsc[i - DEP])
                vector.bn_stats(out=c[:, 0:6], in_=xb[:, s * D:s * D + 512])
                vector.bn_stats(out=c[:, 6:12], in_=xb[:, s * D + 512:s * D + 1024])
                vector.bn_aggr(out=c[:, 12:14], in_=c[:, 0:12])
                agc[i] = VE.n
            if 1 <= i <= NTT:
                j = i - 1
                sj = j % DEP
                c = st[:, sj * 16:sj * 16 + 16]
                vector.wait_ge(SE.h, rsc[j])
                vector.reciprocal(out=c[:, 15:16], in_=c[:, 14:15])
                vector.tensor_scalar(out=xnT[:, sj * D:(sj + 1) * D],
                                     in0=xb[:, sj * D:(sj + 1) * D],
                                     scalar1=c[:, 12:13], scalar2=c[:, 15:16],
                                     op0=ALU.subtract, op1=ALU.mult)
                xnc[j] = VE.n
            if i >= 2 and (i - 2) % 2 == 1:
                j = i - 2
                sj = j % DEP
                vector.wait_ge(TEs.h, tec[j])
                vector.tensor_copy(out=dstv[:, :, j * 128:(j + 1) * 128],
                                   in_=pT[:, sj * 1024:(sj + 1) * 1024].rearrange(
                                       "p (k t) -> p k t", k=NKT))
                cpc[j] = ("ve", VE.n)
        bl.vector(sec_ve)

        if i >= 1 and i - 1 < NTT:
            def sec_pe(tensor, i=i):
                j = i - 1
                sj = j % DEP
                if j == 0:
                    tensor.wait_ge(ctx.S("ldc").h, ldc_target)
                tensor.wait_ge(VE.h, xnc[j])
                if j >= DEP:
                    eng, n = cpc[j - DEP]
                    tensor.wait_ge(VE.h if eng == "ve" else SE.h, n)
                ins = None
                for k in range(NKT):
                    ins = tensor.transpose(
                        pT[:, sj * 1024 + k * 128:sj * 1024 + (k + 1) * 128],
                        xnT[:, sj * D + k * 128:sj * D + (k + 1) * 128],
                        ident[:])
                TEs.inc(ins)
                tec[j] = TEs.n
            bl.tensor(sec_pe)

        def sec_se(scalar, i=i):
            scalar = SS(scalar, SE)
            if i < NTT:
                s = i % DEP
                if i == 0:
                    scalar.wait_ge(ctx.S("ldc").h, ldc_target)
                scalar.wait_ge(VE.h, agc[i])
                scalar.activation(out=st[:, s * 16 + 14:s * 16 + 15],
                                  in_=st[:, s * 16 + 13:s * 16 + 14],
                                  func=AF.Sqrt, bias=ctx.t["eps_col_sb"][:, 0:1])
                rsc[i] = SE.n
            if i >= 2 and (i - 2) % 2 == 0:
                j = i - 2
                sj = j % DEP
                scalar.wait_ge(TEs.h, tec[j])
                scalar.copy(out=dstv[:, :, j * 128:(j + 1) * 128],
                            in_=pT[:, sj * 1024:(sj + 1) * 1024].rearrange(
                                "p (k t) -> p k t", k=NKT))
                cpc[j] = ("se", SE.n)
        bl.scalar(sec_se)

    if spill_dram is not None:
        spillv = spill_dram.rearrange("p (k t) -> p k t", k=NKT)
        chunk_counts = []

        def sec_spill(sync):
            for q in range(NCH):
                for j in (4 * q + 2, 4 * q + 3):
                    eng, n = cpc[j]
                    sync.wait_ge(VE.h if eng == "ve" else SE.h, n)
                SP.dinc(sync.dma_start(out=spillv[:, :, q * 512:(q + 1) * 512],
                                       in_=dstv[:, :, q * 512:(q + 1) * 512]))
                chunk_counts.append(SP.n)
        bl.sync(sec_spill)
        ctx.spill_chunks = chunk_counts
    return dst




def emit_fproj(ctx, xnC_dram, es, dst_es, psum=None):
    bl = ctx.block
    fC = ctx.sb("fC", [128, NKT * T], BF16, dst_es, side="right")
    fT = ctx.sb("fT", [128, NTT * 1024], BF16, dst_es, side="right")
    Wb = ctx.sb("Wb", [128, NKT * 1024], BF16, es, side="right")
    xchunk = ctx.sb("b_xc", [128, 2 * NKT * 512], BF16, es, side="right")
    if psum is None:
        pf = ctx.ps("b_pf", [128, 2 * 512], F32, es)
        pT = ctx.ps("b_pT", [128, 2 * 1024], BF16, es)
    else:
        pf, pT = psum
    ident = ctx.t["ident_sb"]
    b_col = ctx.t["b_col_sb"]
    dWn = ctx.t["d_Wn"]
    xnCv_dram = xnC_dram.rearrange("p (k t) -> p k t", k=NKT)
    ldc_target = ctx.S("ldc").n

    LG = ctx.S("b_lg"); LXC = DStream(ctx, "b_lxc"); TE = ctx.S("b_te"); SE = ctx.S("b_se")
    T2 = ctx.S("b_t2"); V2 = ctx.S("b_v2")

    def sec_wload(gpsimd):
        for k in range(NKT):
            LG.dinc(gpsimd.dma_start(out=Wb[:, k * 1024:(k + 1) * 1024],
                                     in_=dWn[k * 128:(k + 1) * 128, :]))
    bl.gpsimd(sec_wload)

    def sec_xcload(sync):
        for q in range(NCH):
            sync.wait_ge(ctx.S("spill").h, ctx.spill_chunks[q])
            if q >= 2:
                sync.wait_ge(TE.h, (q - 1) * NKT)
            LXC.issue(sync.dma_start(
                out=xchunk[:, (q % 2) * NKT * 512:(q % 2 + 1) * NKT * 512].rearrange(
                    "p (k t) -> p k t", k=NKT),
                in_=xnCv_dram[:, :, q * 512:(q + 1) * 512]))
    bl.sync(sec_xcload)

    def sec_mm(tensor):
        tensor.wait_ge(LG.h, 16 * NKT)
        tensor.wait_ge(ctx.S("ldc").h, ldc_target)
        for q in range(NCH):
            LXC.wait_done(tensor, q)
            for j in range(NKT):
                r = q * NKT + j
                if r >= 2:
                    tensor.wait_ge(SE.h, r - 1)
                ins = None
                for k in range(NKT):
                    ins = tensor.matmul(pf[:, (r % 2) * 512:(r % 2 + 1) * 512],
                                        Wb[:, k * 1024 + j * 128:k * 1024 + (j + 1) * 128],
                                        xchunk[:, (q % 2) * NKT * 512 + k * 512:(q % 2) * NKT * 512 + (k + 1) * 512],
                                        start=(k == 0), stop=(k == NKT - 1))
                TE.inc(ins)
        tensor.wait_ge(SE.h, NKT * NCH)
        for i in range(NTT):
            if i >= 2:
                tensor.wait_ge(V2.h, i - 1)
            ins = None
            for k in range(NKT):
                ins = tensor.transpose(pT[:, (i % 2) * 1024 + k * 128:(i % 2) * 1024 + (k + 1) * 128],
                                       fC[:, k * T + i * 128:k * T + (i + 1) * 128],
                                       ident[:])
            T2.inc(ins)
    bl.tensor(sec_mm)

    def sec_evac(scalar):
        scalar = SS(scalar, SE)
        for q in range(NCH):
            for j in range(NKT):
                r = q * NKT + j
                scalar.wait_ge(TE.h, r + 1)
                scalar.activation(out=fC[:, j * T + q * 512:j * T + (q + 1) * 512],
                                  in_=pf[:, (r % 2) * 512:(r % 2 + 1) * 512],
                                  func=AF.Identity, bias=b_col[:, j:j + 1])
    bl.scalar(sec_evac)

    def sec_trcopy(vector):
        vector = SS(vector, V2)
        for i in range(NTT):
            vector.wait_ge(T2.h, i + 1)
            vector.tensor_copy(out=fT[:, i * 1024:(i + 1) * 1024],
                               in_=pT[:, (i % 2) * 1024:(i % 2 + 1) * 1024])
    bl.vector(sec_trcopy)
    return fC, fT




def emit_attn0(ctx, x_dram, x1p_dram, es):
    bl = ctx.block
    fC, fT = ctx.t["fC"], ctx.t["fT"]
    e_sb = ctx.sb("e_sb", [128, 2 * T], F32, es)
    a_sb = ctx.sb("a_sb", [128, 2 * T], BF16, es)
    aT_sb = ctx.sb("aT_sb", [128, T], BF16, es)
    xr = ctx.sb("c_xr", [128, 2 * D], F32, es)
    x1p = ctx.sb("c_x1p", [128, 2 * D], F32, es)
    cst = ctx.sb("c_st", [128, 4], F32, es)
    inv = ctx.sb("c_inv", [128, 2], F32, es)
    pe_ = ctx.ps("c_pe", [128, 2 * 512], F32, es)
    pA = ctx.ps("c_pA", [128, 2 * 1024], BF16, es)
    pO = ctx.ps("c_pO", [128, 2 * 1024], F32, es)
    band = ctx.t["band_sb"]
    b2bc = ctx.t["b2bc_sb"]
    ident = ctx.t["ident_sb"]

    TEs = ctx.S("c_te"); EVa = ctx.S("c_eva"); EVb = ctx.S("c_evb"); GE = ctx.S("c_ge")
    SEx = ctx.S("c_se"); VEi = ctx.S("c_vi"); TT_ = ctx.S("c_tt")
    TEV = ctx.S("c_tev"); TO = ctx.S("c_to"); OV = ctx.S("c_ov")
    VEm = ctx.S("c_vm")
    ldc_c = ctx.S("ldc").n
    LXR = DStream(ctx, "c_lxr"); SP1 = DStream(ctx, "c_sp1")
    ctx.store_streams.append(SP1)

    def sec_pe(tensor):
        for s in range(NTT + 1):
            if s < NTT:
                for q in range(NCH):
                    r = s * NCH + q
                    if r >= 2:
                        if r % 2 == 0:
                            tensor.wait_ge(EVa.h, r // 2)
                        else:
                            tensor.wait_ge(EVb.h, (r - 1) // 2)
                    ins = None
                    for k in range(NKT):
                        ins = tensor.matmul(pe_[:, (r % 2) * 512:(r % 2 + 1) * 512],
                                            fC[:, k * T + s * 128:k * T + (s + 1) * 128],
                                            fC[:, k * T + q * 512:k * T + (q + 1) * 512],
                                            start=(k == 0), stop=(k == NKT - 1))
                    TEs.inc(ins)
            if s >= 1:
                sp = s - 1
                tensor.wait_ge(SEx.h, sp + 1)
                for g in range(4):
                    gi = sp * 4 + g
                    if gi >= 2:
                        tensor.wait_ge(TEV.h, gi - 1)
                    ins = None
                    for u in range(8):
                        i = g * 8 + u
                        ins = tensor.transpose(
                            pA[:, (gi % 2) * 1024 + u * 128:(gi % 2) * 1024 + (u + 1) * 128],
                            a_sb[:, (sp % 2) * T + i * 128:(sp % 2) * T + (i + 1) * 128],
                            ident[:])
                    TT_.inc(ins)
                    if g >= 1:
                        tensor.wait_ge(TEV.h, sp * 4 + g)
                        if g == 1 and sp >= 2:
                            tensor.wait_ge(OV.h, 2 * (sp - 1))
                        for q2 in range(2):
                            for i in range((g - 1) * 8, g * 8):
                                tensor.matmul(
                                    pO[:, (sp % 2) * 1024 + q2 * 512:(sp % 2) * 1024 + (q2 + 1) * 512],
                                    aT_sb[:, i * 128:(i + 1) * 128],
                                    fT[:, i * 1024 + q2 * 512:i * 1024 + (q2 + 1) * 512],
                                    start=(i == 0), stop=False)
                tensor.wait_ge(TEV.h, sp * 4 + 4)
                for q2 in range(2):
                    ins = None
                    for i in range(24, NTT):
                        ins = tensor.matmul(
                            pO[:, (sp % 2) * 1024 + q2 * 512:(sp % 2) * 1024 + (q2 + 1) * 512],
                            aT_sb[:, i * 128:(i + 1) * 128],
                            fT[:, i * 1024 + q2 * 512:i * 1024 + (q2 + 1) * 512],
                            start=False, stop=(i == NTT - 1))
                    TO.inc(ins)
    bl.tensor(sec_pe)

    def sec_pool(gpsimd):
        gpsimd = SS(gpsimd, GE)
        gpsimd.wait_ge(ctx.S("ldc").h, ldc_c)
        for s in range(NTT):
            gpsimd.wait_ge(EVa.h, (s + 1) * 4)
            gpsimd.wait_ge(EVb.h, (s + 1) * 4)
            base = (s % 2) * T
            lo = s * 128 - 9
            if s == 0:
                gpsimd.tensor_add(out=e_sb[:, base:base + 137],
                                  in0=e_sb[:, base:base + 137], in1=band[:, 9:146])
            elif s == NTT - 1:
                gpsimd.tensor_add(out=e_sb[:, base + lo:base + T],
                                  in0=e_sb[:, base + lo:base + T], in1=band[:, 0:137])
            else:
                gpsimd.tensor_add(out=e_sb[:, base + lo:base + lo + 146],
                                  in0=e_sb[:, base + lo:base + lo + 146], in1=band[:])
    bl.gpsimd(sec_pool)

    def sec_act(scalar):
        ch = Chain(scalar)

        def emit_exp(s):
            ch.wait_ge(VEm.h, s + 1)
            if s >= 2:
                ch.wait_ge(TT_.h, (s - 1) * 4)
                ch.wait_ge(VEi.h, s - 1)
            ch.do(SEx, lambda sc, s=s: sc.activation(
                out=a_sb[:, (s % 2) * T:(s % 2 + 1) * T],
                in_=e_sb[:, (s % 2) * T:(s % 2 + 1) * T],
                func=AF.Exp, bias=cst[:, (s % 2):(s % 2) + 1],
                accum_out=cst[:, 2 + (s % 2):3 + (s % 2)]))

        def emit_tevc(sp):
            for g in range(4):
                gi = sp * 4 + g
                ch.wait_ge(TT_.h, gi + 1)
                if sp >= 1 and g == 0:
                    ch.wait_ge(TO.h, 2 * sp)
                ch.do(TEV, lambda sc, g=g, gi=gi: sc.copy(
                    out=aT_sb[:, g * 1024:(g + 1) * 1024],
                    in_=pA[:, (gi % 2) * 1024:(gi % 2 + 1) * 1024]))

        for s in range(NTT):
            if s >= 1:
                emit_exp(s - 1)
            for q in range(0, NCH, 2):
                r = s * NCH + q
                ch.wait_ge(TEs.h, r + 1)
                if s >= 2 and q == 0:
                    ch.wait_ge(SEx.h, s - 1)
                ch.do(EVa, lambda sc, s=s, q=q, r=r: sc.copy(
                    out=e_sb[:, (s % 2) * T + q * 512:(s % 2) * T + (q + 1) * 512],
                    in_=pe_[:, (r % 2) * 512:(r % 2 + 1) * 512]))
            if s >= 1:
                emit_tevc(s - 1)
        emit_exp(NTT - 1)
        emit_tevc(NTT - 1)
    bl.scalar(sec_act)

    def sec_dve(vector):
        ch = Chain(vector)
        ch.wait_ge(ctx.S("ldc").h, ldc_c)

        def emit_tail(sp):
            ch.wait_ge(SEx.h, sp + 1)
            ch.do(VEi, lambda v, sp=sp: v.reciprocal(
                out=inv[:, (sp % 2):(sp % 2) + 1],
                in_=cst[:, 2 + (sp % 2):3 + (sp % 2)]))
            ch.wait_ge(TO.h, 2 * sp + 2)
            LXR.wait_done(ch, sp)
            SP1.wait_done(ch, sp - 2)
            ch.do(OV, lambda v, sp=sp: v.scalar_tensor_tensor(
                out=x1p[:, (sp % 2) * D:(sp % 2 + 1) * D],
                in0=pO[:, (sp % 2) * 1024:(sp % 2 + 1) * 1024],
                scalar=inv[:, (sp % 2):(sp % 2) + 1],
                in1=xr[:, (sp % 2) * D:(sp % 2 + 1) * D],
                op0=ALU.mult, op1=ALU.add))
            ch.do(OV, lambda v, sp=sp: v.tensor_add(
                out=x1p[:, (sp % 2) * D:(sp % 2 + 1) * D],
                in0=x1p[:, (sp % 2) * D:(sp % 2 + 1) * D],
                in1=b2bc[:]))

        for s in range(NTT):
            for q in range(1, NCH, 2):
                r = s * NCH + q
                ch.wait_ge(TEs.h, r + 1)
                if s >= 2 and q == 1:
                    ch.wait_ge(SEx.h, s - 1)
                ch.do(EVb, lambda v, s=s, q=q, r=r: v.tensor_copy(
                    out=e_sb[:, (s % 2) * T + q * 512:(s % 2) * T + (q + 1) * 512],
                    in_=pe_[:, (r % 2) * 512:(r % 2 + 1) * 512]))
            ch.wait_ge(GE.h, s + 1)
            ch.do(VEm, lambda v, s=s: v.reduce_max(
                out=cst[:, (s % 2):(s % 2) + 1],
                in_=e_sb[:, (s % 2) * T:(s % 2) * T + T], axis=AX.X, negate=True))
            if s >= 1:
                emit_tail(s - 1)
        emit_tail(NTT - 1)
    bl.vector(sec_dve)

    def sec_sync(sync):
        def store(sp):
            sync.wait_ge(OV.h, 2 * (sp + 1))
            SP1.issue(sync.dma_start(out=x1p_dram[sp * 128:(sp + 1) * 128, :],
                                     in_=x1p[:, (sp % 2) * D:(sp % 2 + 1) * D]))
        for s in range(NTT):
            if s >= 2:
                sync.wait_ge(OV.h, 2 * (s - 1))
            LXR.issue(sync.dma_start(out=xr[:, (s % 2) * D:(s % 2 + 1) * D],
                                     in_=x_dram[s * 128:(s + 1) * 128, :]))
            if s >= 2:
                store(s - 2)
        store(NTT - 2)
        store(NTT - 1)
        SP1.wait_all(sync)
    bl.sync(sec_sync)




def emit_ffn0(ctx, xnC_dram, x1p_dram, x1_dram, es):
    bl = ctx.block
    sw1T = ctx.sb("sw1T", [128, NKT * T], BF16, es)
    sw2T = ctx.sb("sw2T", [128, NTT * 1024], BF16, es)
    xchunk = ctx.sb("d_xc", [128, 2 * NKT * 512], BF16, es)
    x1p = ctx.sb("d_x1p", [128, 2 * D], F32, es)
    x1o = ctx.sb("d_x1o", [128, 2 * D], F32, es)
    a2bc = ctx.sb("d_a2bc", [128, 1024], BF16, es)
    a1col = ctx.sb("d_a1col", [128, NTT], F32, es)
    es_prep = contextlib.ExitStack()
    wraw = ctx.sb("d_wraw", [128, 4 * T], BF16, es_prep)
    a1bc = ctx.sb("d_a1bc", [128, T], BF16, es_prep)
    ones128 = ctx.sb("d_ones", [128, 128], BF16, es_prep)
    pAw = ctx.ps("d_pAw", [128, NCH * 512], F32, es_prep)
    b1_col = ctx.t["b1_col_sb"]
    dw1T, dw2T = ctx.t["d_w1T"], ctx.t["d_w2T"]
    xnCv_dram = xnC_dram.rearrange("p (k t) -> p k t", k=NKT)

    LW = ctx.S("d_lw"); SE = ctx.S("d_se"); TE = ctx.S("d_te"); VE = ctx.S("d_ve")
    LXC = ctx.S("d_lxc"); LX1 = DStream(ctx, "d_lx1"); SX1 = DStream(ctx, "d_sx1")
    ctx.store_streams.append(SX1)
    TH = ctx.S("d_th"); SH = ctx.S("d_sh"); TO = ctx.S("d_to"); OV = ctx.S("d_ov")

    def sec_ones(vector):
        vector = SS(vector, VE)
        vector.memset(ones128[:], 1.0)
    bl.vector(sec_ones)

    def sec_wload(gpsimd):
        for k in range(NKT):
            if k >= 4:
                gpsimd.wait_ge(TE.h, (k - 3) * NCH)
            LW.dinc(gpsimd.dma_start(out=wraw[:, (k % 4) * T:(k % 4 + 1) * T],
                                     in_=dw1T[k * 128:(k + 1) * 128, :]))
        for jt in range(NTT):
            r = NKT + jt
            if jt < 4:
                gpsimd.wait_ge(TE.h, NKT * NCH)
            else:
                gpsimd.wait_ge(TE.h, NKT * NCH + (jt - 3) * 2)
            LW.dinc(gpsimd.dma_start(out=wraw[:, (r % 4) * T:(r % 4) * T + 1024],
                                     in_=dw2T[jt * 128:(jt + 1) * 128, :]))
    bl.gpsimd(sec_wload)

    def sec_wact(scalar):
        scalar = SS(scalar, SE)
        for k in range(NKT):
            w = wraw[:, (k % 4) * T:(k % 4 + 1) * T]
            scalar.wait_ge(LW.h, 16 * (k + 1))
            scalar.activation(out=sw1T[:, k * T:(k + 1) * T], in_=w,
                              func=AF.Sign)
            scalar.activation(out=w, in_=w, func=AF.Abs)
        for jt in range(NTT):
            r = NKT + jt
            w = wraw[:, (r % 4) * T:(r % 4) * T + 1024]
            scalar.wait_ge(LW.h, 16 * (NKT + jt + 1))
            scalar.activation(out=sw2T[:, jt * 1024:(jt + 1) * 1024],
                              in_=w, func=AF.Sign)
            scalar.activation(out=w, in_=w, func=AF.Abs)
    bl.scalar(sec_wact)

    def sec_alpha_mm(tensor):
        tensor.wait_ge(VE.h, 1)
        for k in range(NKT):
            tensor.wait_ge(SE.h, 2 * k + 2)
            for jc in range(NCH):
                TE.inc(tensor.matmul(pAw[:, jc * 512:(jc + 1) * 512], ones128[:],
                                     wraw[:, (k % 4) * T + jc * 512:(k % 4) * T + (jc + 1) * 512],
                                     start=(k == 0), stop=(k == NKT - 1)))
        for jt in range(NTT):
            r = NKT + jt
            tensor.wait_ge(SE.h, 2 * NKT + 2 * jt + 2)
            if jt == 0:
                tensor.wait_ge(VE.h, 2)
            ins = None
            for dc in range(2):
                ins = tensor.matmul(pAw[:, dc * 512:(dc + 1) * 512], ones128[:],
                                    wraw[:, (r % 4) * T + dc * 512:(r % 4) * T + (dc + 1) * 512],
                                    start=(jt == 0), stop=(jt == NTT - 1))
            TE.inc(ins, 2)
    bl.tensor(sec_alpha_mm)

    def sec_alpha_ev(vector):
        vector = SS(vector, VE)
        vector.wait_ge(TE.h, NKT * NCH)
        vector.tensor_scalar(out=a1bc[:], in0=pAw[:], scalar1=1.0 / D,
                             scalar2=None, op0=ALU.mult)
        vector.wait_ge(TE.h, NKT * NCH + NTT * 2)
        vector.tensor_scalar(out=a2bc[:], in0=pAw[:, 0:1024], scalar1=1.0 / T,
                             scalar2=None, op0=ALU.mult)
    bl.vector(sec_alpha_ev)

    A1L = ctx.S("d_a1l")

    def sec_a1x(gpsimd):
        gpsimd.wait_ge(VE.h, 2)
        A1L.dinc(gpsimd.dma_start(out=ctx.t["d_a1s"], in_=a1bc[0:1, :]))
        gpsimd.wait_ge(A1L.h, 16)
        A1L.dinc(gpsimd.dma_start(
            out=a1col[:],
            in_=ctx.t["d_a1s"].rearrange("(j p) -> p j", p=128),
            allow_slow_non_contiguous=True))
    bl.gpsimd(sec_a1x)
    VE_PREP = ctx.S("d_ve").n
    es_prep.close()
    hC = ctx.sb("d_hC", [128, NTT * 512], BF16, es)

    pH = ctx.ps("d_pH", [128, 2 * 512], F32, es)
    pO = ctx.ps("d_pO", [128, 2 * 1024], F32, es)

    ldc_d = ctx.S("ldc").n

    def sec_sync(sync):
        sync.wait_ge(ctx.S("ldc").h, ldc_d)
        sync.wait_ge(ctx.S("spill").h, ctx.S("spill").n)

        def store(i):
            sync.wait_ge(OV.h, i + 1)
            s_tok = (i // 4) * 512 + (i % 4) * 128
            SX1.issue(sync.dma_start(out=x1_dram[s_tok:s_tok + 128, :],
                                     in_=x1o[:, (i % 2) * D:(i % 2 + 1) * D]))
        for q in range(NCH):
            if q >= 2:
                sync.wait_ge(TH.h, (q - 1) * NTT)
            LXC.dinc(sync.dma_start(
                out=xchunk[:, (q % 2) * NKT * 512:(q % 2 + 1) * NKT * 512].rearrange(
                    "p (k t) -> p k t", k=NKT),
                in_=xnCv_dram[:, :, q * 512:(q + 1) * 512]))
            for sub in range(4):
                i = q * 4 + sub
                if i >= 2:
                    sync.wait_ge(OV.h, i - 1)
                s_tok = q * 512 + sub * 128
                LX1.issue(sync.dma_start(out=x1p[:, (i % 2) * D:(i % 2 + 1) * D],
                                         in_=x1p_dram[s_tok:s_tok + 128, :]))
                if i >= 2:
                    store(i - 2)
        store(4 * NCH - 2)
        store(4 * NCH - 1)
        SX1.wait_all(sync)
    bl.sync(sec_sync)

    def sec_pe(tensor):
        for q in range(NCH):
            tensor.wait_ge(LXC.h, 16 * (q + 1))
            if q == 0:
                tensor.wait_ge(VE.h, VE_PREP)
                tensor.wait_ge(SE.h, 2 * NKT)
                tensor.wait_ge(A1L.h, 32)
            xoff = (q % 2) * NKT * 512
            for jt in range(NTT):
                r = q * NTT + jt
                if r >= 2:
                    tensor.wait_ge(SH.h, r - 1)
                ins = None
                for k in range(NKT):
                    ins = tensor.matmul(pH[:, (r % 2) * 512:(r % 2 + 1) * 512],
                                        sw1T[:, k * T + jt * 128:k * T + (jt + 1) * 128],
                                        xchunk[:, xoff + k * 512:xoff + (k + 1) * 512],
                                        start=(k == 0), stop=(k == NKT - 1))
                TH.inc(ins)
            tensor.wait_ge(SH.h, (q + 1) * NTT)
            for sub in range(4):
                i = q * 4 + sub
                if i >= 2:
                    tensor.wait_ge(OV.h, i - 1)
                for dc in range(2):
                    ins = None
                    for jt in range(NTT):
                        ins = tensor.matmul(
                            pO[:, (i % 2) * 1024 + dc * 512:(i % 2) * 1024 + (dc + 1) * 512],
                            hC[:, jt * 512 + sub * 128:jt * 512 + (sub + 1) * 128],
                            sw2T[:, jt * 1024 + dc * 512:jt * 1024 + (dc + 1) * 512],
                            start=(jt == 0), stop=(jt == NTT - 1))
                    TO.inc(ins)
    bl.tensor(sec_pe)

    def sec_hevac(scalar):
        scalar = SS(scalar, SH)
        for q in range(NCH):
            for jt in range(NTT):
                r = q * NTT + jt
                scalar.wait_ge(TH.h, r + 1)
                if q >= 1 and jt == 0:
                    scalar.wait_ge(TO.h, q * 8)
                scalar.activation(out=hC[:, jt * 512:(jt + 1) * 512],
                                  in_=pH[:, (r % 2) * 512:(r % 2 + 1) * 512],
                                  func=AF.Gelu, scale=a1col[:, jt:jt + 1],
                                  bias=b1_col[:, jt:jt + 1])
    bl.scalar(sec_hevac)



    def sec_oevac(vector):
        ch = Chain(vector)
        OVm = ctx.S("d_ovm")
        for q in range(NCH):
            for sub in range(4):
                i = q * 4 + sub
                ch.wait_ge(TO.h, 2 * i + 2)
                LX1.wait_done(ch, i)
                SX1.wait_done(ch, i - 2)
                ch.do(OVm, lambda v, i=i: v.tensor_mul(
                    out=x1o[:, (i % 2) * D:(i % 2 + 1) * D],
                    in0=pO[:, (i % 2) * 1024:(i % 2 + 1) * 1024],
                    in1=a2bc[:]))
                ch.do(OV, lambda v, i=i: v.tensor_add(
                    out=x1o[:, (i % 2) * D:(i % 2 + 1) * D],
                    in0=x1o[:, (i % 2) * D:(i % 2 + 1) * D],
                    in1=x1p[:, (i % 2) * D:(i % 2 + 1) * D]))
    bl.vector(sec_oevac)




def emit_rotor_prelude(ctx, es):
    bl = ctx.block
    Wvo = ctx.sb("Wvo", [128, NKT * 1024], BF16, es, side="right")
    ones_r = ctx.sb("f_ones", [128, T], BF16, es, side="right")
    gtab = ctx.sb("f_g", [128, 32], F32, es, side="right")
    hpi_col = ctx.sb("f_hpi", [128, 1], F32, es, side="right")
    thB = ctx.sb("f_thB", [128, NKT * 128], F32, es, side="right")
    uuB = ctx.sb("f_uuB", [128, NKT * 128], F32, es, side="right")
    sinB = ctx.sb("f_sinB", [128, NKT * 128], BF16, es, side="right")
    cosB16 = ctx.sb("f_cosB", [128, NKT * 128], BF16, es, side="right")
    cosB = ctx.sb("f_cosBW", [128, NKT * 128], F32, es, side="right")
    thA = ctx.sb("f_thA", [128, NKT * 32], F32, es, side="right")
    sinA = ctx.sb("f_sinA", [128, NKT * 32], BF16, es, side="right")
    cosA16 = ctx.sb("f_cosA", [128, NKT * 32], BF16, es, side="right")
    cosA = ctx.sb("f_cosAW", [128, NKT * 32], F32, es, side="right")
    iota128 = ctx.t["iota128_sb"]
    iota32 = ctx.t["iota32_sb"]
    fr_col = ctx.t["fr_col_sb"]
    dWvT = ctx.t["d_WvT"]
    LWV = ctx.S("f_lwv")
    VEs = ctx.S("f_ve")
    SEs = ctx.S("f_se")
    ldc_target = ctx.S("ldc").n

    def bc3(ap2d, mid, inner):
        if ap2d.shape[1] == inner:
            return ap2d.rearrange("p (o u) -> p o u", o=1).to_broadcast(
                [128, mid, inner])
        return ap2d.rearrange("p (b o) -> p b o", o=1).to_broadcast(
            [128, mid, inner])
    ctx.t["f_bc3"] = bc3

    def sec_wload_v(gpsimd):
        for k in range(NKT):
            LWV.dinc(gpsimd.dma_start(out=Wvo[:, k * 1024:(k + 1) * 1024],
                                      in_=dWvT[k * 128:(k + 1) * 128, :]))
    bl.gpsimd(sec_wload_v)

    def sec_g(vector):
        vector = SS(vector, VEs)
        vector.wait_ge(ctx.S("ldc").h, ldc_target)
        g, kg = gtab[:, 0:8], gtab[:, 24:32]
        vector.tensor_scalar(out=g, in0=fr_col[:], scalar1=128.0, scalar2=None,
                             op0=ALU.mult)
        vector.tensor_scalar(out=kg, in0=g, scalar1=INV2PI, scalar2=MAGIC,
                             op0=ALU.mult, op1=ALU.add)
        vector.tensor_scalar(out=kg, in0=kg, scalar1=MAGIC, scalar2=None,
                             op0=ALU.subtract)
        vector.scalar_tensor_tensor(out=g, in0=kg, scalar=-C1, in1=g,
                                    op0=ALU.mult, op1=ALU.add)
        vector.scalar_tensor_tensor(out=g, in0=kg, scalar=-C2, in1=g,
                                    op0=ALU.mult, op1=ALU.add)
        vector.memset(ones_r[:], 1.0)
        vector.memset(hpi_col[:], HALF_PI)
        thBv = thB[:].rearrange("p (k u) -> p k u", k=NKT)
        vector.tensor_mul(out=thBv, in0=bc3(fr_col[:], NKT, 128),
                          in1=bc3(iota128[:], NKT, 128))
        vector.tensor_scalar(out=uuB[:], in0=thB[:], scalar1=INV2PI,
                             scalar2=MAGIC, op0=ALU.mult, op1=ALU.add)
        vector.tensor_scalar(out=uuB[:], in0=uuB[:], scalar1=MAGIC,
                             scalar2=None, op0=ALU.subtract)
        vector.scalar_tensor_tensor(out=thB[:], in0=uuB[:], scalar=-TWOPI_F,
                                    in1=thB[:], op0=ALU.mult, op1=ALU.add)
        vector.tensor_scalar(out=cosB[:], in0=thB[:], scalar1=-PI_F,
                             scalar2=PI_F, op0=ALU.max, op1=ALU.min)
        thAv = thA[:].rearrange("p (k b) -> p k b", k=NKT)
        vector.tensor_mul(out=thAv, in0=bc3(gtab[:, 0:8], NKT, 32),
                          in1=bc3(iota32[:, 0:32], NKT, 32))
        vector.tensor_scalar(out=uuB[:, 0:NKT * 32], in0=thA[:], scalar1=INV2PI,
                             scalar2=MAGIC, op0=ALU.mult, op1=ALU.add)
        vector.tensor_scalar(out=uuB[:, 0:NKT * 32], in0=uuB[:, 0:NKT * 32],
                             scalar1=MAGIC, scalar2=None, op0=ALU.subtract)
        vector.scalar_tensor_tensor(out=thA[:], in0=uuB[:, 0:NKT * 32],
                                    scalar=-TWOPI_F,
                                    in1=thA[:], op0=ALU.mult, op1=ALU.add)
        vector.tensor_scalar(out=cosA[:], in0=thA[:], scalar1=-PI_F,
                             scalar2=PI_F, op0=ALU.max, op1=ALU.min)
    bl.vector(sec_g)
    ve_gbuild = VEs.n

    def sec_se_tab(scalar):
        scalar = SS(scalar, SEs)
        scalar.wait_ge(VEs.h, ve_gbuild)
        scalar.activation(out=sinB[:], in_=cosB[:], func=AF.Sin)
        scalar.activation(out=cosB[:], in_=cosB[:], func=AF.Abs)
        scalar.activation(out=cosB16[:], in_=cosB[:], func=AF.Sin,
                          scale=-1.0, bias=hpi_col[:, 0:1])
        scalar.activation(out=sinA[:], in_=cosA[:], func=AF.Sin)
        scalar.activation(out=cosA[:], in_=cosA[:], func=AF.Abs)
        scalar.activation(out=cosA16[:], in_=cosA[:], func=AF.Sin,
                          scale=-1.0, bias=hpi_col[:, 0:1])
    bl.scalar(sec_se_tab)
    ctx.t["f_setab_n"] = SEs.n


def emit_rotor(ctx, x1_dram, x2p_dram, rC_dram, es):
    bl = ctx.block
    xn1C = ctx.t["xn1C"]
    es_t = contextlib.ExitStack()
    Wvo = ctx.t["Wvo"]
    v_sb = ctx.sb("f_v", [128, 2 * T], BF16, es_t)
    sint = ctx.sb("f_sin", [128, T], BF16, es_t)
    cost = ctx.sb("f_cos", [128, T], BF16, es_t)
    tmp = ctx.sb("f_tmp", [128, T], BF16, es_t)
    Pb = ctx.sb("f_P", [128, T], BF16, es_t)
    Qb = ctx.sb("f_Q", [128, T], BF16, es_t)
    Pc = ctx.sb("f_Pc", [128, T], BF16, es_t)
    Qc = ctx.sb("f_Qc", [128, T], BF16, es_t)
    ones_r = ctx.t["f_ones"]
    sinA, cosA16 = ctx.t["f_sinA"], ctx.t["f_cosA"]
    sinB, cosB16 = ctx.t["f_sinB"], ctx.t["f_cosB"]
    pV = ctx.ps("f_pV", [128, 2 * 512], F32, es_t)
    dWoT = ctx.t["d_WoT"]
    bc3 = ctx.t["f_bc3"]

    LWV = ctx.S("f_lwv"); LWO2 = ctx.S("f_lwo")
    TE = ctx.S("f_te"); SEs = ctx.S("f_se"); VEs = ctx.S("f_ve")
    GEs = ctx.S("f_ge"); SRC = ctx.S("f_src")
    se_tab = ctx.t["f_setab_n"]
    se_base = SEs.n
    te_base = TE.n

    def sec_vmm(tensor):
        tensor.wait_ge(LWV.h, 16 * NKT)
        for k in range(NKT):
            for q in range(NCH):
                m = k * NCH + q
                if m >= 2:
                    kp2, qp2 = divmod(m - 2, NCH)
                    tensor.wait_ge(SEs.h, se_base + 8 * kp2 + qp2 + 1)
                ins = None
                for dt_ in range(NKT):
                    ins = tensor.matmul(
                        pV[:, (m % 2) * 512:(m % 2 + 1) * 512],
                        Wvo[:, dt_ * 1024 + k * 128:dt_ * 1024 + (k + 1) * 128],
                        xn1C[:, dt_ * T + q * 512:dt_ * T + (q + 1) * 512],
                        start=(dt_ == 0), stop=(dt_ == NKT - 1))
                TE.inc(ins)
    bl.tensor(sec_vmm)

    cnt = {nm: {} for nm in ("recon", "P", "Q", "S1", "S2", "m1", "m2", "r")}
    for kc in range(NKT + 1):
        k = kc
        kp = kc - 1

        if kp >= 0:
            def sec_ve_scan(vector, k=kp):
                vector = SS(vector, VEs)
                if k >= 1:
                    vector.wait_ge(SRC.h, 16 * k)
                vector.tensor_tensor_scan(out=Pc[:], data0=ones_r[:], data1=Pb[:],
                                          initial=0.0, op0=ALU.mult, op1=ALU.add)
                cnt["S1"][k] = VEs.n
                vector.tensor_tensor_scan(out=Qc[:], data0=ones_r[:], data1=Qb[:],
                                          initial=0.0, op0=ALU.mult, op1=ALU.add)
                cnt["S2"][k] = VEs.n
                vector.tensor_mul(out=Pb[:], in0=cost[:], in1=Pc[:])
                cnt["m1"][k] = VEs.n
                vector.tensor_mul(out=Qb[:], in0=sint[:], in1=Qc[:])
                cnt["m2"][k] = VEs.n
                vector.tensor_add(out=Pc[:], in0=Pb[:], in1=Qb[:])
                cnt["r"][k] = VEs.n
            bl.vector(sec_ve_scan)

            def sec_store(sync, k=kp):
                sync.wait_ge(VEs.h, cnt["r"][k])
                SRC.dinc(sync.dma_start(out=rC_dram[:, k * T:(k + 1) * T],
                                        in_=Pc[:]))
            bl.sync(sec_store)

        if k < NKT:
            def sec_ve_recon(vector, k=k):
                vector = SS(vector, VEs)
                vector.wait_ge(SEs.h, se_tab)
                As = bc3(sinA[:, k * 32:(k + 1) * 32], 32, 128)
                Ac = bc3(cosA16[:, k * 32:(k + 1) * 32], 32, 128)
                Bs = bc3(sinB[:, k * 128:(k + 1) * 128], 32, 128)
                Bc = bc3(cosB16[:, k * 128:(k + 1) * 128], 32, 128)
                sv = sint[:].rearrange("p (b u) -> p b u", b=32)
                cv = cost[:].rearrange("p (b u) -> p b u", b=32)
                tv = tmp[:].rearrange("p (b u) -> p b u", b=32)
                vector.tensor_mul(out=sv, in0=As, in1=Bc)
                vector.tensor_mul(out=tv, in0=Ac, in1=Bs)
                vector.tensor_add(out=sint[:], in0=sint[:], in1=tmp[:])
                vector.tensor_mul(out=cv, in0=Ac, in1=Bc)
                vector.tensor_mul(out=tv, in0=As, in1=Bs)
                vector.tensor_sub(out=cost[:], in0=cost[:], in1=tmp[:])
                cnt["recon"][k] = VEs.n
            bl.vector(sec_ve_recon)

        if k < NKT:
            def sec_se_ev(scalar, k=k):
                scalar = SS(scalar, SEs)
                if k >= 2:
                    scalar.wait_ge(VEs.h, cnt["Q"][k - 2])
                vo = (k % 2) * T
                for q in range(NCH):
                    m = k * NCH + q
                    scalar.wait_ge(TE.h, te_base + m + 1)
                    scalar.copy(out=v_sb[:, vo + q * 512:vo + (q + 1) * 512],
                                in_=pV[:, (m % 2) * 512:(m % 2 + 1) * 512])
                assert SEs.n == se_base + 8 * (k + 1), (SEs.n, se_base, k)
            bl.scalar(sec_se_ev)

            def sec_ve_p(vector, k=k):
                vector = SS(vector, VEs)
                vector.wait_ge(SEs.h, se_base + 8 * (k + 1))
                vector.tensor_mul(out=Pb[:], in0=v_sb[:, (k % 2) * T:(k % 2 + 1) * T],
                                  in1=cost[:])
                cnt["P"][k] = VEs.n
            bl.vector(sec_ve_p)

            def sec_ve_q(vector, k=k):
                vector = SS(vector, VEs)
                vector.tensor_mul(out=Qb[:], in0=v_sb[:, (k % 2) * T:(k % 2 + 1) * T],
                                  in1=sint[:])
                cnt["Q"][k] = VEs.n
            bl.vector(sec_ve_q)

    te_vmm_all = TE.n

    def sec_wload_o(gpsimd):
        gpsimd.wait_ge(TE.h, te_vmm_all)
        for k in range(NKT):
            LWO2.dinc(gpsimd.dma_start(out=Wvo[:, k * 1024:(k + 1) * 1024],
                                       in_=dWoT[k * 128:(k + 1) * 128, :]))
    bl.gpsimd(sec_wload_o)
    lw_o = ctx.S("f_lwo").n

    es_t.close()
    rbT = ctx.sb("f_rbT", [128, 2 * 1024], BF16, es)
    x1r = ctx.sb("f_x1r", [128, 2 * D], F32, es)
    x2p = ctx.sb("f_x2p", [128, 2 * D], F32, es)
    pO = ctx.ps("f_pO", [128, 2 * 1024], F32, es)
    TO = ctx.S("f_to"); OV = ctx.S("f_ov")
    LX1 = DStream(ctx, "f_lx1"); SX2 = DStream(ctx, "f_sx2")
    LRB = DStream(ctx, "f_lrb")
    ctx.store_streams.append(SX2)
    rCv_dram = rC_dram.rearrange("p (k t) -> p k t", k=NKT)

    def sec_sync2(sync):
        sync.wait_ge(SRC.h, 16 * NKT)

        def store(bp):
            sync.wait_ge(OV.h, bp + 1)
            SX2.issue(sync.dma_start(out=x2p_dram[bp * 128:(bp + 1) * 128, :],
                                     in_=x2p[:, (bp % 2) * D:(bp % 2 + 1) * D]))
        for b in range(NTT):
            if b >= 2:
                sync.wait_ge(TO.h, 2 * (b - 1))
            LRB.issue(sync.dma_start(
                out=rbT[:, (b % 2) * 1024:(b % 2 + 1) * 1024].rearrange(
                    "p (k t) -> p k t", k=NKT),
                in_=rCv_dram[:, :, b * 128:(b + 1) * 128]))
            if b >= 2:
                sync.wait_ge(OV.h, b - 1)
            LX1.issue(sync.dma_start(out=x1r[:, (b % 2) * D:(b % 2 + 1) * D],
                                     in_=x1_dram[b * 128:(b + 1) * 128, :]))
            if b >= 2:
                store(b - 2)
        store(NTT - 2)
        store(NTT - 1)
        SX2.wait_all(sync)
    bl.sync(sec_sync2)

    def sec_omm(tensor):
        tensor.wait_ge(LWO2.h, lw_o)
        for b in range(NTT):
            LRB.wait_done(tensor, b)
            if b >= 2:
                tensor.wait_ge(OV.h, b - 1)
            for q2 in range(2):
                ins = None
                for k in range(NKT):
                    ins = tensor.matmul(
                        pO[:, (b % 2) * 1024 + q2 * 512:(b % 2) * 1024 + (q2 + 1) * 512],
                        rbT[:, (b % 2) * 1024 + k * 128:(b % 2) * 1024 + (k + 1) * 128],
                        Wvo[:, k * 1024 + q2 * 512:k * 1024 + (q2 + 1) * 512],
                        start=(k == 0), stop=(k == NKT - 1))
                TO.inc(ins)
    bl.tensor(sec_omm)

    def sec_oev(vector):
        vector = SS(vector, OV)
        for b in range(NTT):
            vector.wait_ge(TO.h, 2 * b + 2)
            LX1.wait_done(vector, b)
            SX2.wait_done(vector, b - 2)
            vector.tensor_add(out=x2p[:, (b % 2) * D:(b % 2 + 1) * D],
                              in0=pO[:, (b % 2) * 1024:(b % 2 + 1) * 1024],
                              in1=x1r[:, (b % 2) * D:(b % 2 + 1) * D])
    bl.vector(sec_oev)




def emit_ffn1(ctx, xn1C_dram, x2p_dram, out_dram, es):
    bl = ctx.block
    if "WinTb" in ctx.t:
        WinTb, WoutTb = ctx.t["WinTb"], ctx.t["WoutTb"]
        preloaded = True
    else:
        WinTb = ctx.sb("WinTb", [128, NKT * JW], BF16, es)
        WoutTb = ctx.sb("WoutTb", [128, NJT * 1024], BF16, es)
        preloaded = False
    acts = ctx.sb("g_acts", [128, NJT * 512], BF16, es)
    dec = ctx.sb("g_dec", [128, 11 * 512], BF16, es)
    msk2 = ctx.sb("g_msk", [128, 2 * NJT * 512], BF16, es)
    u0 = ctx.sb("g_u0", [128, 11 * 512], BF16, es)
    u1 = ctx.sb("g_u1", [128, 11 * 512], BF16, es)
    xchunk = ctx.sb("g_xc", [128, 2 * NKT * 512], BF16, es)
    x2p = ctx.sb("g_x2p", [128, 2 * D], F32, es)
    xout = ctx.sb("g_out", [128, D], F32, es)
    pL = ctx.ps("g_pL", [128, 2 * 512], F32, es)
    pS = ctx.ps("g_pS", [128, 2 * 512], F32, es)
    pO = ctx.ps("g_pO", [128, 2 * 1024], F32, es)
    bin_col = ctx.t["bin_col_sb"]
    S0sb, S1sb = ctx.t["S0_sb"], ctx.t["S1_sb"]
    dWinT, dWoutT = ctx.t["d_WinT"], ctx.t["d_WoutT"]
    xn1Cv_dram = xn1C_dram.rearrange("p (k t) -> p k t", k=NKT)
    ldc_target = ctx.S("ldc").n

    LWI = ctx.S("g_lwi"); LWO = ctx.S("g_lwo")
    TE = ctx.S("g_te"); SE = ctx.S("g_se"); VE = ctx.S("g_ve")
    LXC = DStream(ctx, "g_lxc"); LX2 = DStream(ctx, "g_lx2")
    SOUT = DStream(ctx, "g_sout")
    OV = ctx.S("g_ov")

    if not preloaded:
        def sec_wload(gpsimd):
            for k in range(NKT):
                LWI.dinc(gpsimd.dma_start(out=WinTb[:, k * JW:(k + 1) * JW],
                                          in_=dWinT[k * 128:(k + 1) * 128, :]))
            for jt in range(NJT):
                LWO.dinc(gpsimd.dma_start(
                    out=WoutTb[:, jt * 1024:(jt + 1) * 1024],
                    in_=dWoutT[jt * 128:(jt + 1) * 128, :]))
        bl.gpsimd(sec_wload)

    def sec_xc0(sync):
        sync.wait_ge(ctx.S("ldc").h, ldc_target)
        sync.wait_ge(ctx.S("spill").h, ctx.S("spill").n)
        LXC.issue(sync.dma_start(
            out=xchunk[:, 0:NKT * 512].rearrange("p (k t) -> p k t", k=NKT),
            in_=xn1Cv_dram[:, :, 0:512]))
        LXC.issue(sync.dma_start(
            out=xchunk[:, NKT * 512:].rearrange("p (k t) -> p k t", k=NKT),
            in_=xn1Cv_dram[:, :, 512:1024]))
    bl.sync(sec_xc0)

    silu_end = {}; dec_end = {}; tree_end = {}
    tev_cnt = {}; uu_cnt = {}; tmm_cnt = {}; chain_to = {}

    def emit_omm_chain(q_out, c, part=None):
        sub, dc = c // 2, c % 2
        i2 = q_out * 4 + sub
        jts = range(NJT) if part is None else (
            range(0, 10) if part == 0 else range(10, NJT))

        def sec_ommc(tensor, q_out=q_out, sub=sub, dc=dc, i2=i2, jts=jts,
                     part=part):
            if part in (None, 0):
                if q_out == 0 and sub == 0 and dc == 0:
                    tensor.wait_ge(LWO.h, 16 * NJT)
                tensor.wait_ge(VE.h, tree_end[q_out])
                if i2 >= 2 and dc == 0:
                    tensor.wait_ge(OV.h, i2 - 1)
            moff = (q_out % 2) * NJT * 512
            ins = None
            for jt in jts:
                ins = tensor.matmul(
                    pO[:, (i2 % 2) * 1024 + dc * 512:(i2 % 2) * 1024 + (dc + 1) * 512],
                    msk2[:, moff + jt * 512 + sub * 128:moff + jt * 512 + (sub + 1) * 128],
                    WoutTb[:, jt * 1024 + dc * 512:jt * 1024 + (dc + 1) * 512],
                    start=(jt == 0), stop=(jt == NJT - 1))
            if part in (None, 1):
                TO = ctx.S("g_to")
                TO.inc(ins)
                chain_to[(q_out, sub, dc)] = TO.n
        bl.tensor(sec_ommc)

    TO = ctx.S("g_to")

    for q in range(NCH):
        te_l0 = TE.n
        ve0 = ctx.S("g_ve").n
        se0 = ctx.S("g_se").n
        xoff = (q % 2) * NKT * 512

        def sec_lmm(tensor, q=q, xoff=xoff, ve0=ve0, se0=se0):
            LXC.wait_done(tensor, q)
            if q == 0:
                tensor.wait_ge(LWI.h, 16 * NKT)
            for jt in range(NJT):
                if jt < 2 and q >= 1:
                    tensor.wait_ge(SE.h, silu_end[q - 1])
                    tensor.wait_ge(VE.h, dec_end[q - 1])
                if jt >= 2:
                    tensor.wait_ge(SE.h, se0 + jt - 1)
                    if jt - 2 < 11:
                        tensor.wait_ge(VE.h, ve0 + jt - 1)
                ins = None
                for k in range(NKT):
                    ins = tensor.matmul(pL[:, (jt % 2) * 512:(jt % 2 + 1) * 512],
                                        WinTb[:, k * JW + jt * 128:k * JW + (jt + 1) * 128],
                                        xchunk[:, xoff + k * 512:xoff + (k + 1) * 512],
                                        start=(k == 0), stop=(k == NKT - 1))
                TE.inc(ins)
        bl.tensor(sec_lmm)
        te_lmm = TE.n

        def sec_silu(scalar, q=q, te_l0=te_l0):
            scalar = SS(scalar, SE)
            if q >= 1:
                scalar.wait_ge(VE.h, tree_end[q - 1])
            for jt in range(NJT):
                scalar.wait_ge(TE.h, te_l0 + jt + 1)
                scalar.activation(out=acts[:, jt * 512:(jt + 1) * 512],
                                  in_=pL[:, (jt % 2) * 512:(jt % 2 + 1) * 512],
                                  func=AF.Silu, bias=bin_col[:, jt:jt + 1])
        bl.scalar(sec_silu)
        silu_end[q] = ctx.S("g_se").n

        def sec_dec(vector, q=q, te_l0=te_l0, se0=se0):
            vector = SS(vector, VE)
            for jt in range(11):
                vector.wait_ge(TE.h, te_l0 + jt + 1)
                vector.wait_ge(SE.h, se0 + jt + 1)
                vector.tensor_scalar(out=dec[:, jt * 512:(jt + 1) * 512],
                                     in0=pL[:, (jt % 2) * 512:(jt % 2 + 1) * 512],
                                     scalar1=bin_col[:, jt:jt + 1], scalar2=0.0,
                                     op0=ALU.add, op1=ALU.is_gt)
            vector.tensor_copy(out=u1[:, 0:512], in_=dec[:, 0:512])
            vector.tensor_scalar(out=u0[:, 0:512], in0=dec[:, 0:512],
                                 scalar1=-1.0, scalar2=1.0,
                                 op0=ALU.mult, op1=ALU.add)
            uu_cnt[(q, 0)] = ctx.S("g_ve").n
            if q >= 2:
                vector.wait_ge(TO.h, chain_to[(q - 2, 3, 1)])
            vector.tensor_copy(out=msk2[:, (q % 2) * NJT * 512:(q % 2) * NJT * 512 + 512],
                               in_=acts[:, 0:512])
        bl.vector(sec_dec)
        dec_end[q] = ctx.S("g_ve").n

        if q + 2 < NCH:
            def sec_xc(sync, q=q, te_lmm=te_lmm):
                sync.wait_ge(TE.h, te_lmm)
                LXC.issue(sync.dma_start(
                    out=xchunk[:, (q % 2) * NKT * 512:(q % 2 + 1) * NKT * 512].rearrange(
                        "p (k t) -> p k t", k=NKT),
                    in_=xn1Cv_dram[:, :, (q + 2) * 512:(q + 3) * 512]))
            bl.sync(sec_xc)

        if q >= 1:
            def sec_x2pload(sync, q=q):
                for sub in range(4):
                    i2 = (q - 1) * 4 + sub
                    if i2 >= 2:
                        sync.wait_ge(OV.h, i2 - 1)
                    s_tok = (q - 1) * 512 + sub * 128
                    LX2.issue(sync.dma_start(
                        out=x2p[:, (i2 % 2) * D:(i2 % 2 + 1) * D],
                        in_=x2p_dram[s_tok:s_tok + 128, :]))
            bl.sync(sec_x2pload)

        for r, (ct, pt) in enumerate(TREE_SMETA):
            def sec_tmm(tensor, q=q, r=r, ct=ct, pt=pt):
                tensor.wait_ge(VE.h, uu_cnt[(q, pt)])
                if r >= 2:
                    tensor.wait_ge(VE.h, tev_cnt[(q, r - 2)])
                elif q >= 1:
                    tensor.wait_ge(VE.h, tev_cnt[(q - 1, 16 + r)])
                tensor.matmul(pS[:, (r % 2) * 512:(r % 2 + 1) * 512],
                              S0sb[:, r * 128:(r + 1) * 128],
                              u0[:, pt * 512:(pt + 1) * 512],
                              start=True, stop=False)
                TE.inc(tensor.matmul(pS[:, (r % 2) * 512:(r % 2 + 1) * 512],
                                     S1sb[:, r * 128:(r + 1) * 128],
                                     u1[:, pt * 512:(pt + 1) * 512],
                                     start=False, stop=True))
                tmm_cnt[(q, r)] = ctx.S("g_te").n
            bl.tensor(sec_tmm)

            if q >= 1 and 1 <= r <= 16:
                emit_omm_chain(q - 1, (r - 1) // 2, part=(r - 1) % 2)

            if q >= 1 and r % 4 == 0 and r >= 4:
                subo = r // 4 - 1
                i2o = (q - 1) * 4 + subo

                def sec_oev1(vector, q=q, subo=subo, i2o=i2o):
                    vector = SS(vector, OV)
                    vector.wait_ge(TO.h, chain_to[(q - 1, subo, 1)])
                    LX2.wait_done(vector, i2o)
                    SOUT.wait_done(vector, i2o - 1)
                    vector.tensor_add(out=xout[:, 0:D],
                                      in0=pO[:, (i2o % 2) * 1024:(i2o % 2 + 1) * 1024],
                                      in1=x2p[:, (i2o % 2) * D:(i2o % 2 + 1) * D])
                bl.vector(sec_oev1)

            def sec_tev(vector, q=q, r=r, ct=ct, se0=se0):
                vector = SS(vector, VE)
                vector.wait_ge(TE.h, tmm_cnt[(q, r)])
                ps = pS[:, (r % 2) * 512:(r % 2 + 1) * 512]
                moff = (q % 2) * NJT * 512
                if ct < 11:
                    vector.tensor_mul(out=u1[:, ct * 512:(ct + 1) * 512],
                                      in0=ps, in1=dec[:, ct * 512:(ct + 1) * 512])
                    vector.tensor_sub(out=u0[:, ct * 512:(ct + 1) * 512],
                                      in0=ps, in1=u1[:, ct * 512:(ct + 1) * 512])
                    uu_cnt[(q, ct)] = ctx.S("g_ve").n
                vector.wait_ge(SE.h, se0 + ct + 1)
                vector.tensor_mul(out=msk2[:, moff + ct * 512:moff + (ct + 1) * 512],
                                  in0=acts[:, ct * 512:(ct + 1) * 512], in1=ps)
                tev_cnt[(q, r)] = ctx.S("g_ve").n
            bl.vector(sec_tev)
        tree_end[q] = ctx.S("g_ve").n

        if q >= 1:
            def sec_ostore(sync, q=q):
                for sub in range(4):
                    i2 = (q - 1) * 4 + sub
                    sync.wait_ge(OV.h, i2 + 1)
                    s_tok = (q - 1) * 512 + sub * 128
                    SOUT.issue(sync.dma_start(
                        out=out_dram[s_tok:s_tok + 128, :],
                        in_=xout[:, 0:D]))
            bl.gpsimd(sec_ostore)

    qt = NCH - 1

    def sec_x2pload_t(sync):
        for sub in range(4):
            i2 = qt * 4 + sub
            sync.wait_ge(OV.h, i2 - 1)
            s_tok = qt * 512 + sub * 128
            LX2.issue(sync.dma_start(out=x2p[:, (i2 % 2) * D:(i2 % 2 + 1) * D],
                                     in_=x2p_dram[s_tok:s_tok + 128, :]))
    bl.sync(sec_x2pload_t)
    for c in range(8):
        emit_omm_chain(qt, c)

    def sec_oev_t(vector):
        vector = SS(vector, OV)
        for sub in range(4):
            i2 = qt * 4 + sub
            vector.wait_ge(TO.h, chain_to[(qt, sub, 1)])
            LX2.wait_done(vector, i2)
            SOUT.wait_done(vector, i2 - 1)
            vector.tensor_add(out=xout[:, 0:D],
                              in0=pO[:, (i2 % 2) * 1024:(i2 % 2 + 1) * 1024],
                              in1=x2p[:, (i2 % 2) * D:(i2 % 2 + 1) * D])
    bl.vector(sec_oev_t)

    def sec_ostore_t(sync):
        for sub in range(4):
            i2 = qt * 4 + sub
            sync.wait_ge(OV.h, i2 + 1)
            s_tok = qt * 512 + sub * 128
            SOUT.issue(sync.dma_start(out=out_dram[s_tok:s_tok + 128, :],
                                      in_=xout[:, 0:D]))
    bl.gpsimd(sec_ostore_t)

    def sec_final(gpsimd):
        SOUT.wait_all(gpsimd)
    bl.gpsimd(sec_final)



PHASE_INPUTS = {
    "A": [], "B": ["Wn"], "C": [], "D": ["w1T", "w2T"],
    "E": [], "F": ["WvT", "WoT"], "G": ["WinT", "WoutT"],
}
TOP_CONSTS = {
    "A": ["ident", "eps_col"], "B": ["ident", "b_col"], "C": ["ident"],
    "D": ["b1_col"], "E": ["ident", "eps_col"],
    "F": ["fr_col"], "G": [],
}
HBM_T = {
    "xnC_h": ([128, NKT * T], BF16, "A", ["B", "D"]),
    "x1p_h": ([T, D], F32, "C", ["D"]),
    "x1_h": ([T, D], F32, "D", ["E", "F"]),
    "xn1C_h": ([128, NKT * T], BF16, "E", ["F", "G"]),
    "x2p_h": ([T, D], F32, "F", ["G"]),
}
PHASE_ORDER = ["A", "B", "C", "D", "E", "F", "G"]


def build_group(phases):
    nc = bass.Bass("TRN2", target_bir_lowering=False)
    in_names, out_names = [], []
    consts = set()
    dins = set()
    for ph in phases:
        consts.update(TOP_CONSTS[ph])
        dins.update(PHASE_INPUTS[ph])
    if "C" in phases:
        dins.update(["band", "b2_row"])
    if "F" in phases:
        dins.update(["iota128_row", "iota32_row"])
    if "G" in phases:
        dins.update(["bin_col", "S0all", "S1all"])
    if "C" in phases:
        assert "B" in phases
    if "F" in phases and "E" not in phases:
        pass

    es = contextlib.ExitStack()
    with es:
        dram = {}
        for nm in sorted(dins | consts):
            shape, dt = CONST_SPECS[nm]
            dram["d_" + nm] = nc.dram_tensor(nm, shape, dt, kind="ExternalInput").ap()
            in_names.append(nm)
        if "A" in phases or "C" in phases:
            dram["d_x"] = nc.dram_tensor("x", [T, D], F32, kind="ExternalInput").ap()
            in_names.append("x")
        pset = set(phases)
        for nm, (shape, dt, prod, cons) in HBM_T.items():
            produced = prod in pset
            consumed_here = bool(pset & set(cons))
            consumed_later = bool((set(cons) - pset))
            if not produced and not consumed_here:
                continue
            if produced and consumed_later:
                kind = "ExternalOutput"
                out_names.append(nm)
            elif produced:
                kind = "Internal"
            else:
                kind = "ExternalInput"
                in_names.append(nm)
            dram["d_" + nm] = nc.dram_tensor(nm, shape, dt, kind=kind).ap()
        if "D" in phases:
            dram["d_a1s"] = nc.dram_tensor("a1s", [T], F32,
                                           kind="Internal").ap()
        if "F" in phases:
            dram["d_rC_h"] = nc.dram_tensor("rC_h", [128, NKT * T], BF16,
                                            kind="Internal").ap()
        if "G" in phases:
            dram["d_out"] = nc.dram_tensor("out", [T, D], F32,
                                           kind="ExternalOutput").ap()
            out_names.append("out")

        block = es.enter_context(nc.Block())
        ctx = Ctx(nc, es, block)
        ctx.t.update(dram)
        emit_preload(ctx, sorted(consts))

        es_bpsum = contextlib.ExitStack()
        b_psum = None
        if "A" in phases and "B" in phases:
            b_psum = (ctx.ps("b_pf", [128, 2 * 512], F32, es_bpsum),
                      ctx.ps("b_pT", [128, 2 * 1024], BF16, es_bpsum))
        if "A" in phases:
            with contextlib.ExitStack() as sA:
                emit_ln(ctx, "a", ctx.t["d_x"], "xnC", ctx.t["d_xnC_h"], sA,
                        dst_es=sA)
            barrier(ctx)
        with contextlib.ExitStack() as s_bc:
            if "B" in phases:
                with contextlib.ExitStack() as sB:
                    emit_fproj(ctx, ctx.t["d_xnC_h"], sB, dst_es=s_bc,
                               psum=b_psum)
                barrier(ctx)
                es_bpsum.close()
            if "C" in phases:
                with contextlib.ExitStack() as sC:
                    emit_preload(ctx, ["band", "b2_row"], es=sC)
                    emit_attn0(ctx, ctx.t["d_x"], ctx.t["d_x1p_h"], sC)
                barrier(ctx)
        if "D" in phases:
            with contextlib.ExitStack() as sD:
                emit_ffn0(ctx, ctx.t["d_xnC_h"], ctx.t["d_x1p_h"],
                          ctx.t["d_x1_h"], sD)
            barrier(ctx)
        with contextlib.ExitStack() as s_ef:
            if "E" in phases:
                if "F" in phases:
                    emit_preload(ctx, ["iota128_row", "iota32_row"], es=s_ef)
                with contextlib.ExitStack() as sE:
                    emit_ln(ctx, "e", ctx.t["d_x1_h"], "xn1C",
                            ctx.t["d_xn1C_h"], sE, dst_es=s_ef)
                    if "F" in phases:
                        emit_rotor_prelude(ctx, s_ef)
                barrier(ctx)
            if "F" in phases:
                with contextlib.ExitStack() as sF:
                    if "xn1C" not in ctx.t:
                        t = ctx.sb("xn1C", [128, NKT * T], BF16, s_ef)
                        LD = ctx.S("ld_xn1c")

                        def sec(sync):
                            LD.dinc(sync.dma_start(out=t[:], in_=ctx.t["d_xn1C_h"]))
                        block.sync(sec)
                        tgt = LD.n

                        def secw(eng):
                            eng.wait_ge(LD.h, tgt)
                        block.tensor(secw)
                    if "E" not in phases:
                        emit_preload(ctx, ["iota128_row", "iota32_row"], es=sF)
                        emit_rotor_prelude(ctx, sF)
                    emit_rotor(ctx, ctx.t["d_x1_h"], ctx.t["d_x2p_h"],
                               ctx.t["d_rC_h"], sF)
                barrier(ctx)
        if "G" in phases:
            with contextlib.ExitStack() as sG:
                emit_preload(ctx, ["bin_col", "S0all", "S1all"], es=sG)
                emit_ffn1(ctx, ctx.t["d_xn1C_h"], ctx.t["d_x2p_h"],
                          ctx.t["d_out"], sG)
            barrier(ctx)
        if "G" not in phases:
            def sec_fin(gpsimd):
                if "spill" in ctx.sem:
                    gpsimd.wait_ge(ctx.sem["spill"].h, ctx.sem["spill"].n)
                for st_ in ctx.store_streams:
                    st_.wait_all(gpsimd)
            block.gpsimd(sec_fin)
    return nc, in_names, out_names


GROUPS_UNFUSED = [["A", "B", "C"], ["D"], ["E"], ["F"], ["G"]]
GROUPS_FUSED = [["A", "B", "C", "D", "E", "F", "G"]]
GROUPS = GROUPS_FUSED

_BUILD_CACHE = {}


def kernel(**inputs):
    common, per_core = host_prep(inputs)
    carry = [dict(pc) for pc in per_core]
    result = None
    for phases in GROUPS:
        key = tuple(phases)
        if key not in _BUILD_CACHE:
            _BUILD_CACHE[key] = build_group(phases)
        nc, in_names, out_names = _BUILD_CACHE[key]
        in_maps = []
        for ci in range(8):
            m = {}
            for nm in in_names:
                m[nm] = common[nm] if nm in common else carry[ci][nm]
            in_maps.append(m)
        res = run_bass_kernel_spmd(nc, in_maps, core_ids=list(range(8)))
        for ci in range(8):
            for nm in out_names:
                carry[ci][nm] = np.asarray(res.results[ci][nm])
        if "G" in phases:
            result = np.stack([carry[ci]["out"].astype(np.float32)
                               for ci in range(8)], axis=0)
    return result


# version 32
# speedup vs baseline: 1.0213x; 1.0005x over previous
import contextlib
import math

import numpy as np
import ml_dtypes

import concourse.bass as bass
import concourse.mybir as mybir
from concourse.bass_utils import run_bass_kernel_spmd

F32 = mybir.dt.float32
BF16 = mybir.dt.bfloat16
I32 = mybir.dt.int32
AF = mybir.ActivationFunctionType
ALU = mybir.AluOpType
AX = mybir.AxisListType

D = 1024
T = 4096
NKT = 8
NTT = 32
NCH = 8

TREE_DEPTH = 7
TREE_PAR = 8
N_NODES = 255
TREE_TILE_BASE = [0, 1, 2, 3, 4, 5, 7, 11]
NJT = 19
JW = NJT * 128

TWO_PI = 6.283185307179586
C1 = float(np.float32(6.28125))
C2 = float(np.float32(TWO_PI - 6.28125))
C3 = float(np.float32(TWO_PI - 6.28125 - float(np.float32(TWO_PI - 6.28125))))
INV2PI = float(np.float32(1.0 / TWO_PI))
HALF_PI = float(np.float32(math.pi / 2))
PI_F = float(np.float32(math.pi))
TWOPI_F = float(np.float32(TWO_PI))
MAGIC = 12582912.0
LN_EPS = 1e-5

BF = ml_dtypes.bfloat16



def _tree_perm():
    slot_src = np.full(JW, -1, dtype=np.int64)
    for p in range(TREE_PAR):
        for d in range(TREE_DEPTH + 1):
            base = 2 ** d - 1
            for k in range(2 ** d):
                j = p * N_NODES + base + k
                idx = p * (2 ** d) + k
                slot_src[128 * TREE_TILE_BASE[d] + idx] = j
    return slot_src


def _tree_scatter():
    smeta = []
    mats0, mats1 = [], []
    for d in range(TREE_DEPTH):
        n_par = TREE_PAR * 2 ** d
        n_child = 2 * n_par
        pt0, ct0 = TREE_TILE_BASE[d], TREE_TILE_BASE[d + 1]
        n_ct = (n_child + 127) // 128
        for ci in range(n_ct):
            S0 = np.zeros((128, 128), dtype=np.float32)
            S1 = np.zeros((128, 128), dtype=np.float32)
            for cl in range(min(128, n_child - ci * 128)):
                cg = ci * 128 + cl
                q = cg // 2
                b = cg % 2
                (S1 if b else S0)[q % 128, cl] = 1.0
            smeta.append((ct0 + ci, pt0 + (ci * 64) // 128))
            mats0.append(S0)
            mats1.append(S1)
    return smeta, (np.concatenate(mats0, 1).astype(BF),
                   np.concatenate(mats1, 1).astype(BF))


TREE_SMETA, (_S0ALL, _S1ALL) = _tree_scatter()
N_SC = len(TREE_SMETA)


def _band_patch():
    Cm = np.zeros((128, 146), dtype=np.float32)
    for i in range(128):
        for w in range(146):
            delta = i - (w - 9)
            Cm[i, w] = -0.01 * (10.0 - min(abs(delta), 10.0))
    return Cm


def host_prep(inp):
    f32 = np.float32
    cont = np.ascontiguousarray
    c = {}
    _ = inp["L1_attn_Wk"]
    c["Wn"] = cont(inp["L0_attn_W"].astype(f32))
    c["b_col"] = cont(inp["L0_attn_b"].astype(f32).reshape(8, 128).T)
    c["w1T"] = cont(inp["L0_ffn_w1"].astype(f32).T)
    c["b1_col"] = cont(inp["L0_ffn_b1"].astype(f32).reshape(32, 128).T)
    c["w2T"] = cont(inp["L0_ffn_w2"].astype(f32).T)
    c["b2_row"] = cont(inp["L0_ffn_b2"].astype(f32).reshape(1, 1024))
    c["WvT"] = cont(inp["L1_attn_Wv"].astype(f32).T)
    c["WoT"] = cont(inp["L1_attn_Wo"].astype(f32).T)
    c["fr_col"] = cont(inp["L1_attn_freqs"].astype(f32).reshape(8, 128).T)
    slot = _tree_perm()
    used = slot >= 0
    WinP = np.zeros((JW, D), dtype=f32)
    binP = np.zeros(JW, dtype=f32)
    WoutP = np.zeros((D, JW), dtype=f32)
    WinP[used] = inp["L1_ffn_W_in"].astype(f32)[slot[used]]
    binP[used] = inp["L1_ffn_b_in"].astype(f32)[slot[used]]
    WoutP[:, used] = inp["L1_ffn_W_out"].astype(f32)[:, slot[used]]
    c["WinT"] = cont(WinP.T)
    c["bin_col"] = cont(binP.reshape(NJT, 128).T)
    c["WoutT"] = cont(WoutP.T)
    c["ident"] = np.eye(128, dtype=f32).astype(BF)
    c["eps_col"] = np.full((128, 1), 1e-5, dtype=f32)
    c["band"] = _band_patch()
    c["iota128_row"] = np.arange(128, dtype=f32).reshape(1, 128)
    c["iota32_row"] = np.arange(32, dtype=f32).reshape(1, 32)
    t_idx = np.arange(T)
    c["brow"] = (t_idx // 128).astype(BF).reshape(1, T)
    c["urow"] = (t_idx % 128).astype(BF).reshape(1, T)
    c["S0all"] = _S0ALL
    c["S1all"] = _S1ALL
    per_core = [{"x": cont(inp["L0_attn_x"].astype(f32)[i])} for i in range(8)]
    return c, per_core


CONST_SPECS = {
    "Wn": ([1024, 1024], F32), "b_col": ([128, 8], F32),
    "w1T": ([1024, 4096], F32), "b1_col": ([128, 32], F32),
    "w2T": ([4096, 1024], F32), "b2_row": ([1, 1024], F32),
    "WvT": ([1024, 1024], F32), "WoT": ([1024, 1024], F32),
    "fr_col": ([128, 8], F32), "WinT": ([1024, JW], F32),
    "bin_col": ([128, NJT], F32), "WoutT": ([JW, 1024], F32),
    "ident": ([128, 128], BF16), "band": ([128, 146], F32),
    "eps_col": ([128, 1], F32),
    "iota128_row": ([1, 128], F32), "iota32_row": ([1, 32], F32),
    "brow": ([1, T], BF16), "urow": ([1, T], BF16),
    "S0all": ([128, N_SC * 128], BF16), "S1all": ([128, N_SC * 128], BF16),
}



class Sem:
    __slots__ = ("h", "n")

    def __init__(self, nc, es, name):
        self.h = es.enter_context(nc.semaphore(name))
        self.n = 0

    def inc(self, inst, k=1):
        inst.then_inc(self.h, k)
        self.n += k
        return self.n

    def dinc(self, inst):
        return self.inc(inst, 16)


class DStream:

    def __init__(self, ctx, name, nbuf=2):
        self.sems = [ctx.S(name + str(j)) for j in range(nbuf)]
        self.nbuf = nbuf
        self.count = 0

    def issue(self, inst):
        i = self.count
        self.sems[i % self.nbuf].inc(inst, 16)
        self.count += 1
        return i

    def wait_done(self, eng, i):
        if i < 0:
            return
        eng.wait_ge(self.sems[i % self.nbuf].h, 16 * (i // self.nbuf + 1))

    def wait_all(self, eng):
        for s in self.sems:
            if s.n:
                eng.wait_ge(s.h, s.n)


_SS_WRAP = {
    "tensor_copy", "tensor_tensor", "tensor_add", "tensor_sub", "tensor_mul",
    "tensor_max", "tensor_scalar", "tensor_single_scalar", "tensor_scalar_add",
    "tensor_scalar_sub", "tensor_scalar_mul", "tensor_scalar_max",
    "tensor_scalar_min", "tensor_relu", "tensor_reduce", "reduce_max",
    "reduce_sum", "scalar_tensor_tensor", "tensor_tensor_scan", "memset",
    "memzero", "reciprocal", "select", "transpose", "activation", "copy",
    "sign", "square", "sqrt", "mul", "add", "engine_nop", "bn_stats",
    "bn_aggr", "stream_shuffle", "cody_waite_cascade", "add_range_wrap",
    "affine_then_add", "affine_mul_reduce", "tensor_tensor_reduce",
}


class SS:

    def __init__(self, raw, sem):
        self._r = raw
        self._s = sem

    def __getattr__(self, name):
        f = getattr(self._r, name)
        if name in _SS_WRAP:
            def w(*a, **k):
                if self._s.n:
                    self._r.wait_ge(self._s.h, self._s.n)
                ins = f(*a, **k)
                self._s.inc(ins)
                return ins
            return w
        return f


class Chain:

    def __init__(self, raw):
        self.raw = raw
        self.prev = None

    def wait_ge(self, h, n):
        self.raw.wait_ge(h, n)

    def do(self, sem, f):
        if self.prev is not None:
            ps, pn = self.prev
            self.raw.wait_ge(ps.h, pn)
        ins = f(self.raw)
        sem.inc(ins)
        self.prev = (sem, sem.n)
        return ins


class Ctx:
    def __init__(self, nc, es, block):
        self.nc = nc
        self.es = es
        self.block = block
        self.t = {}
        self.sem = {}
        self.store_streams = []

    def S(self, name):
        if name not in self.sem:
            self.sem[name] = Sem(self.nc, self.es, name)
        return self.sem[name]

    def sb(self, name, shape, dtype, es=None, side="left"):
        h = (es or self.es).enter_context(
            self.nc.sbuf_tensor(name, shape, dtype, side=side))
        self.t[name] = h
        return h

    def ps(self, name, shape, dtype, es=None):
        h = (es or self.es).enter_context(self.nc.psum_tensor(name, shape, dtype))
        self.t[name] = h
        return h


def barrier(ctx):
    bl = ctx.block
    sems = [ctx.S("bar_" + e) for e in ("sy", "te", "ve", "se", "ge")]
    all_waits = [(s.h, s.n) for nm, s in ctx.sem.items()
                 if s.n and not nm.startswith("bar_")]

    def mk(idx):
        def f(eng):
            for h, n in all_waits:
                eng.wait_ge(h, n)
            sems[idx].inc(eng.nop())
        return f
    bl.sync(mk(0)); bl.tensor(mk(1)); bl.vector(mk(2)); bl.scalar(mk(3)); bl.gpsimd(mk(4))
    targets = [s.n for s in sems]

    def mkw(my_idx):
        def f(eng):
            for i, s in enumerate(sems):
                if i != my_idx:
                    eng.wait_ge(s.h, targets[i])
        return f
    bl.sync(mkw(0)); bl.tensor(mkw(1)); bl.vector(mkw(2)); bl.scalar(mkw(3)); bl.gpsimd(mkw(4))


def emit_preload(ctx, names, es=None):
    bl = ctx.block
    LDC = ctx.S("ldc")
    table = {
        "ident": ("ident_sb", [128, 128], BF16, False),
        "eps_col": ("eps_col_sb", [128, 1], F32, False),
        "band": ("band_sb", [128, 146], F32, False),
        "b_col": ("b_col_sb", [128, 8], F32, False),
        "b1_col": ("b1_col_sb", [128, 32], F32, False),
        "bin_col": ("bin_col_sb", [128, NJT], F32, False),
        "fr_col": ("fr_col_sb", [128, 8], F32, False),
        "S0all": ("S0_sb", [128, N_SC * 128], BF16, False),
        "S1all": ("S1_sb", [128, N_SC * 128], BF16, False),
        "iota128_row": ("iota128_sb", [128, 128], F32, True),
        "iota32_row": ("iota32_sb", [128, 32], F32, True),
        "b2_row": ("b2bc_sb", [128, 1024], F32, True),
        "brow": ("brow_sb", [128, T], BF16, True),
        "urow": ("urow_sb", [128, T], BF16, True),
    }
    plans = []
    for nm in names:
        sbn, shape, dt, bc = table[nm]
        if sbn not in ctx.t:
            ctx.sb(sbn, shape, dt, es)
        plans.append((sbn, nm, bc))

    def sec(sync):
        for sbn, nm, bc in plans:
            src = ctx.t["d_" + nm]
            if bc:
                ins = sync.dma_start(out=ctx.t[sbn][:],
                                     in_=src[0:1, :].partition_broadcast(128))
            else:
                ins = sync.dma_start(out=ctx.t[sbn][:], in_=src)
            LDC.dinc(ins)
    bl.sync(sec)




def emit_ln(ctx, tag, src_dram, dst_name, spill_dram, es, dst_es=None,
            dst_side="left"):
    bl = ctx.block
    DEP = 4
    dst = ctx.sb(dst_name, [128, NKT * T], BF16, dst_es or es, side=dst_side)
    xb = ctx.sb(tag + "_x", [128, DEP * D], F32, es)
    xnT = ctx.sb(tag + "_xnT", [128, DEP * D], BF16, es)
    st = ctx.sb(tag + "_st", [128, 16 * DEP], F32, es)
    pT = ctx.ps(tag + "_pT", [128, DEP * 1024], BF16, es)
    ident = ctx.t["ident_sb"]
    ldc_target = ctx.S("ldc").n

    LD = [DStream(ctx, tag + "ldx"), DStream(ctx, tag + "ldy")]
    VE = ctx.S(tag + "ve"); SE = ctx.S(tag + "se"); TEs = ctx.S(tag + "te")
    if not hasattr(ctx, "spill_ds"):
        ctx.spill_ds = DStream(ctx, "spill", nbuf=4)
    SPD = ctx.spill_ds
    xnc = {}; agc = {}; rsc = {}; cpc = {}; tec = {}; stc = {}

    dstv = dst[:].rearrange("p (k t) -> p k t", k=NKT)

    for i in range(NTT + 2):
        s = i % DEP

        if i < NTT:
            def sec_load(eng, i=i, s=s):
                if i >= DEP:
                    eng.wait_ge(VE.h, xnc[i - DEP])
                LD[i % 2].issue(eng.dma_start(
                    out=xb[:, s * D:(s + 1) * D],
                    in_=src_dram[i * 128:(i + 1) * 128, :]))
            (bl.sync if i % 2 == 0 else bl.gpsimd)(sec_load)

        def sec_ve(vector, i=i):
            vector = SS(vector, VE)
            if i < NTT:
                s = i % DEP
                c = st[:, s * 16:s * 16 + 16]
                LD[i % 2].wait_done(vector, i // 2)
                if i >= DEP:
                    vector.wait_ge(SE.h, rsc[i - DEP])
                vector.bn_stats(out=c[:, 0:6], in_=xb[:, s * D:s * D + 512])
                vector.bn_stats(out=c[:, 6:12], in_=xb[:, s * D + 512:s * D + 1024])
                vector.bn_aggr(out=c[:, 12:14], in_=c[:, 0:12])
                agc[i] = VE.n
            if 1 <= i <= NTT:
                j = i - 1
                sj = j % DEP
                c = st[:, sj * 16:sj * 16 + 16]
                vector.wait_ge(SE.h, rsc[j])
                vector.reciprocal(out=c[:, 15:16], in_=c[:, 14:15])
                vector.tensor_scalar(out=xnT[:, sj * D:(sj + 1) * D],
                                     in0=xb[:, sj * D:(sj + 1) * D],
                                     scalar1=c[:, 12:13], scalar2=c[:, 15:16],
                                     op0=ALU.subtract, op1=ALU.mult)
                xnc[j] = VE.n
            if i >= 2 and (i - 2) % 2 == 1:
                j = i - 2
                sj = j % DEP
                vector.wait_ge(TEs.h, tec[j])
                vector.tensor_copy(out=dstv[:, :, j * 128:(j + 1) * 128],
                                   in_=pT[:, sj * 1024:(sj + 1) * 1024].rearrange(
                                       "p (k t) -> p k t", k=NKT))
                cpc[j] = ("ve", VE.n)
        bl.vector(sec_ve)

        if i >= 1 and i - 1 < NTT:
            def sec_pe(tensor, i=i):
                j = i - 1
                sj = j % DEP
                if j == 0:
                    tensor.wait_ge(ctx.S("ldc").h, ldc_target)
                tensor.wait_ge(VE.h, xnc[j])
                if j >= DEP:
                    eng, n = cpc[j - DEP]
                    tensor.wait_ge(VE.h if eng == "ve" else SE.h, n)
                ins = None
                for k in range(NKT):
                    ins = tensor.transpose(
                        pT[:, sj * 1024 + k * 128:sj * 1024 + (k + 1) * 128],
                        xnT[:, sj * D + k * 128:sj * D + (k + 1) * 128],
                        ident[:])
                TEs.inc(ins)
                tec[j] = TEs.n
            bl.tensor(sec_pe)

        def sec_se(scalar, i=i):
            scalar = SS(scalar, SE)
            if i < NTT:
                s = i % DEP
                if i == 0:
                    scalar.wait_ge(ctx.S("ldc").h, ldc_target)
                scalar.wait_ge(VE.h, agc[i])
                scalar.activation(out=st[:, s * 16 + 14:s * 16 + 15],
                                  in_=st[:, s * 16 + 13:s * 16 + 14],
                                  func=AF.Sqrt, bias=ctx.t["eps_col_sb"][:, 0:1])
                rsc[i] = SE.n
            if i >= 2 and (i - 2) % 2 == 0:
                j = i - 2
                sj = j % DEP
                scalar.wait_ge(TEs.h, tec[j])
                scalar.copy(out=dstv[:, :, j * 128:(j + 1) * 128],
                            in_=pT[:, sj * 1024:(sj + 1) * 1024].rearrange(
                                "p (k t) -> p k t", k=NKT))
                cpc[j] = ("se", SE.n)
        bl.scalar(sec_se)

    if spill_dram is not None:
        spillv = spill_dram.rearrange("p (k t) -> p k t", k=NKT)
        chunk_ids = []

        def sec_spill(sync):
            for q in range(NCH):
                for j in (4 * q + 2, 4 * q + 3):
                    eng, n = cpc[j]
                    sync.wait_ge(VE.h if eng == "ve" else SE.h, n)
                chunk_ids.append(SPD.issue(
                    sync.dma_start(out=spillv[:, :, q * 512:(q + 1) * 512],
                                   in_=dstv[:, :, q * 512:(q + 1) * 512])))
        bl.sync(sec_spill)
        ctx.spill_chunks = chunk_ids
    return dst




def emit_fproj(ctx, xnC_dram, es, dst_es, psum=None):
    bl = ctx.block
    fC = ctx.sb("fC", [128, NKT * T], BF16, dst_es, side="right")
    fT = ctx.sb("fT", [128, NTT * 1024], BF16, dst_es, side="right")
    Wb = ctx.sb("Wb", [128, NKT * 1024], BF16, es, side="right")
    xchunk = ctx.sb("b_xc", [128, 2 * NKT * 512], BF16, es, side="right")
    if psum is None:
        pf = ctx.ps("b_pf", [128, 2 * 512], F32, es)
        pT = ctx.ps("b_pT", [128, 2 * 1024], BF16, es)
    else:
        pf, pT = psum
    ident = ctx.t["ident_sb"]
    b_col = ctx.t["b_col_sb"]
    dWn = ctx.t["d_Wn"]
    xnCv_dram = xnC_dram.rearrange("p (k t) -> p k t", k=NKT)
    ldc_target = ctx.S("ldc").n

    LG = ctx.S("b_lg"); LXC = DStream(ctx, "b_lxc"); TE = ctx.S("b_te"); SE = ctx.S("b_se")
    T2 = ctx.S("b_t2"); V2 = ctx.S("b_v2")

    def sec_wload(gpsimd):
        for k in range(NKT):
            LG.dinc(gpsimd.dma_start(out=Wb[:, k * 1024:(k + 1) * 1024],
                                     in_=dWn[k * 128:(k + 1) * 128, :]))
    bl.gpsimd(sec_wload)

    def sec_xcload(sync):
        for q in range(NCH):
            ctx.spill_ds.wait_done(sync, ctx.spill_chunks[q])
            if q >= 2:
                sync.wait_ge(TE.h, (q - 1) * NKT)
            LXC.issue(sync.dma_start(
                out=xchunk[:, (q % 2) * NKT * 512:(q % 2 + 1) * NKT * 512].rearrange(
                    "p (k t) -> p k t", k=NKT),
                in_=xnCv_dram[:, :, q * 512:(q + 1) * 512]))
    bl.sync(sec_xcload)

    def sec_mm(tensor):
        tensor.wait_ge(LG.h, 16 * NKT)
        tensor.wait_ge(ctx.S("ldc").h, ldc_target)
        for q in range(NCH):
            LXC.wait_done(tensor, q)
            for j in range(NKT):
                r = q * NKT + j
                if r >= 2:
                    tensor.wait_ge(SE.h, r - 1)
                ins = None
                for k in range(NKT):
                    ins = tensor.matmul(pf[:, (r % 2) * 512:(r % 2 + 1) * 512],
                                        Wb[:, k * 1024 + j * 128:k * 1024 + (j + 1) * 128],
                                        xchunk[:, (q % 2) * NKT * 512 + k * 512:(q % 2) * NKT * 512 + (k + 1) * 512],
                                        start=(k == 0), stop=(k == NKT - 1))
                TE.inc(ins)
        tensor.wait_ge(SE.h, NKT * NCH)
        for i in range(NTT):
            if i >= 2:
                tensor.wait_ge(V2.h, i - 1)
            ins = None
            for k in range(NKT):
                ins = tensor.transpose(pT[:, (i % 2) * 1024 + k * 128:(i % 2) * 1024 + (k + 1) * 128],
                                       fC[:, k * T + i * 128:k * T + (i + 1) * 128],
                                       ident[:])
            T2.inc(ins)
    bl.tensor(sec_mm)

    def sec_evac(scalar):
        scalar = SS(scalar, SE)
        for q in range(NCH):
            for j in range(NKT):
                r = q * NKT + j
                scalar.wait_ge(TE.h, r + 1)
                scalar.activation(out=fC[:, j * T + q * 512:j * T + (q + 1) * 512],
                                  in_=pf[:, (r % 2) * 512:(r % 2 + 1) * 512],
                                  func=AF.Identity, bias=b_col[:, j:j + 1])
    bl.scalar(sec_evac)

    def sec_trcopy(vector):
        vector = SS(vector, V2)
        for i in range(NTT):
            vector.wait_ge(T2.h, i + 1)
            vector.tensor_copy(out=fT[:, i * 1024:(i + 1) * 1024],
                               in_=pT[:, (i % 2) * 1024:(i % 2 + 1) * 1024])
    bl.vector(sec_trcopy)
    return fC, fT




def emit_attn0(ctx, x_dram, x1p_dram, es):
    bl = ctx.block
    fC, fT = ctx.t["fC"], ctx.t["fT"]
    e_sb = ctx.sb("e_sb", [128, 2 * T], F32, es)
    a_sb = ctx.sb("a_sb", [128, 2 * T], BF16, es)
    aT_sb = ctx.sb("aT_sb", [128, T], BF16, es)
    xr = ctx.sb("c_xr", [128, 2 * D], F32, es)
    x1p = ctx.sb("c_x1p", [128, 2 * D], F32, es)
    cst = ctx.sb("c_st", [128, 4], F32, es)
    inv = ctx.sb("c_inv", [128, 2], F32, es)
    pe_ = ctx.ps("c_pe", [128, 2 * 512], F32, es)
    pA = ctx.ps("c_pA", [128, 2 * 1024], BF16, es)
    pO = ctx.ps("c_pO", [128, 2 * 1024], F32, es)
    band = ctx.t["band_sb"]
    b2bc = ctx.t["b2bc_sb"]
    ident = ctx.t["ident_sb"]

    TEs = ctx.S("c_te"); EVa = ctx.S("c_eva"); EVb = ctx.S("c_evb"); GE = ctx.S("c_ge")
    SEx = ctx.S("c_se"); VEi = ctx.S("c_vi"); TT_ = ctx.S("c_tt")
    TEV = ctx.S("c_tev"); TO = ctx.S("c_to"); OV = ctx.S("c_ov")
    VEm = ctx.S("c_vm")
    ldc_c = ctx.S("ldc").n
    LXR = DStream(ctx, "c_lxr"); SP1 = DStream(ctx, "c_sp1")
    ctx.store_streams.append(SP1)

    def sec_pe(tensor):
        for s in range(NTT + 1):
            if s < NTT:
                for q in range(NCH):
                    r = s * NCH + q
                    if r >= 2:
                        if r % 2 == 0:
                            tensor.wait_ge(EVa.h, r // 2)
                        else:
                            tensor.wait_ge(EVb.h, (r - 1) // 2)
                    ins = None
                    for k in range(NKT):
                        ins = tensor.matmul(pe_[:, (r % 2) * 512:(r % 2 + 1) * 512],
                                            fC[:, k * T + s * 128:k * T + (s + 1) * 128],
                                            fC[:, k * T + q * 512:k * T + (q + 1) * 512],
                                            start=(k == 0), stop=(k == NKT - 1))
                    TEs.inc(ins)
            if s >= 1:
                sp = s - 1
                tensor.wait_ge(SEx.h, sp + 1)
                for g in range(4):
                    gi = sp * 4 + g
                    if gi >= 2:
                        tensor.wait_ge(TEV.h, gi - 1)
                    ins = None
                    for u in range(8):
                        i = g * 8 + u
                        ins = tensor.transpose(
                            pA[:, (gi % 2) * 1024 + u * 128:(gi % 2) * 1024 + (u + 1) * 128],
                            a_sb[:, (sp % 2) * T + i * 128:(sp % 2) * T + (i + 1) * 128],
                            ident[:])
                    TT_.inc(ins)
                    if g >= 1:
                        tensor.wait_ge(TEV.h, sp * 4 + g)
                        if g == 1 and sp >= 2:
                            tensor.wait_ge(OV.h, 2 * (sp - 1))
                        for q2 in range(2):
                            for i in range((g - 1) * 8, g * 8):
                                tensor.matmul(
                                    pO[:, (sp % 2) * 1024 + q2 * 512:(sp % 2) * 1024 + (q2 + 1) * 512],
                                    aT_sb[:, i * 128:(i + 1) * 128],
                                    fT[:, i * 1024 + q2 * 512:i * 1024 + (q2 + 1) * 512],
                                    start=(i == 0), stop=False)
                tensor.wait_ge(TEV.h, sp * 4 + 4)
                for q2 in range(2):
                    ins = None
                    for i in range(24, NTT):
                        ins = tensor.matmul(
                            pO[:, (sp % 2) * 1024 + q2 * 512:(sp % 2) * 1024 + (q2 + 1) * 512],
                            aT_sb[:, i * 128:(i + 1) * 128],
                            fT[:, i * 1024 + q2 * 512:i * 1024 + (q2 + 1) * 512],
                            start=False, stop=(i == NTT - 1))
                    TO.inc(ins)
    bl.tensor(sec_pe)

    def sec_pool(gpsimd):
        gpsimd = SS(gpsimd, GE)
        gpsimd.wait_ge(ctx.S("ldc").h, ldc_c)
        for s in range(NTT):
            gpsimd.wait_ge(EVa.h, (s + 1) * 4)
            gpsimd.wait_ge(EVb.h, (s + 1) * 4)
            base = (s % 2) * T
            lo = s * 128 - 9
            if s == 0:
                gpsimd.tensor_add(out=e_sb[:, base:base + 137],
                                  in0=e_sb[:, base:base + 137], in1=band[:, 9:146])
            elif s == NTT - 1:
                gpsimd.tensor_add(out=e_sb[:, base + lo:base + T],
                                  in0=e_sb[:, base + lo:base + T], in1=band[:, 0:137])
            else:
                gpsimd.tensor_add(out=e_sb[:, base + lo:base + lo + 146],
                                  in0=e_sb[:, base + lo:base + lo + 146], in1=band[:])
    bl.gpsimd(sec_pool)

    def sec_act(scalar):
        ch = Chain(scalar)

        def emit_exp(s):
            ch.wait_ge(VEm.h, s + 1)
            if s >= 2:
                ch.wait_ge(TT_.h, (s - 1) * 4)
                ch.wait_ge(VEi.h, s - 1)
            ch.do(SEx, lambda sc, s=s: sc.activation(
                out=a_sb[:, (s % 2) * T:(s % 2 + 1) * T],
                in_=e_sb[:, (s % 2) * T:(s % 2 + 1) * T],
                func=AF.Exp, bias=cst[:, (s % 2):(s % 2) + 1],
                accum_out=cst[:, 2 + (s % 2):3 + (s % 2)]))

        def emit_tevc(sp):
            for g in range(4):
                gi = sp * 4 + g
                ch.wait_ge(TT_.h, gi + 1)
                if sp >= 1 and g == 0:
                    ch.wait_ge(TO.h, 2 * sp)
                ch.do(TEV, lambda sc, g=g, gi=gi: sc.copy(
                    out=aT_sb[:, g * 1024:(g + 1) * 1024],
                    in_=pA[:, (gi % 2) * 1024:(gi % 2 + 1) * 1024]))

        for s in range(NTT):
            if s >= 1:
                emit_exp(s - 1)
            for q in range(0, NCH, 2):
                r = s * NCH + q
                ch.wait_ge(TEs.h, r + 1)
                if s >= 2 and q == 0:
                    ch.wait_ge(SEx.h, s - 1)
                ch.do(EVa, lambda sc, s=s, q=q, r=r: sc.copy(
                    out=e_sb[:, (s % 2) * T + q * 512:(s % 2) * T + (q + 1) * 512],
                    in_=pe_[:, (r % 2) * 512:(r % 2 + 1) * 512]))
            if s >= 1:
                emit_tevc(s - 1)
        emit_exp(NTT - 1)
        emit_tevc(NTT - 1)
    bl.scalar(sec_act)

    def sec_dve(vector):
        ch = Chain(vector)
        ch.wait_ge(ctx.S("ldc").h, ldc_c)

        def emit_tail(sp):
            ch.wait_ge(SEx.h, sp + 1)
            ch.do(VEi, lambda v, sp=sp: v.reciprocal(
                out=inv[:, (sp % 2):(sp % 2) + 1],
                in_=cst[:, 2 + (sp % 2):3 + (sp % 2)]))
            ch.wait_ge(TO.h, 2 * sp + 2)
            LXR.wait_done(ch, sp)
            SP1.wait_done(ch, sp - 2)
            ch.do(OV, lambda v, sp=sp: v.scalar_tensor_tensor(
                out=x1p[:, (sp % 2) * D:(sp % 2 + 1) * D],
                in0=pO[:, (sp % 2) * 1024:(sp % 2 + 1) * 1024],
                scalar=inv[:, (sp % 2):(sp % 2) + 1],
                in1=xr[:, (sp % 2) * D:(sp % 2 + 1) * D],
                op0=ALU.mult, op1=ALU.add))
            ch.do(OV, lambda v, sp=sp: v.tensor_add(
                out=x1p[:, (sp % 2) * D:(sp % 2 + 1) * D],
                in0=x1p[:, (sp % 2) * D:(sp % 2 + 1) * D],
                in1=b2bc[:]))

        for s in range(NTT):
            for q in range(1, NCH, 2):
                r = s * NCH + q
                ch.wait_ge(TEs.h, r + 1)
                if s >= 2 and q == 1:
                    ch.wait_ge(SEx.h, s - 1)
                ch.do(EVb, lambda v, s=s, q=q, r=r: v.tensor_copy(
                    out=e_sb[:, (s % 2) * T + q * 512:(s % 2) * T + (q + 1) * 512],
                    in_=pe_[:, (r % 2) * 512:(r % 2 + 1) * 512]))
            ch.wait_ge(GE.h, s + 1)
            ch.do(VEm, lambda v, s=s: v.reduce_max(
                out=cst[:, (s % 2):(s % 2) + 1],
                in_=e_sb[:, (s % 2) * T:(s % 2) * T + T], axis=AX.X, negate=True))
            if s >= 1:
                emit_tail(s - 1)
        emit_tail(NTT - 1)
    bl.vector(sec_dve)

    def sec_sync(sync):
        def store(sp):
            sync.wait_ge(OV.h, 2 * (sp + 1))
            SP1.issue(sync.dma_start(out=x1p_dram[sp * 128:(sp + 1) * 128, :],
                                     in_=x1p[:, (sp % 2) * D:(sp % 2 + 1) * D]))
        for s in range(NTT):
            if s >= 2:
                sync.wait_ge(OV.h, 2 * (s - 1))
            LXR.issue(sync.dma_start(out=xr[:, (s % 2) * D:(s % 2 + 1) * D],
                                     in_=x_dram[s * 128:(s + 1) * 128, :]))
            if s >= 2:
                store(s - 2)
        store(NTT - 2)
        store(NTT - 1)
        SP1.wait_all(sync)
    bl.sync(sec_sync)




def emit_ffn0(ctx, xnC_dram, x1p_dram, x1_dram, es):
    bl = ctx.block
    sw1T = ctx.sb("sw1T", [128, NKT * T], BF16, es)
    sw2T = ctx.sb("sw2T", [128, NTT * 1024], BF16, es)
    xchunk = ctx.sb("d_xc", [128, 2 * NKT * 512], BF16, es)
    x1p = ctx.sb("d_x1p", [128, 2 * D], F32, es)
    x1o = ctx.sb("d_x1o", [128, 2 * D], F32, es)
    a2bc = ctx.sb("d_a2bc", [128, 1024], BF16, es)
    a1col = ctx.sb("d_a1col", [128, NTT], F32, es)
    es_prep = contextlib.ExitStack()
    wraw = ctx.sb("d_wraw", [128, 4 * T], BF16, es_prep)
    a1bc = ctx.sb("d_a1bc", [128, T], BF16, es_prep)
    ones128 = ctx.sb("d_ones", [128, 128], BF16, es_prep)
    pAw = ctx.ps("d_pAw", [128, NCH * 512], F32, es_prep)
    b1_col = ctx.t["b1_col_sb"]
    dw1T, dw2T = ctx.t["d_w1T"], ctx.t["d_w2T"]
    xnCv_dram = xnC_dram.rearrange("p (k t) -> p k t", k=NKT)

    LW = ctx.S("d_lw"); SE = ctx.S("d_se"); TE = ctx.S("d_te"); VE = ctx.S("d_ve")
    LXC = ctx.S("d_lxc"); LX1 = DStream(ctx, "d_lx1"); SX1 = DStream(ctx, "d_sx1")
    ctx.store_streams.append(SX1)
    TH = ctx.S("d_th"); SH = ctx.S("d_sh"); TO = ctx.S("d_to"); OV = ctx.S("d_ov")

    def sec_ones(vector):
        vector = SS(vector, VE)
        vector.memset(ones128[:], 1.0)
    bl.vector(sec_ones)

    def sec_wload(gpsimd):
        for k in range(NKT):
            if k >= 4:
                gpsimd.wait_ge(TE.h, (k - 3) * NCH)
            LW.dinc(gpsimd.dma_start(out=wraw[:, (k % 4) * T:(k % 4 + 1) * T],
                                     in_=dw1T[k * 128:(k + 1) * 128, :]))
        for jt in range(NTT):
            r = NKT + jt
            if jt < 4:
                gpsimd.wait_ge(TE.h, NKT * NCH)
            else:
                gpsimd.wait_ge(TE.h, NKT * NCH + (jt - 3) * 2)
            LW.dinc(gpsimd.dma_start(out=wraw[:, (r % 4) * T:(r % 4) * T + 1024],
                                     in_=dw2T[jt * 128:(jt + 1) * 128, :]))
    bl.gpsimd(sec_wload)

    def sec_wact(scalar):
        scalar = SS(scalar, SE)
        for k in range(NKT):
            w = wraw[:, (k % 4) * T:(k % 4 + 1) * T]
            scalar.wait_ge(LW.h, 16 * (k + 1))
            scalar.activation(out=sw1T[:, k * T:(k + 1) * T], in_=w,
                              func=AF.Sign)
            scalar.activation(out=w, in_=w, func=AF.Abs)
        for jt in range(NTT):
            r = NKT + jt
            w = wraw[:, (r % 4) * T:(r % 4) * T + 1024]
            scalar.wait_ge(LW.h, 16 * (NKT + jt + 1))
            scalar.activation(out=sw2T[:, jt * 1024:(jt + 1) * 1024],
                              in_=w, func=AF.Sign)
            scalar.activation(out=w, in_=w, func=AF.Abs)
    bl.scalar(sec_wact)

    def sec_alpha_mm(tensor):
        tensor.wait_ge(VE.h, 1)
        for k in range(NKT):
            tensor.wait_ge(SE.h, 2 * k + 2)
            for jc in range(NCH):
                TE.inc(tensor.matmul(pAw[:, jc * 512:(jc + 1) * 512], ones128[:],
                                     wraw[:, (k % 4) * T + jc * 512:(k % 4) * T + (jc + 1) * 512],
                                     start=(k == 0), stop=(k == NKT - 1)))
        for jt in range(NTT):
            r = NKT + jt
            tensor.wait_ge(SE.h, 2 * NKT + 2 * jt + 2)
            if jt == 0:
                tensor.wait_ge(VE.h, 2)
            ins = None
            for dc in range(2):
                ins = tensor.matmul(pAw[:, dc * 512:(dc + 1) * 512], ones128[:],
                                    wraw[:, (r % 4) * T + dc * 512:(r % 4) * T + (dc + 1) * 512],
                                    start=(jt == 0), stop=(jt == NTT - 1))
            TE.inc(ins, 2)
    bl.tensor(sec_alpha_mm)

    def sec_alpha_ev(vector):
        vector = SS(vector, VE)
        vector.wait_ge(TE.h, NKT * NCH)
        vector.tensor_scalar(out=a1bc[:], in0=pAw[:], scalar1=1.0 / D,
                             scalar2=None, op0=ALU.mult)
        vector.wait_ge(TE.h, NKT * NCH + NTT * 2)
        vector.tensor_scalar(out=a2bc[:], in0=pAw[:, 0:1024], scalar1=1.0 / T,
                             scalar2=None, op0=ALU.mult)
    bl.vector(sec_alpha_ev)

    A1L = ctx.S("d_a1l")

    def sec_a1x(gpsimd):
        gpsimd.wait_ge(VE.h, 2)
        A1L.dinc(gpsimd.dma_start(out=ctx.t["d_a1s"], in_=a1bc[0:1, :]))
        gpsimd.wait_ge(A1L.h, 16)
        A1L.dinc(gpsimd.dma_start(
            out=a1col[:],
            in_=ctx.t["d_a1s"].rearrange("(j p) -> p j", p=128),
            allow_slow_non_contiguous=True))
    bl.gpsimd(sec_a1x)
    VE_PREP = ctx.S("d_ve").n
    es_prep.close()
    hC = ctx.sb("d_hC", [128, NTT * 512], BF16, es)

    pH = ctx.ps("d_pH", [128, 2 * 512], F32, es)
    pO = ctx.ps("d_pO", [128, 2 * 1024], F32, es)

    ldc_d = ctx.S("ldc").n

    def sec_sync(sync):
        sync.wait_ge(ctx.S("ldc").h, ldc_d)
        ctx.spill_ds.wait_all(sync)

        def store(i):
            sync.wait_ge(OV.h, i + 1)
            s_tok = (i // 4) * 512 + (i % 4) * 128
            SX1.issue(sync.dma_start(out=x1_dram[s_tok:s_tok + 128, :],
                                     in_=x1o[:, (i % 2) * D:(i % 2 + 1) * D]))
        for q in range(NCH):
            if q >= 2:
                sync.wait_ge(TH.h, (q - 1) * NTT)
            LXC.dinc(sync.dma_start(
                out=xchunk[:, (q % 2) * NKT * 512:(q % 2 + 1) * NKT * 512].rearrange(
                    "p (k t) -> p k t", k=NKT),
                in_=xnCv_dram[:, :, q * 512:(q + 1) * 512]))
            for sub in range(4):
                i = q * 4 + sub
                if i >= 2:
                    sync.wait_ge(OV.h, i - 1)
                s_tok = q * 512 + sub * 128
                LX1.issue(sync.dma_start(out=x1p[:, (i % 2) * D:(i % 2 + 1) * D],
                                         in_=x1p_dram[s_tok:s_tok + 128, :]))
                if i >= 2:
                    store(i - 2)
        store(4 * NCH - 2)
        store(4 * NCH - 1)
        SX1.wait_all(sync)
    bl.sync(sec_sync)

    def sec_pe(tensor):
        for q in range(NCH):
            tensor.wait_ge(LXC.h, 16 * (q + 1))
            if q == 0:
                tensor.wait_ge(VE.h, VE_PREP)
                tensor.wait_ge(SE.h, 2 * NKT)
                tensor.wait_ge(A1L.h, 32)
            xoff = (q % 2) * NKT * 512
            for jt in range(NTT):
                r = q * NTT + jt
                if r >= 2:
                    tensor.wait_ge(SH.h, r - 1)
                ins = None
                for k in range(NKT):
                    ins = tensor.matmul(pH[:, (r % 2) * 512:(r % 2 + 1) * 512],
                                        sw1T[:, k * T + jt * 128:k * T + (jt + 1) * 128],
                                        xchunk[:, xoff + k * 512:xoff + (k + 1) * 512],
                                        start=(k == 0), stop=(k == NKT - 1))
                TH.inc(ins)
            tensor.wait_ge(SH.h, (q + 1) * NTT)
            for sub in range(4):
                i = q * 4 + sub
                if i >= 2:
                    tensor.wait_ge(OV.h, i - 1)
                for dc in range(2):
                    ins = None
                    for jt in range(NTT):
                        ins = tensor.matmul(
                            pO[:, (i % 2) * 1024 + dc * 512:(i % 2) * 1024 + (dc + 1) * 512],
                            hC[:, jt * 512 + sub * 128:jt * 512 + (sub + 1) * 128],
                            sw2T[:, jt * 1024 + dc * 512:jt * 1024 + (dc + 1) * 512],
                            start=(jt == 0), stop=(jt == NTT - 1))
                    TO.inc(ins)
    bl.tensor(sec_pe)

    def sec_hevac(scalar):
        scalar = SS(scalar, SH)
        for q in range(NCH):
            for jt in range(NTT):
                r = q * NTT + jt
                scalar.wait_ge(TH.h, r + 1)
                if q >= 1 and jt == 0:
                    scalar.wait_ge(TO.h, q * 8)
                scalar.activation(out=hC[:, jt * 512:(jt + 1) * 512],
                                  in_=pH[:, (r % 2) * 512:(r % 2 + 1) * 512],
                                  func=AF.Gelu, scale=a1col[:, jt:jt + 1],
                                  bias=b1_col[:, jt:jt + 1])
    bl.scalar(sec_hevac)



    def sec_oevac(vector):
        ch = Chain(vector)
        OVm = ctx.S("d_ovm")
        for q in range(NCH):
            for sub in range(4):
                i = q * 4 + sub
                ch.wait_ge(TO.h, 2 * i + 2)
                LX1.wait_done(ch, i)
                SX1.wait_done(ch, i - 2)
                ch.do(OVm, lambda v, i=i: v.tensor_mul(
                    out=x1o[:, (i % 2) * D:(i % 2 + 1) * D],
                    in0=pO[:, (i % 2) * 1024:(i % 2 + 1) * 1024],
                    in1=a2bc[:]))
                ch.do(OV, lambda v, i=i: v.tensor_add(
                    out=x1o[:, (i % 2) * D:(i % 2 + 1) * D],
                    in0=x1o[:, (i % 2) * D:(i % 2 + 1) * D],
                    in1=x1p[:, (i % 2) * D:(i % 2 + 1) * D]))
    bl.vector(sec_oevac)




def emit_rotor_prelude(ctx, es):
    bl = ctx.block
    Wvo = ctx.sb("Wvo", [128, NKT * 1024], BF16, es, side="right")
    ones_r = ctx.sb("f_ones", [128, T], BF16, es, side="right")
    gtab = ctx.sb("f_g", [128, 32], F32, es, side="right")
    hpi_col = ctx.sb("f_hpi", [128, 1], F32, es, side="right")
    thB = ctx.sb("f_thB", [128, NKT * 128], F32, es, side="right")
    uuB = ctx.sb("f_uuB", [128, NKT * 128], F32, es, side="right")
    sinB = ctx.sb("f_sinB", [128, NKT * 128], BF16, es, side="right")
    cosB16 = ctx.sb("f_cosB", [128, NKT * 128], BF16, es, side="right")
    cosB = ctx.sb("f_cosBW", [128, NKT * 128], F32, es, side="right")
    thA = ctx.sb("f_thA", [128, NKT * 32], F32, es, side="right")
    sinA = ctx.sb("f_sinA", [128, NKT * 32], BF16, es, side="right")
    cosA16 = ctx.sb("f_cosA", [128, NKT * 32], BF16, es, side="right")
    cosA = ctx.sb("f_cosAW", [128, NKT * 32], F32, es, side="right")
    iota128 = ctx.t["iota128_sb"]
    iota32 = ctx.t["iota32_sb"]
    fr_col = ctx.t["fr_col_sb"]
    dWvT = ctx.t["d_WvT"]
    LWV = ctx.S("f_lwv")
    VEs = ctx.S("f_ve")
    SEs = ctx.S("f_se")
    ldc_target = ctx.S("ldc").n

    def bc3(ap2d, mid, inner):
        if ap2d.shape[1] == inner:
            return ap2d.rearrange("p (o u) -> p o u", o=1).to_broadcast(
                [128, mid, inner])
        return ap2d.rearrange("p (b o) -> p b o", o=1).to_broadcast(
            [128, mid, inner])
    ctx.t["f_bc3"] = bc3

    def sec_wload_v(gpsimd):
        for k in range(NKT):
            LWV.dinc(gpsimd.dma_start(out=Wvo[:, k * 1024:(k + 1) * 1024],
                                      in_=dWvT[k * 128:(k + 1) * 128, :]))
    bl.gpsimd(sec_wload_v)

    def sec_g(vector):
        vector = SS(vector, VEs)
        vector.wait_ge(ctx.S("ldc").h, ldc_target)
        g, kg = gtab[:, 0:8], gtab[:, 24:32]
        vector.tensor_scalar(out=g, in0=fr_col[:], scalar1=128.0, scalar2=None,
                             op0=ALU.mult)
        vector.tensor_scalar(out=kg, in0=g, scalar1=INV2PI, scalar2=MAGIC,
                             op0=ALU.mult, op1=ALU.add)
        vector.tensor_scalar(out=kg, in0=kg, scalar1=MAGIC, scalar2=None,
                             op0=ALU.subtract)
        vector.scalar_tensor_tensor(out=g, in0=kg, scalar=-C1, in1=g,
                                    op0=ALU.mult, op1=ALU.add)
        vector.scalar_tensor_tensor(out=g, in0=kg, scalar=-C2, in1=g,
                                    op0=ALU.mult, op1=ALU.add)
        vector.memset(ones_r[:], 1.0)
        vector.memset(hpi_col[:], HALF_PI)
        thBv = thB[:].rearrange("p (k u) -> p k u", k=NKT)
        vector.tensor_mul(out=thBv, in0=bc3(fr_col[:], NKT, 128),
                          in1=bc3(iota128[:], NKT, 128))
        vector.tensor_scalar(out=uuB[:], in0=thB[:], scalar1=INV2PI,
                             scalar2=MAGIC, op0=ALU.mult, op1=ALU.add)
        vector.tensor_scalar(out=uuB[:], in0=uuB[:], scalar1=MAGIC,
                             scalar2=None, op0=ALU.subtract)
        vector.scalar_tensor_tensor(out=thB[:], in0=uuB[:], scalar=-TWOPI_F,
                                    in1=thB[:], op0=ALU.mult, op1=ALU.add)
        vector.tensor_scalar(out=cosB[:], in0=thB[:], scalar1=-PI_F,
                             scalar2=PI_F, op0=ALU.max, op1=ALU.min)
        thAv = thA[:].rearrange("p (k b) -> p k b", k=NKT)
        vector.tensor_mul(out=thAv, in0=bc3(gtab[:, 0:8], NKT, 32),
                          in1=bc3(iota32[:, 0:32], NKT, 32))
        vector.tensor_scalar(out=uuB[:, 0:NKT * 32], in0=thA[:], scalar1=INV2PI,
                             scalar2=MAGIC, op0=ALU.mult, op1=ALU.add)
        vector.tensor_scalar(out=uuB[:, 0:NKT * 32], in0=uuB[:, 0:NKT * 32],
                             scalar1=MAGIC, scalar2=None, op0=ALU.subtract)
        vector.scalar_tensor_tensor(out=thA[:], in0=uuB[:, 0:NKT * 32],
                                    scalar=-TWOPI_F,
                                    in1=thA[:], op0=ALU.mult, op1=ALU.add)
        vector.tensor_scalar(out=cosA[:], in0=thA[:], scalar1=-PI_F,
                             scalar2=PI_F, op0=ALU.max, op1=ALU.min)
    bl.vector(sec_g)
    ve_gbuild = VEs.n

    def sec_se_tab(scalar):
        scalar = SS(scalar, SEs)
        scalar.wait_ge(VEs.h, ve_gbuild)
        scalar.activation(out=sinB[:], in_=cosB[:], func=AF.Sin)
        scalar.activation(out=cosB[:], in_=cosB[:], func=AF.Abs)
        scalar.activation(out=cosB16[:], in_=cosB[:], func=AF.Sin,
                          scale=-1.0, bias=hpi_col[:, 0:1])
        scalar.activation(out=sinA[:], in_=cosA[:], func=AF.Sin)
        scalar.activation(out=cosA[:], in_=cosA[:], func=AF.Abs)
        scalar.activation(out=cosA16[:], in_=cosA[:], func=AF.Sin,
                          scale=-1.0, bias=hpi_col[:, 0:1])
    bl.scalar(sec_se_tab)
    ctx.t["f_setab_n"] = SEs.n


def emit_rotor(ctx, x1_dram, x2p_dram, rC_dram, es):
    bl = ctx.block
    xn1C = ctx.t["xn1C"]
    es_t = contextlib.ExitStack()
    Wvo = ctx.t["Wvo"]
    v_sb = ctx.sb("f_v", [128, 2 * T], BF16, es_t)
    sint = ctx.sb("f_sin", [128, T], BF16, es_t)
    cost = ctx.sb("f_cos", [128, T], BF16, es_t)
    tmp = ctx.sb("f_tmp", [128, T], BF16, es_t)
    Pb = ctx.sb("f_P", [128, T], BF16, es_t)
    Qb = ctx.sb("f_Q", [128, T], BF16, es_t)
    Pc = ctx.sb("f_Pc", [128, T], BF16, es_t)
    Qc = ctx.sb("f_Qc", [128, T], BF16, es_t)
    ones_r = ctx.t["f_ones"]
    sinA, cosA16 = ctx.t["f_sinA"], ctx.t["f_cosA"]
    sinB, cosB16 = ctx.t["f_sinB"], ctx.t["f_cosB"]
    pV = ctx.ps("f_pV", [128, 2 * 512], F32, es_t)
    dWoT = ctx.t["d_WoT"]
    bc3 = ctx.t["f_bc3"]

    LWV = ctx.S("f_lwv"); LWO2 = ctx.S("f_lwo")
    TE = ctx.S("f_te"); SEs = ctx.S("f_se"); VEs = ctx.S("f_ve")
    GEs = ctx.S("f_ge"); SRC = ctx.S("f_src")
    se_tab = ctx.t["f_setab_n"]
    se_base = SEs.n
    te_base = TE.n

    def sec_vmm(tensor):
        tensor.wait_ge(LWV.h, 16 * NKT)
        for k in range(NKT):
            for q in range(NCH):
                m = k * NCH + q
                if m >= 2:
                    kp2, qp2 = divmod(m - 2, NCH)
                    tensor.wait_ge(SEs.h, se_base + 8 * kp2 + qp2 + 1)
                ins = None
                for dt_ in range(NKT):
                    ins = tensor.matmul(
                        pV[:, (m % 2) * 512:(m % 2 + 1) * 512],
                        Wvo[:, dt_ * 1024 + k * 128:dt_ * 1024 + (k + 1) * 128],
                        xn1C[:, dt_ * T + q * 512:dt_ * T + (q + 1) * 512],
                        start=(dt_ == 0), stop=(dt_ == NKT - 1))
                TE.inc(ins)
    bl.tensor(sec_vmm)

    cnt = {nm: {} for nm in ("recon", "P", "Q", "S1", "S2", "m1", "m2", "r")}
    for kc in range(NKT + 1):
        k = kc
        kp = kc - 1

        if kp >= 0:
            def sec_ve_scan(vector, k=kp):
                vector = SS(vector, VEs)
                if k >= 1:
                    vector.wait_ge(SRC.h, 16 * k)
                vector.tensor_tensor_scan(out=Pc[:], data0=ones_r[:], data1=Pb[:],
                                          initial=0.0, op0=ALU.mult, op1=ALU.add)
                cnt["S1"][k] = VEs.n
                vector.tensor_tensor_scan(out=Qc[:], data0=ones_r[:], data1=Qb[:],
                                          initial=0.0, op0=ALU.mult, op1=ALU.add)
                cnt["S2"][k] = VEs.n
                vector.tensor_mul(out=Pb[:], in0=cost[:], in1=Pc[:])
                cnt["m1"][k] = VEs.n
                vector.tensor_mul(out=Qb[:], in0=sint[:], in1=Qc[:])
                cnt["m2"][k] = VEs.n
                vector.tensor_add(out=Pc[:], in0=Pb[:], in1=Qb[:])
                cnt["r"][k] = VEs.n
            bl.vector(sec_ve_scan)

            def sec_store(sync, k=kp):
                sync.wait_ge(VEs.h, cnt["r"][k])
                SRC.dinc(sync.dma_start(out=rC_dram[:, k * T:(k + 1) * T],
                                        in_=Pc[:]))
            bl.sync(sec_store)

        if k < NKT:
            def sec_ve_recon(vector, k=k):
                vector = SS(vector, VEs)
                vector.wait_ge(SEs.h, se_tab)
                As = bc3(sinA[:, k * 32:(k + 1) * 32], 32, 128)
                Ac = bc3(cosA16[:, k * 32:(k + 1) * 32], 32, 128)
                Bs = bc3(sinB[:, k * 128:(k + 1) * 128], 32, 128)
                Bc = bc3(cosB16[:, k * 128:(k + 1) * 128], 32, 128)
                sv = sint[:].rearrange("p (b u) -> p b u", b=32)
                cv = cost[:].rearrange("p (b u) -> p b u", b=32)
                tv = tmp[:].rearrange("p (b u) -> p b u", b=32)
                vector.tensor_mul(out=sv, in0=As, in1=Bc)
                vector.tensor_mul(out=tv, in0=Ac, in1=Bs)
                vector.tensor_add(out=sint[:], in0=sint[:], in1=tmp[:])
                vector.tensor_mul(out=cv, in0=Ac, in1=Bc)
                vector.tensor_mul(out=tv, in0=As, in1=Bs)
                vector.tensor_sub(out=cost[:], in0=cost[:], in1=tmp[:])
                cnt["recon"][k] = VEs.n
            bl.vector(sec_ve_recon)

        if k < NKT:
            def sec_se_ev(scalar, k=k):
                scalar = SS(scalar, SEs)
                if k >= 2:
                    scalar.wait_ge(VEs.h, cnt["Q"][k - 2])
                vo = (k % 2) * T
                for q in range(NCH):
                    m = k * NCH + q
                    scalar.wait_ge(TE.h, te_base + m + 1)
                    scalar.copy(out=v_sb[:, vo + q * 512:vo + (q + 1) * 512],
                                in_=pV[:, (m % 2) * 512:(m % 2 + 1) * 512])
                assert SEs.n == se_base + 8 * (k + 1), (SEs.n, se_base, k)
            bl.scalar(sec_se_ev)

            def sec_ve_p(vector, k=k):
                vector = SS(vector, VEs)
                vector.wait_ge(SEs.h, se_base + 8 * (k + 1))
                vector.tensor_mul(out=Pb[:], in0=v_sb[:, (k % 2) * T:(k % 2 + 1) * T],
                                  in1=cost[:])
                cnt["P"][k] = VEs.n
            bl.vector(sec_ve_p)

            def sec_ve_q(vector, k=k):
                vector = SS(vector, VEs)
                vector.tensor_mul(out=Qb[:], in0=v_sb[:, (k % 2) * T:(k % 2 + 1) * T],
                                  in1=sint[:])
                cnt["Q"][k] = VEs.n
            bl.vector(sec_ve_q)

    te_vmm_all = TE.n

    def sec_wload_o(gpsimd):
        gpsimd.wait_ge(TE.h, te_vmm_all)
        for k in range(NKT):
            LWO2.dinc(gpsimd.dma_start(out=Wvo[:, k * 1024:(k + 1) * 1024],
                                       in_=dWoT[k * 128:(k + 1) * 128, :]))
    bl.gpsimd(sec_wload_o)
    lw_o = ctx.S("f_lwo").n

    es_t.close()
    rbT = ctx.sb("f_rbT", [128, 2 * 1024], BF16, es)
    x1r = ctx.sb("f_x1r", [128, 2 * D], F32, es)
    x2p = ctx.sb("f_x2p", [128, 2 * D], F32, es)
    pO = ctx.ps("f_pO", [128, 2 * 1024], F32, es)
    TO = ctx.S("f_to"); OV = ctx.S("f_ov")
    LX1 = DStream(ctx, "f_lx1"); SX2 = DStream(ctx, "f_sx2")
    LRB = DStream(ctx, "f_lrb")
    ctx.store_streams.append(SX2)
    rCv_dram = rC_dram.rearrange("p (k t) -> p k t", k=NKT)

    def sec_sync2(sync):
        sync.wait_ge(SRC.h, 16 * NKT)

        def store(bp):
            sync.wait_ge(OV.h, bp + 1)
            SX2.issue(sync.dma_start(out=x2p_dram[bp * 128:(bp + 1) * 128, :],
                                     in_=x2p[:, (bp % 2) * D:(bp % 2 + 1) * D]))
        for b in range(NTT):
            if b >= 2:
                sync.wait_ge(TO.h, 2 * (b - 1))
            LRB.issue(sync.dma_start(
                out=rbT[:, (b % 2) * 1024:(b % 2 + 1) * 1024].rearrange(
                    "p (k t) -> p k t", k=NKT),
                in_=rCv_dram[:, :, b * 128:(b + 1) * 128]))
            if b >= 2:
                sync.wait_ge(OV.h, b - 1)
            LX1.issue(sync.dma_start(out=x1r[:, (b % 2) * D:(b % 2 + 1) * D],
                                     in_=x1_dram[b * 128:(b + 1) * 128, :]))
            if b >= 2:
                store(b - 2)
        store(NTT - 2)
        store(NTT - 1)
        SX2.wait_all(sync)
    bl.sync(sec_sync2)

    def sec_omm(tensor):
        tensor.wait_ge(LWO2.h, lw_o)
        for b in range(NTT):
            LRB.wait_done(tensor, b)
            if b >= 2:
                tensor.wait_ge(OV.h, b - 1)
            for q2 in range(2):
                ins = None
                for k in range(NKT):
                    ins = tensor.matmul(
                        pO[:, (b % 2) * 1024 + q2 * 512:(b % 2) * 1024 + (q2 + 1) * 512],
                        rbT[:, (b % 2) * 1024 + k * 128:(b % 2) * 1024 + (k + 1) * 128],
                        Wvo[:, k * 1024 + q2 * 512:k * 1024 + (q2 + 1) * 512],
                        start=(k == 0), stop=(k == NKT - 1))
                TO.inc(ins)
    bl.tensor(sec_omm)

    def sec_oev(vector):
        vector = SS(vector, OV)
        for b in range(NTT):
            vector.wait_ge(TO.h, 2 * b + 2)
            LX1.wait_done(vector, b)
            SX2.wait_done(vector, b - 2)
            vector.tensor_add(out=x2p[:, (b % 2) * D:(b % 2 + 1) * D],
                              in0=pO[:, (b % 2) * 1024:(b % 2 + 1) * 1024],
                              in1=x1r[:, (b % 2) * D:(b % 2 + 1) * D])
    bl.vector(sec_oev)




def emit_ffn1(ctx, xn1C_dram, x2p_dram, out_dram, es):
    bl = ctx.block
    if "WinTb" in ctx.t:
        WinTb, WoutTb = ctx.t["WinTb"], ctx.t["WoutTb"]
        preloaded = True
    else:
        WinTb = ctx.sb("WinTb", [128, NKT * JW], BF16, es)
        WoutTb = ctx.sb("WoutTb", [128, NJT * 1024], BF16, es)
        preloaded = False
    acts = ctx.sb("g_acts", [128, NJT * 512], BF16, es)
    dec = ctx.sb("g_dec", [128, 11 * 512], BF16, es)
    msk2 = ctx.sb("g_msk", [128, 2 * NJT * 512], BF16, es)
    u0 = ctx.sb("g_u0", [128, 11 * 512], BF16, es)
    u1 = ctx.sb("g_u1", [128, 11 * 512], BF16, es)
    xchunk = ctx.sb("g_xc", [128, 2 * NKT * 512], BF16, es)
    x2p = ctx.sb("g_x2p", [128, 2 * D], F32, es)
    xout = ctx.sb("g_out", [128, D], F32, es)
    pL = ctx.ps("g_pL", [128, 2 * 512], F32, es)
    pS = ctx.ps("g_pS", [128, 2 * 512], F32, es)
    pO = ctx.ps("g_pO", [128, 2 * 1024], F32, es)
    bin_col = ctx.t["bin_col_sb"]
    S0sb, S1sb = ctx.t["S0_sb"], ctx.t["S1_sb"]
    dWinT, dWoutT = ctx.t["d_WinT"], ctx.t["d_WoutT"]
    xn1Cv_dram = xn1C_dram.rearrange("p (k t) -> p k t", k=NKT)
    ldc_target = ctx.S("ldc").n

    LWI = ctx.S("g_lwi"); LWO = ctx.S("g_lwo")
    TE = ctx.S("g_te"); SE = ctx.S("g_se"); VE = ctx.S("g_ve")
    LXC = DStream(ctx, "g_lxc"); LX2 = DStream(ctx, "g_lx2")
    SOUT = DStream(ctx, "g_sout")
    OV = ctx.S("g_ov")

    if not preloaded:
        def sec_wload(gpsimd):
            for k in range(NKT):
                LWI.dinc(gpsimd.dma_start(out=WinTb[:, k * JW:(k + 1) * JW],
                                          in_=dWinT[k * 128:(k + 1) * 128, :]))
            for jt in range(NJT):
                LWO.dinc(gpsimd.dma_start(
                    out=WoutTb[:, jt * 1024:(jt + 1) * 1024],
                    in_=dWoutT[jt * 128:(jt + 1) * 128, :]))
        bl.gpsimd(sec_wload)

    def sec_xc0(sync):
        sync.wait_ge(ctx.S("ldc").h, ldc_target)
        ctx.spill_ds.wait_all(sync)
        LXC.issue(sync.dma_start(
            out=xchunk[:, 0:NKT * 512].rearrange("p (k t) -> p k t", k=NKT),
            in_=xn1Cv_dram[:, :, 0:512]))
        LXC.issue(sync.dma_start(
            out=xchunk[:, NKT * 512:].rearrange("p (k t) -> p k t", k=NKT),
            in_=xn1Cv_dram[:, :, 512:1024]))
    bl.sync(sec_xc0)

    silu_end = {}; dec_end = {}; tree_end = {}
    tev_cnt = {}; uu_cnt = {}; tmm_cnt = {}; chain_to = {}

    def emit_omm_chain(q_out, c, part=None):
        sub, dc = c // 2, c % 2
        i2 = q_out * 4 + sub
        jts = range(NJT) if part is None else (
            range(0, 10) if part == 0 else range(10, NJT))

        def sec_ommc(tensor, q_out=q_out, sub=sub, dc=dc, i2=i2, jts=jts,
                     part=part):
            if part in (None, 0):
                if q_out == 0 and sub == 0 and dc == 0:
                    tensor.wait_ge(LWO.h, 16 * NJT)
                tensor.wait_ge(VE.h, tree_end[q_out])
                if i2 >= 2 and dc == 0:
                    tensor.wait_ge(OV.h, i2 - 1)
            moff = (q_out % 2) * NJT * 512
            ins = None
            for jt in jts:
                ins = tensor.matmul(
                    pO[:, (i2 % 2) * 1024 + dc * 512:(i2 % 2) * 1024 + (dc + 1) * 512],
                    msk2[:, moff + jt * 512 + sub * 128:moff + jt * 512 + (sub + 1) * 128],
                    WoutTb[:, jt * 1024 + dc * 512:jt * 1024 + (dc + 1) * 512],
                    start=(jt == 0), stop=(jt == NJT - 1))
            if part in (None, 1):
                TO = ctx.S("g_to")
                TO.inc(ins)
                chain_to[(q_out, sub, dc)] = TO.n
        bl.tensor(sec_ommc)

    TO = ctx.S("g_to")

    for q in range(NCH):
        te_l0 = TE.n
        ve0 = ctx.S("g_ve").n
        se0 = ctx.S("g_se").n
        xoff = (q % 2) * NKT * 512

        def sec_lmm(tensor, q=q, xoff=xoff, ve0=ve0, se0=se0):
            LXC.wait_done(tensor, q)
            if q == 0:
                tensor.wait_ge(LWI.h, 16 * NKT)
            for jt in range(NJT):
                if jt < 2 and q >= 1:
                    tensor.wait_ge(SE.h, silu_end[q - 1])
                    tensor.wait_ge(VE.h, dec_end[q - 1])
                if jt >= 2:
                    tensor.wait_ge(SE.h, se0 + jt - 1)
                    if jt - 2 < 11:
                        tensor.wait_ge(VE.h, ve0 + jt - 1)
                ins = None
                for k in range(NKT):
                    ins = tensor.matmul(pL[:, (jt % 2) * 512:(jt % 2 + 1) * 512],
                                        WinTb[:, k * JW + jt * 128:k * JW + (jt + 1) * 128],
                                        xchunk[:, xoff + k * 512:xoff + (k + 1) * 512],
                                        start=(k == 0), stop=(k == NKT - 1))
                TE.inc(ins)
        bl.tensor(sec_lmm)
        te_lmm = TE.n

        def sec_silu(scalar, q=q, te_l0=te_l0):
            scalar = SS(scalar, SE)
            if q >= 1:
                scalar.wait_ge(VE.h, tree_end[q - 1])
            for jt in range(NJT):
                scalar.wait_ge(TE.h, te_l0 + jt + 1)
                scalar.activation(out=acts[:, jt * 512:(jt + 1) * 512],
                                  in_=pL[:, (jt % 2) * 512:(jt % 2 + 1) * 512],
                                  func=AF.Silu, bias=bin_col[:, jt:jt + 1])
        bl.scalar(sec_silu)
        silu_end[q] = ctx.S("g_se").n

        def sec_dec(vector, q=q, te_l0=te_l0, se0=se0):
            vector = SS(vector, VE)
            for jt in range(11):
                vector.wait_ge(TE.h, te_l0 + jt + 1)
                vector.wait_ge(SE.h, se0 + jt + 1)
                vector.tensor_scalar(out=dec[:, jt * 512:(jt + 1) * 512],
                                     in0=pL[:, (jt % 2) * 512:(jt % 2 + 1) * 512],
                                     scalar1=bin_col[:, jt:jt + 1], scalar2=0.0,
                                     op0=ALU.add, op1=ALU.is_gt)
            vector.tensor_copy(out=u1[:, 0:512], in_=dec[:, 0:512])
            vector.tensor_scalar(out=u0[:, 0:512], in0=dec[:, 0:512],
                                 scalar1=-1.0, scalar2=1.0,
                                 op0=ALU.mult, op1=ALU.add)
            uu_cnt[(q, 0)] = ctx.S("g_ve").n
            if q >= 2:
                vector.wait_ge(TO.h, chain_to[(q - 2, 3, 1)])
            vector.tensor_copy(out=msk2[:, (q % 2) * NJT * 512:(q % 2) * NJT * 512 + 512],
                               in_=acts[:, 0:512])
        bl.vector(sec_dec)
        dec_end[q] = ctx.S("g_ve").n

        if q + 2 < NCH:
            def sec_xc(sync, q=q, te_lmm=te_lmm):
                sync.wait_ge(TE.h, te_lmm)
                LXC.issue(sync.dma_start(
                    out=xchunk[:, (q % 2) * NKT * 512:(q % 2 + 1) * NKT * 512].rearrange(
                        "p (k t) -> p k t", k=NKT),
                    in_=xn1Cv_dram[:, :, (q + 2) * 512:(q + 3) * 512]))
            bl.sync(sec_xc)

        if q >= 1:
            def sec_x2pload(sync, q=q):
                for sub in range(4):
                    i2 = (q - 1) * 4 + sub
                    if i2 >= 2:
                        sync.wait_ge(OV.h, i2 - 1)
                    s_tok = (q - 1) * 512 + sub * 128
                    LX2.issue(sync.dma_start(
                        out=x2p[:, (i2 % 2) * D:(i2 % 2 + 1) * D],
                        in_=x2p_dram[s_tok:s_tok + 128, :]))
            bl.sync(sec_x2pload)

        for r, (ct, pt) in enumerate(TREE_SMETA):
            def sec_tmm(tensor, q=q, r=r, ct=ct, pt=pt):
                tensor.wait_ge(VE.h, uu_cnt[(q, pt)])
                if r >= 2:
                    tensor.wait_ge(VE.h, tev_cnt[(q, r - 2)])
                elif q >= 1:
                    tensor.wait_ge(VE.h, tev_cnt[(q - 1, 16 + r)])
                tensor.matmul(pS[:, (r % 2) * 512:(r % 2 + 1) * 512],
                              S0sb[:, r * 128:(r + 1) * 128],
                              u0[:, pt * 512:(pt + 1) * 512],
                              start=True, stop=False)
                TE.inc(tensor.matmul(pS[:, (r % 2) * 512:(r % 2 + 1) * 512],
                                     S1sb[:, r * 128:(r + 1) * 128],
                                     u1[:, pt * 512:(pt + 1) * 512],
                                     start=False, stop=True))
                tmm_cnt[(q, r)] = ctx.S("g_te").n
            bl.tensor(sec_tmm)

            if q >= 1 and 1 <= r <= 16:
                emit_omm_chain(q - 1, (r - 1) // 2, part=(r - 1) % 2)

            if q >= 1 and r % 4 == 0 and r >= 4:
                subo = r // 4 - 1
                i2o = (q - 1) * 4 + subo

                def sec_oev1(vector, q=q, subo=subo, i2o=i2o):
                    vector = SS(vector, OV)
                    vector.wait_ge(TO.h, chain_to[(q - 1, subo, 1)])
                    LX2.wait_done(vector, i2o)
                    SOUT.wait_done(vector, i2o - 1)
                    vector.tensor_add(out=xout[:, 0:D],
                                      in0=pO[:, (i2o % 2) * 1024:(i2o % 2 + 1) * 1024],
                                      in1=x2p[:, (i2o % 2) * D:(i2o % 2 + 1) * D])
                bl.vector(sec_oev1)

            def sec_tev(vector, q=q, r=r, ct=ct, se0=se0):
                vector = SS(vector, VE)
                vector.wait_ge(TE.h, tmm_cnt[(q, r)])
                ps = pS[:, (r % 2) * 512:(r % 2 + 1) * 512]
                moff = (q % 2) * NJT * 512
                if ct < 11:
                    vector.tensor_mul(out=u1[:, ct * 512:(ct + 1) * 512],
                                      in0=ps, in1=dec[:, ct * 512:(ct + 1) * 512])
                    vector.tensor_sub(out=u0[:, ct * 512:(ct + 1) * 512],
                                      in0=ps, in1=u1[:, ct * 512:(ct + 1) * 512])
                    uu_cnt[(q, ct)] = ctx.S("g_ve").n
                vector.wait_ge(SE.h, se0 + ct + 1)
                vector.tensor_mul(out=msk2[:, moff + ct * 512:moff + (ct + 1) * 512],
                                  in0=acts[:, ct * 512:(ct + 1) * 512], in1=ps)
                tev_cnt[(q, r)] = ctx.S("g_ve").n
            bl.vector(sec_tev)
        tree_end[q] = ctx.S("g_ve").n

        if q >= 1:
            def sec_ostore(sync, q=q):
                for sub in range(4):
                    i2 = (q - 1) * 4 + sub
                    sync.wait_ge(OV.h, i2 + 1)
                    s_tok = (q - 1) * 512 + sub * 128
                    SOUT.issue(sync.dma_start(
                        out=out_dram[s_tok:s_tok + 128, :],
                        in_=xout[:, 0:D]))
            bl.gpsimd(sec_ostore)

    qt = NCH - 1

    def sec_x2pload_t(sync):
        for sub in range(4):
            i2 = qt * 4 + sub
            sync.wait_ge(OV.h, i2 - 1)
            s_tok = qt * 512 + sub * 128
            LX2.issue(sync.dma_start(out=x2p[:, (i2 % 2) * D:(i2 % 2 + 1) * D],
                                     in_=x2p_dram[s_tok:s_tok + 128, :]))
    bl.sync(sec_x2pload_t)
    for c in range(8):
        emit_omm_chain(qt, c)

    def sec_oev_t(vector):
        vector = SS(vector, OV)
        for sub in range(4):
            i2 = qt * 4 + sub
            vector.wait_ge(TO.h, chain_to[(qt, sub, 1)])
            LX2.wait_done(vector, i2)
            SOUT.wait_done(vector, i2 - 1)
            vector.tensor_add(out=xout[:, 0:D],
                              in0=pO[:, (i2 % 2) * 1024:(i2 % 2 + 1) * 1024],
                              in1=x2p[:, (i2 % 2) * D:(i2 % 2 + 1) * D])
    bl.vector(sec_oev_t)

    def sec_ostore_t(sync):
        for sub in range(4):
            i2 = qt * 4 + sub
            sync.wait_ge(OV.h, i2 + 1)
            s_tok = qt * 512 + sub * 128
            SOUT.issue(sync.dma_start(out=out_dram[s_tok:s_tok + 128, :],
                                      in_=xout[:, 0:D]))
    bl.gpsimd(sec_ostore_t)

    def sec_final(gpsimd):
        SOUT.wait_all(gpsimd)
    bl.gpsimd(sec_final)



PHASE_INPUTS = {
    "A": [], "B": ["Wn"], "C": [], "D": ["w1T", "w2T"],
    "E": [], "F": ["WvT", "WoT"], "G": ["WinT", "WoutT"],
}
TOP_CONSTS = {
    "A": ["ident", "eps_col"], "B": ["ident", "b_col"], "C": ["ident"],
    "D": ["b1_col"], "E": ["ident", "eps_col"],
    "F": ["fr_col"], "G": [],
}
HBM_T = {
    "xnC_h": ([128, NKT * T], BF16, "A", ["B", "D"]),
    "x1p_h": ([T, D], F32, "C", ["D"]),
    "x1_h": ([T, D], F32, "D", ["E", "F"]),
    "xn1C_h": ([128, NKT * T], BF16, "E", ["F", "G"]),
    "x2p_h": ([T, D], F32, "F", ["G"]),
}
PHASE_ORDER = ["A", "B", "C", "D", "E", "F", "G"]


def build_group(phases):
    nc = bass.Bass("TRN2", target_bir_lowering=False)
    in_names, out_names = [], []
    consts = set()
    dins = set()
    for ph in phases:
        consts.update(TOP_CONSTS[ph])
        dins.update(PHASE_INPUTS[ph])
    if "C" in phases:
        dins.update(["band", "b2_row"])
    if "F" in phases:
        dins.update(["iota128_row", "iota32_row"])
    if "G" in phases:
        dins.update(["bin_col", "S0all", "S1all"])
    if "C" in phases:
        assert "B" in phases
    if "F" in phases and "E" not in phases:
        pass

    es = contextlib.ExitStack()
    with es:
        dram = {}
        for nm in sorted(dins | consts):
            shape, dt = CONST_SPECS[nm]
            dram["d_" + nm] = nc.dram_tensor(nm, shape, dt, kind="ExternalInput").ap()
            in_names.append(nm)
        if "A" in phases or "C" in phases:
            dram["d_x"] = nc.dram_tensor("x", [T, D], F32, kind="ExternalInput").ap()
            in_names.append("x")
        pset = set(phases)
        for nm, (shape, dt, prod, cons) in HBM_T.items():
            produced = prod in pset
            consumed_here = bool(pset & set(cons))
            consumed_later = bool((set(cons) - pset))
            if not produced and not consumed_here:
                continue
            if produced and consumed_later:
                kind = "ExternalOutput"
                out_names.append(nm)
            elif produced:
                kind = "Internal"
            else:
                kind = "ExternalInput"
                in_names.append(nm)
            dram["d_" + nm] = nc.dram_tensor(nm, shape, dt, kind=kind).ap()
        if "D" in phases:
            dram["d_a1s"] = nc.dram_tensor("a1s", [T], F32,
                                           kind="Internal").ap()
        if "F" in phases:
            dram["d_rC_h"] = nc.dram_tensor("rC_h", [128, NKT * T], BF16,
                                            kind="Internal").ap()
        if "G" in phases:
            dram["d_out"] = nc.dram_tensor("out", [T, D], F32,
                                           kind="ExternalOutput").ap()
            out_names.append("out")

        block = es.enter_context(nc.Block())
        ctx = Ctx(nc, es, block)
        ctx.t.update(dram)
        emit_preload(ctx, sorted(consts))

        es_bpsum = contextlib.ExitStack()
        b_psum = None
        if "A" in phases and "B" in phases:
            b_psum = (ctx.ps("b_pf", [128, 2 * 512], F32, es_bpsum),
                      ctx.ps("b_pT", [128, 2 * 1024], BF16, es_bpsum))
        if "A" in phases:
            with contextlib.ExitStack() as sA:
                emit_ln(ctx, "a", ctx.t["d_x"], "xnC", ctx.t["d_xnC_h"], sA,
                        dst_es=sA)
            barrier(ctx)
        with contextlib.ExitStack() as s_bc:
            if "B" in phases:
                with contextlib.ExitStack() as sB:
                    emit_fproj(ctx, ctx.t["d_xnC_h"], sB, dst_es=s_bc,
                               psum=b_psum)
                barrier(ctx)
                es_bpsum.close()
            if "C" in phases:
                with contextlib.ExitStack() as sC:
                    emit_preload(ctx, ["band", "b2_row"], es=sC)
                    emit_attn0(ctx, ctx.t["d_x"], ctx.t["d_x1p_h"], sC)
                barrier(ctx)
        if "D" in phases:
            with contextlib.ExitStack() as sD:
                emit_ffn0(ctx, ctx.t["d_xnC_h"], ctx.t["d_x1p_h"],
                          ctx.t["d_x1_h"], sD)
            barrier(ctx)
        with contextlib.ExitStack() as s_ef:
            if "E" in phases:
                if "F" in phases:
                    emit_preload(ctx, ["iota128_row", "iota32_row"], es=s_ef)
                with contextlib.ExitStack() as sE:
                    emit_ln(ctx, "e", ctx.t["d_x1_h"], "xn1C",
                            ctx.t["d_xn1C_h"], sE, dst_es=s_ef)
                    if "F" in phases:
                        emit_rotor_prelude(ctx, s_ef)
                barrier(ctx)
            if "F" in phases:
                with contextlib.ExitStack() as sF:
                    if "xn1C" not in ctx.t:
                        t = ctx.sb("xn1C", [128, NKT * T], BF16, s_ef)
                        LD = ctx.S("ld_xn1c")

                        def sec(sync):
                            LD.dinc(sync.dma_start(out=t[:], in_=ctx.t["d_xn1C_h"]))
                        block.sync(sec)
                        tgt = LD.n

                        def secw(eng):
                            eng.wait_ge(LD.h, tgt)
                        block.tensor(secw)
                    if "E" not in phases:
                        emit_preload(ctx, ["iota128_row", "iota32_row"], es=sF)
                        emit_rotor_prelude(ctx, sF)
                    emit_rotor(ctx, ctx.t["d_x1_h"], ctx.t["d_x2p_h"],
                               ctx.t["d_rC_h"], sF)
                barrier(ctx)
        if "G" in phases:
            with contextlib.ExitStack() as sG:
                emit_preload(ctx, ["bin_col", "S0all", "S1all"], es=sG)
                emit_ffn1(ctx, ctx.t["d_xn1C_h"], ctx.t["d_x2p_h"],
                          ctx.t["d_out"], sG)
            barrier(ctx)
        if "G" not in phases:
            def sec_fin(gpsimd):
                if hasattr(ctx, "spill_ds"):
                    ctx.spill_ds.wait_all(gpsimd)
                for st_ in ctx.store_streams:
                    st_.wait_all(gpsimd)
            block.gpsimd(sec_fin)
    return nc, in_names, out_names


GROUPS_UNFUSED = [["A", "B", "C"], ["D"], ["E"], ["F"], ["G"]]
GROUPS_FUSED = [["A", "B", "C", "D", "E", "F", "G"]]
GROUPS = GROUPS_FUSED

_BUILD_CACHE = {}


def kernel(**inputs):
    common, per_core = host_prep(inputs)
    carry = [dict(pc) for pc in per_core]
    result = None
    for phases in GROUPS:
        key = tuple(phases)
        if key not in _BUILD_CACHE:
            _BUILD_CACHE[key] = build_group(phases)
        nc, in_names, out_names = _BUILD_CACHE[key]
        in_maps = []
        for ci in range(8):
            m = {}
            for nm in in_names:
                m[nm] = common[nm] if nm in common else carry[ci][nm]
            in_maps.append(m)
        res = run_bass_kernel_spmd(nc, in_maps, core_ids=list(range(8)))
        for ci in range(8):
            for nm in out_names:
                carry[ci][nm] = np.asarray(res.results[ci][nm])
        if "G" in phases:
            result = np.stack([carry[ci]["out"].astype(np.float32)
                               for ci in range(8)], axis=0)
    return result
